# version 03
# baseline (speedup 1.0000x reference)
import math
import numpy as np
import ml_dtypes
import concourse.bass as bass
import concourse.mybir as mybir
from concourse.bass_utils import run_bass_kernel_spmd
from concourse.alu_op_type import AluOpType as ALU
from concourse import dve_ops
from concourse.dve_op import DveOp
from concourse.dve_spec import Spec, Src0, Src1, C0, C1, Zero, maxx, minn, select

F32 = mybir.dt.float32
BF16 = mybir.dt.bfloat16
I32 = mybir.dt.int32
AF = mybir.ActivationFunctionType
AX = mybir.AxisListType.X
BF = ml_dtypes.bfloat16

D = 1024
SEQ = 2048
NSEQ = 4
NT = SEQ // 128
NH = 16
DH = 64
NPAIR = NH // 2

FF_PAR = 8
FF_NODES = 255
FF_W = FF_PAR * FF_NODES
FF_WP = 2048
FF_DEPTH = 7

FI = 2816
FI_NC = FI // 128

def _register_qops():
    if any(o.name == "QMX_SEL" for o in dve_ops.OPS):
        return
    qmx_sel = DveOp(
        "QMX_SEL",
        Spec(body=select(maxx(Src0, Zero - Src0) < C0, Src0, Src1)),
        subdim=False,
        uops_sha={},
        reference=lambda in0, in1, s0, s1, imm2: np.where(
            np.abs(in0) < s0, in0, in1).astype(np.float32),
    )
    from concourse.dve_spec import lower as _dve_lower
    for op in (qmx_sel,):
        for ver in ("v3", "v4"):
            try:
                r = _dve_lower(op.spec, ver=ver, perf_en=False, name=op.name)
                op.uops_sha[ver] = r.sha(ver)
            except Exception:
                pass
        dve_ops.OPS.append(op)
        dve_ops._SUB_OPCODE_FOR_NAME[op.name] = (
            dve_ops._CUSTOM_DVE_ROW_BASE + len(dve_ops.OPS) - 1
        )
        dve_ops.CUSTOM_DVE_REFS[op.name] = op.ref
    assert max(dve_ops._SUB_OPCODE_FOR_NAME.values()) < 0x20


ENGS = ("tensor", "vector", "scalar", "gpsimd", "sync")


class Sem:
    def __init__(self, k, name):
        self.h = k.ctx.enter_context(k.nc.semaphore(name))
        self.name = name
        self.n = 0


class K:
    def __init__(self, nc, ctx):
        self.nc = nc
        self.ctx = ctx
        self.plan = []
        self.s = {e: Sem(self, "s_" + e) for e in ENGS}
        NP = 8
        self.sdma_pool = {q: [Sem(self, f"d_{q}{i}") for i in range(NP)]
                          for q in ("sync", "gpsimd")}
        self.sdma_idx = {q: 0 for q in ("sync", "gpsimd")}
        self.last_wait = {e: {} for e in ENGS}
        self.n_emitted = {e: 0 for e in ENGS}

    def sb(self, name, shape, dt=F32):
        return self.ctx.enter_context(self.nc.sbuf_tensor(name, shape, dt))

    def ps(self, name, shape, dt=F32):
        return self.ctx.enter_context(self.nc.psum_tensor(name, shape, dt))

    def _coalesce(self, eng, waits):
        byname = {}
        for (sem, n) in waits:
            if n <= 0:
                continue
            if sem.name not in byname or byname[sem.name][1] < n:
                byname[sem.name] = (sem, n)
        wl = []
        for (sem, n) in byname.values():
            prev = self.last_wait[eng].get(sem.name, -1)
            if n > prev:
                wl.append((sem.h, n))
                self.last_wait[eng][sem.name] = n
        return wl

    def op(self, eng, f, waits=(), inc=True):
        wl = self._coalesce(eng, waits)
        psem = self.s[eng]
        tok = None
        if inc:
            psem.n += 1
            tok = (psem, psem.n)
            nn = psem.n

            def fn(e, wl=wl, f=f, nn=nn, h=psem.h):
                for (sh, n) in wl:
                    e.wait_ge(sh, n)
                f(e).then_inc(h, 1)
        else:
            def fn(e, wl=wl, f=f):
                for (sh, n) in wl:
                    e.wait_ge(sh, n)
                f(e)
        self.plan.append((eng, fn))
        self.n_emitted[eng] += 1
        return tok

    def dma(self, queue, out, in_, waits=()):
        pool = self.sdma_pool[queue]
        dsem = pool[self.sdma_idx[queue] % len(pool)]
        self.sdma_idx[queue] += 1
        wl = self._coalesce(queue, list(waits) + [(dsem, dsem.n)])
        dsem.n += 16
        tok = (dsem, dsem.n)

        def fn(e, wl=wl, out=out, in_=in_, h=dsem.h):
            for (sh, n) in wl:
                e.wait_ge(sh, n)
            e.dma_start(out=out, in_=in_).then_inc(h, 16)

        self.plan.append((queue, fn))
        self.n_emitted[queue] += 1
        return tok

    def barrier_token(self, eng):
        return (self.s[eng], self.s[eng].n)

    def replay(self):
        with self.nc.Block() as block:
            for engname in ENGS:
                items = [fn for e, fn in self.plan if e == engname]
                if not items:
                    continue
                dec = getattr(block, engname)

                def section(eng, items=items):
                    for fn in items:
                        fn(eng)

                dec(section)


def tmax(*toks):
    best = {}
    for t in toks:
        if t is None:
            continue
        sem, n = t
        if sem.name not in best or best[sem.name][1] < n:
            best[sem.name] = (sem, n)
    return list(best.values())


def host_consts():
    c = {}
    i = np.arange(DH // 2, dtype=np.float32)
    theta = np.power(np.float32(10000.0), -2.0 * (i - 1.0) / DH).astype(np.float32)
    ang = np.arange(SEQ, dtype=np.float32)[:, None] * theta[None, :]
    cos = np.cos(ang).astype(np.float32)
    sin = np.sin(ang).astype(np.float32)
    c["rope_cos"] = cos.reshape(NT, 128, 32).transpose(1, 0, 2).reshape(128, NT * 32).copy()
    c["rope_sin"] = sin.reshape(NT, 128, 32).transpose(1, 0, 2).reshape(128, NT * 32).copy()
    tri = np.tril(np.ones((128, 128), dtype=np.float32)).T
    c["tri_mask"] = tri.astype(BF)
    sel = np.zeros((4, 4, 32, 128), dtype=np.float32)
    for qc in range(4):
        for j in range(4):
            for d in range(128):
                sel[qc, j, (d // 64) * 16 + 4 * qc + j, d] = 1.0
    c["selmat"] = np.ascontiguousarray(sel.reshape(16, 32, 128).transpose(1, 0, 2).reshape(32, 16 * 128)).astype(BF)
    c["ones_col"] = np.ones((128, 1), dtype=BF)
    c["ones_row"] = np.ones((1, 128), dtype=BF)
    c["ident"] = np.eye(128, dtype=BF)
    return c


def emit_ln(k, bufs, x_ap, x_tok, out_ap, sc):
    junk, ssq, s1, mean, var, rstd, nmr = (sc["junk"], sc["ssq"], sc["s1"],
                                           sc["mean"], sc["var"], sc["rstd"],
                                           sc["nmr"])
    war = sc.get("war", [])
    t1 = k.op("scalar", lambda s: s.activation(junk[:], x_ap, AF.Square,
                                               accum_out=ssq[:]),
              waits=list(x_tok) + war)
    t2 = k.op("vector", lambda v: v.tensor_reduce(s1[:], x_ap, AX, ALU.add),
              waits=list(x_tok) + war)
    t3 = k.op("vector", lambda v: v.tensor_scalar(mean[:], s1[:], 1.0 / D, None,
                                                  ALU.mult), waits=[t2])
    t4 = k.op("vector", lambda v: v.tensor_tensor(var[:], mean[:], mean[:],
                                                  ALU.mult), waits=[t3])
    t5 = k.op("vector", lambda v: v.scalar_tensor_tensor(var[:], ssq[:], 1.0 / D,
                                                         var[:], ALU.mult,
                                                         ALU.subtract),
              waits=[t1, t4])
    t5b = k.op("vector", lambda v: v.tensor_scalar(var[:], var[:], 1e-5, None,
                                                   ALU.add), waits=[t5])
    t6 = k.op("scalar", lambda s: s.activation(rstd[:], var[:], AF.Sqrt),
              waits=[t5b])
    t7 = k.op("vector", lambda v: v.reciprocal(rstd[:], rstd[:]), waits=[t6])
    t8 = k.op("vector", lambda v: v.scalar_tensor_tensor(nmr[:], mean[:], -1.0,
                                                         rstd[:], ALU.mult,
                                                         ALU.mult),
              waits=[t7])
    t9 = k.op("scalar", lambda s: s.activation(out_ap, x_ap, AF.Identity,
                                               bias=nmr[:], scale=rstd[:]),
              waits=[t8] + list(x_tok))
    sc["war"] = [t9]
    return [t9]


def make_ln_scratch(k, tag):
    return {
        "junk": k.nc_junk_view,
        "ssq": k.sb(f"ln_sq_{tag}", [128, 1], F32),
        "s1": k.sb(f"ln_s1_{tag}", [128, 1], F32),
        "mean": k.sb(f"ln_m_{tag}", [128, 1], F32),
        "var": k.sb(f"ln_v_{tag}", [128, 1], F32),
        "rstd": k.sb(f"ln_rs_{tag}", [128, 1], F32),
        "nmr": k.sb(f"ln_nm_{tag}", [128, 1], F32),
    }


def emit_transpose(k, src_ap, nchunks, psum, dst_ap, ident, waits, war):
    ttoks = []
    for c in range(nchunks):
        tt = k.op("tensor",
                  lambda t, c=c: t.transpose(psum[:, c * 128:(c + 1) * 128],
                                             src_ap(c), ident[:]),
                  waits=list(waits) + list(war))
        ttoks.append(tt)
    ev = k.op("vector",
              lambda v: v.tensor_copy(dst_ap, psum[:, 0:nchunks * 128]),
              waits=ttoks)
    return [ev]


def emit_ffnA(k, env, seq, call, x_toks):
    nc = k.nc
    e = env
    wtag = f"fa{seq}_{call}"
    war_w = e["ffnA_w_war"] + e.get("arena_war", [])
    tw1 = k.dma("gpsimd", e["w_inT"][:].rearrange("p (c n) -> p c n", n=FF_W),
                e["in_w_inT"].rearrange("(c p) n -> p c n", p=128), waits=war_w)
    tw2 = k.dma("gpsimd", e["b_in"][:], e["in_b_in"].unsqueeze(0), waits=war_w)
    tw3 = k.dma("gpsimd", e["w_outT"][:].rearrange("p (c n) -> p c n", n=D),
                e["in_w_outT"].rearrange("(c p) n -> p c n", p=128), waits=war_w)
    wtok = [tw1, tw2, tw3]

    lnx, lnxT = e["lnx_bf"], e["lnxT"]
    acts, dec, vtree = e["acts"], e["dec"], e["vtree"]
    masked, maskedT = e["masked"], e["maskedT"]
    ps_tp, ps_l, ps_o = e["ps_tp"], e["ps_l"], e["ps_o"]
    lnsc = e["lnsc"]
    x_sb = e["x_sb"]
    ones_row, b_in = e["ones_row"], e["b_in"]
    ident = e["ident"]

    prev = {"ev2": [], "silu": [], "dec": [], "stt": [], "lnxT_used": [],
            "ln": [], "mask": []}
    new_x_toks = []
    last_user = []
    for ti in range(NT):
        x_tile = x_sb[:, ti * D:(ti + 1) * D]
        lntok = emit_ln(k, None, x_tile, x_toks[ti] + prev["lnxT_used"],
                        lnx[:], lnsc)
        ttok = emit_transpose(
            k, lambda c: lnx[:, c * 128:(c + 1) * 128], 8, ps_tp,
            lnxT[:], ident, lntok, prev["ev2"] + prev["lnxT_used"])
        prev["lnxT_used"] = []
        mm_waits = ttok + wtok + prev["silu"] + prev["dec"]
        ltoks = []
        nch = [(0, 512), (512, 1024), (1024, 1536), (1536, 2040)]
        for (n0, n1) in nch:
            for c in range(8):
                tok = k.op("tensor",
                           lambda t, c=c, n0=n0, n1=n1: t.matmul(
                               ps_l[:, n0:n1],
                               lnxT[:, c * 128:(c + 1) * 128],
                               e["w_inT"][:, c * FF_W + n0:c * FF_W + n1],
                               start=(c == 0), stop=False),
                           waits=mm_waits)
            tok = k.op("tensor",
                       lambda t, n0=n0, n1=n1: t.matmul(
                           ps_l[:, n0:n1], ones_row[:],
                           b_in[:, n0:n1], start=False, stop=True),
                       waits=mm_waits)
            ltoks.append(tok)
            prev["lnxT_used"].append(tok)
        stok = k.op("scalar", lambda s: s.activation(acts[:, 0:FF_W], ps_l[:, 0:FF_W],
                                                     AF.Silu),
                    waits=ltoks + prev["mask"])
        prev["silu"] = [stok]
        acts_t = acts[:].rearrange("p (t n) -> p t n", t=FF_PAR)
        dec_t = dec[:].rearrange("p (t n) -> p t n", t=FF_PAR)
        dtok = k.op("vector", lambda v: v.tensor_scalar(
            dec_t[:, :, 0:127], acts_t[:, :, 0:127], 0.0, None, ALU.is_gt),
            waits=[stok])
        prev["dec"] = [dtok]
        v_t = vtree[:].rearrange("p (t n) -> p t n", t=FF_PAR)
        ttk = k.op("vector", lambda v: v.memset(v_t[:, :, 0:1], 1.0),
                   waits=prev["stt"])
        cur = [ttk, dtok]
        for dlev in range(FF_DEPTH):
            pd = (1 << dlev) - 1
            cd = 1 << dlev
            c0 = (1 << (dlev + 1)) - 1
            tr = k.op("vector", lambda v, pd=pd, cd=cd, c0=c0: v.tensor_tensor(
                v_t[:, :, c0 + 1:c0 + 2 * cd:2], v_t[:, :, pd:pd + cd],
                dec_t[:, :, pd:pd + cd], ALU.mult), waits=cur)
            tl = k.op("vector", lambda v, pd=pd, cd=cd, c0=c0: v.tensor_tensor(
                v_t[:, :, c0:c0 + 2 * cd:2], v_t[:, :, pd:pd + cd],
                v_t[:, :, c0 + 1:c0 + 2 * cd:2], ALU.subtract), waits=[tr])
            cur = [tl, tr]
        mtok = k.op("vector", lambda v: v.tensor_tensor(
            masked[:, 0:FF_W], acts[:, 0:FF_W], vtree[:, 0:FF_W], ALU.mult),
            waits=cur + [stok] + prev["ev2"])
        prev["mask"] = [mtok]
        prev["stt"] = prev["stt"]
        ttok2 = emit_transpose(
            k, lambda c: masked[:, c * 128:(c + 1) * 128], 16, ps_tp,
            maskedT[:], ident, [mtok], ttok)
        prev["ev2"] = ttok2
        ow = ttok2 + wtok + prev["stt"]
        rts = []
        for (pso, n0) in ((e["ps_o"], 0), (e["ps_x"], 512)):
            for c in range(16):
                tok = k.op("tensor", lambda t, c=c, n0=n0, pso=pso: t.matmul(
                    pso[:], maskedT[:, c * 128:(c + 1) * 128],
                    e["w_outT"][:, c * D + n0:c * D + n0 + 512],
                    start=(c == 0), stop=(c == 15)), waits=ow)
            rtok = k.op("vector", lambda v, x_tile=x_tile, n0=n0, pso=pso:
                        v.scalar_tensor_tensor(
                            x_tile[:, n0:n0 + 512], pso[:], 0.5,
                            x_tile[:, n0:n0 + 512], ALU.mult, ALU.add),
                        waits=[tok] + x_toks[ti] + lntok)
            rts.append(rtok)
        prev["stt"] = rts
        new_x_toks.append(rts)
        last_user += rts
    env["ffnA_w_war"] = tmax(*last_user)
    return new_x_toks


ALL_BLOCKS = ("ffnA0", "attnA", "ffnA1", "ffnB0", "attnB", "ffnB1")


def build_program(blocks, n_seq=NSEQ):
    from contextlib import ExitStack
    _register_qops()
    nc = bass.Bass("TRN2", target_bir_lowering=False)
    din = {}
    def dram(name, shape, dt=F32, kind="ExternalInput"):
        din[name] = nc.dram_tensor(name, shape, dt, kind=kind).ap()
        return din[name]

    x_in = dram("x", [n_seq * SEQ, D])
    y_out = dram("y", [n_seq * SEQ, D], kind="ExternalOutput")
    dram("c_rope_cos", [128, NT * 32]); dram("c_rope_sin", [128, NT * 32])
    dram("c_tri", [128, 128], BF16)
    dram("c_sel", [32, 16 * 128], BF16)
    dram("c_ones_col", [128, 1], BF16); dram("c_ones_row", [1, 128], BF16)
    dram("c_ident", [128, 128], BF16)
    need = set(blocks)
    if need & {"ffnA0", "ffnA1"}:
        dram("w_inT_f", [D, FF_W]); dram("b_in_f", [FF_W]); dram("w_outT_f", [FF_WP, D])
    if "attnA" in need:
        dram("wq_a", [D, D]); dram("wk_a", [D, D]); dram("wv_a", [D, D])
        dram("wo_a", [D, D]); dram("bo_a", [D])
    if "attnB" in need:
        dram("wq_b", [D, D]); dram("wk_b", [D, D]); dram("wv_b", [D, D])
        dram("wo_b", [D, D])
        dram("bq_b", [D]); dram("bk_b", [D]); dram("bv_b", [D]); dram("bo_b", [D])
    if need & {"ffnB0", "ffnB1"}:
        dram("w_gate", [FI, D]); dram("w_up", [FI, D]); dram("w_down", [D, FI])
        din["wgqT"] = nc.dram_tensor("wgqT", [D, FI], BF16, kind="Internal").ap()
        din["wuqT"] = nc.dram_tensor("wuqT", [D, FI], BF16, kind="Internal").ap()
        din["wdqT"] = nc.dram_tensor("wdqT", [FI, D], BF16, kind="Internal").ap()

    with ExitStack() as ctx:
        k = K(nc, ctx)
        e = {"din": din}
        e["x_sb"] = k.sb("x_sb", [128, NT * D], BF16)
        e["stage"] = k.sb("stage", [128, D], F32)
        e["ident"] = k.sb("ident_sb", [128, 128], BF16)
        e["ones_row"] = k.sb("ones_row", [1, 128], BF16)
        e["ones_col"] = k.sb("ones_col", [128, 1], BF16)
        e["tri"] = k.sb("tri", [128, 128], BF16)
        e["lnx_bf"] = k.sb("lnx_bf", [128, D], BF16)
        e["lnxT"] = k.sb("lnxT", [128, D], BF16)
        k.nc_junk_view = e["lnx_bf"][:]
        e["lnsc"] = make_ln_scratch(k, "g")
        WA = 67584
        SA = 17280
        e["warena"] = k.sb("warena", [128, WA], BF16)
        e["sarena"] = k.sb("sarena", [128, SA], BF16)
        wa, sa = e["warena"], e["sarena"]
        if need & {"ffnA0", "ffnA1"}:
            e["in_w_inT"] = din["w_inT_f"]
            e["in_b_in"] = din["b_in_f"]
            e["in_w_outT"] = din["w_outT_f"]
            e["w_inT"] = wa[:, 0:8 * FF_W]
            e["b_in"] = wa[0:1, 32768:32768 + FF_W]
            e["w_outT"] = wa[:, 16320:16320 + 16 * D]
            e["acts"] = sa[:, 0:FF_W]
            e["dec"] = sa[:, 2048:2048 + FF_W]
            e["vtree"] = sa[:, 4096:4096 + FF_W]
            e["masked"] = sa[:, 6144:6144 + FF_WP]
            e["maskedT"] = sa[:, 8192:8192 + FF_WP]
            e["ffnA_w_war"] = []
        if need & {"ffnB0", "ffnB1"}:
            e["fb_wg"] = wa[:, 0:8 * FI]
            e["fb_wu"] = wa[:, 22528:45056]
            e["fb_wd"] = wa[:, 45056:45056 + FI_NC * D]
            e["fb_lnx32"] = sa[:, 0:2048].bitcast(F32)
            e["fb_xq"] = sa[:, 2048:3072]
            e["fb_xqT"] = sa[:, 3072:4096]
            qa = sa[:, 4096:6144].bitcast(F32)
            qb = sa[:, 6144:8192].bitcast(F32)
            qc = sa[:, 8192:10240].bitcast(F32)
            qbm = sa[:, 10240:10624].bitcast(F32)
            qsc = sa[:, 10624:11008].bitcast(F32)
            qrc = sa[:, 11008:11392].bitcast(F32)
            e["fb_qs"] = {"a": qa, "b": qb, "c": qc, "bm": qbm, "scale": qsc,
                          "rec": qrc}
            e["fb_hq"] = sa[:, 11392:11392 + FI]
            e["fb_hqT"] = sa[:, 14208:15232]
            e["fb_hsil"] = sa[:, 15232:16256].bitcast(F32)
            e["fb_hmul"] = sa[:, 16256:17280].bitcast(F32)
            e["wprep"] = wa[:, 0:5632].bitcast(F32)
            e["wprep_q"] = wa[:, 5632:8448]
            e["wprep_qT"] = wa[:, 8448:10496]
            e["wprep_qs"] = {"a": wa[:, 10496:16128].bitcast(F32),
                             "b": wa[:, 16128:21760].bitcast(F32),
                             "c": wa[:, 21760:27392].bitcast(F32),
                             "bm": wa[:, 27392:27744].bitcast(F32),
                             "scale": wa[:, 27744:28096].bitcast(F32),
                             "rec": wa[:, 28096:28448].bitcast(F32)}
        if "attnB" in need:
            e["ab_wq"] = wa[:, 0:8192]
            e["ab_wk"] = wa[:, 8192:16384]
            e["ab_wv"] = wa[:, 16384:24576]
            e["ab_wo"] = wa[:, 24576:32768]
            e["ab_brows"] = wa[0:1, 32768:32768 + 4 * D]
            e["ab_qm"] = sa[0:64, 0:2048]
            e["ab_km"] = sa[0:64, 2048:4096]
            e["ab_vaug"] = sa[:, 4096:4096 + 8 * 130]
            e["ab_pt0"] = sa[:, 5440:5952]
            e["ab_pt1"] = sa[:, 5952:6464]
            e["ab_aT"] = sa[:, 6464:7488]
            e["ab_rec"] = sa[:, 7488:11584].bitcast(F32)
            e["ab_bc"] = sa[0:64, 11584:15680].bitcast(F32)
            e["ab_ones"] = k.sb("ab_ones", [128, 64], BF16)
            e["ab_onesf"] = k.sb("ab_onesf", [128, 64], F32)
        if "attnA" in need:
            e["aa_lnxT"] = wa[:, 0:16384]
            e["aa_yT"] = wa[:, 16384:32768]
            e["aa_wq"] = wa[:, 32768:40960]
            e["aa_wk"] = wa[:, 40960:49152]
            e["aa_wv"] = wa[:, 49152:57344]
            e["aa_wout"] = wa[:, 57344:65536]
            e["aa_cos"] = wa[:, 65536:66560].bitcast(F32)
            e["aa_sin"] = wa[:, 66560:67584].bitcast(F32)
            e["aa_pairm"] = sa[:, 0:8192]
            e["aa_vtok"] = sa[:, 8192:10240]
            e["aa_tok4"] = sa[:, 10240:10752]
            e["aa_expscr"] = sa[:, 10752:11776]
            e["aa_ropesc"] = k.sb("aa_ropesc", [128, 128], F32)[:]
            e["aa_pt"] = sa[:, 11776:12800]
            e["aa_sel"] = sa[0:32, 12800:14848]
            e["aa_recT"] = sa[0:32, 14848:14976]
            e["aa_rcb"] = sa[:, 14976:15008]
            e["aa_reccol"] = sa[:, 15040:15104].bitcast(F32)
            e["aa_denp"] = sa[:, 15104:15112].bitcast(F32)
            e["aa_bc"] = sa[:, 15112:16136].bitcast(F32)
            e["aa_bout"] = sa[0:1, 16136:17160]
        e["ps_tp"] = k.ps("ps_tp", [128, 2048], BF16)
        e["ps_l"] = k.ps("ps_l", [128, 2048], F32)
        e["ps_o"] = k.ps("ps_o", [128, 512], F32)
        e["ps_x"] = k.ps("ps_x", [128, 512], F32)

        ct = []
        ct.append(k.dma("sync", e["ident"][:], din["c_ident"]))
        ct.append(k.dma("sync", e["ones_row"][:], din["c_ones_row"]))
        ct.append(k.dma("sync", e["ones_col"][:], din["c_ones_col"]))
        ct.append(k.dma("sync", e["tri"][:], din["c_tri"]))
        e["const_tok"] = ct
        if need & {"ffnA0", "ffnA1"}:
            k.op("vector", lambda v: v.memset(e["masked"][:, FF_W:FF_WP], 0.0))
        if "attnB" in need:
            k.op("vector", lambda v: v.memset(e["ab_ones"][:], 1.0))
            k.op("vector", lambda v: v.memset(e["ab_onesf"][:], 1.0))
            va3 = e["ab_vaug"].rearrange("p (c s) -> p c s", s=130)
            k.op("vector", lambda v: v.memset(va3[:, :, 64:65], 1.0))
            k.op("vector", lambda v: v.memset(va3[:, :, 129:130], 1.0))
        e["wprep_done"] = []
        if need & {"ffnB0", "ffnB1"}:
            emit_ffnB_weight_prep(k, e)

        for seq in range(n_seq):
            xv = x_in.rearrange("(s t p) d -> s p t d", s=n_seq, p=128)[seq]
            tld = k.dma("gpsimd", e["x_sb"][:].rearrange("p (t d) -> p t d", d=D),
                        xv, waits=e.get("x_war", []))
            x_toks = [[tld] + ct for _ in range(NT)]
            for b in blocks:
                if b.startswith("ffnA"):
                    x_toks = emit_ffnA(k, e, seq, b[-1], x_toks)
                elif b == "attnA":
                    x_toks = emit_attnA(k, e, seq, x_toks)
                elif b.startswith("ffnB"):
                    x_toks = emit_ffnB(k, e, seq, b[-1], x_toks)
                elif b == "attnB":
                    x_toks = emit_attnB(k, e, seq, x_toks)
                e["arena_war"] = tmax(*[t for toks in x_toks for t in toks])
            yv = y_out.rearrange("(s t p) d -> s p t d", s=n_seq, p=128)[seq]
            st_done = []
            for ti in range(NT):
                tok = k.op("scalar", lambda s, ti=ti: s.copy(
                    e["stage"][:], e["x_sb"][:, ti * D:(ti + 1) * D]),
                    waits=x_toks[ti] + e.get("stage_war", []))
                sd = k.dma("sync", yv[:, ti, :], e["stage"][:], waits=[tok])
                e["stage_war"] = [sd]
                st_done.append(sd)
            e["x_war"] = tmax(*(t for toks in x_toks for t in toks)) + st_done
            e["final_store"] = st_done
        fw = e["final_store"]
        k.op("gpsimd", lambda g: g.engine_nop(), waits=fw, inc=False)
        k.replay()
    return nc


def _rope_deinterleave_cols(w):
    idx = np.zeros(D, dtype=np.int64)
    for h in range(NH):
        for i in range(32):
            idx[h * 64 + i] = h * 64 + 2 * i
            idx[h * 64 + 32 + i] = h * 64 + 2 * i + 1
    return np.ascontiguousarray(w[:, idx])


def host_prep(inputs, blocks):
    need = set(blocks)
    m = {}
    if need & {"ffnA0", "ffnA1"}:
        m["w_inT_f"] = np.ascontiguousarray(inputs["L0_ffn_W_in"].T)
        m["b_in_f"] = np.ascontiguousarray(inputs["L0_ffn_b_in"])
        wo = np.zeros((FF_WP, D), dtype=np.float32)
        wo[:FF_W] = inputs["L0_ffn_W_out"].T
        m["w_outT_f"] = wo
    if "attnA" in need:
        wqkv = inputs["L0_attn_Wqkv"]
        m["wq_a"] = _rope_deinterleave_cols(np.ascontiguousarray(wqkv[:, 0:D]))
        m["wk_a"] = _rope_deinterleave_cols(np.ascontiguousarray(wqkv[:, D:2 * D]))
        m["wv_a"] = np.ascontiguousarray(wqkv[:, 2 * D:3 * D])
        m["wo_a"] = np.ascontiguousarray(inputs["L0_attn_Wout"])
        m["bo_a"] = np.ascontiguousarray(inputs["L0_attn_bout"])
    if "attnB" in need:
        m["wq_b"] = np.ascontiguousarray(inputs["L1_attn_Wq"])
        m["wk_b"] = np.ascontiguousarray(inputs["L1_attn_Wk"])
        m["wv_b"] = np.ascontiguousarray(inputs["L1_attn_Wv"])
        m["wo_b"] = np.ascontiguousarray(inputs["L1_attn_Wo"])
        m["bq_b"] = np.ascontiguousarray(inputs["L1_attn_bq"])
        m["bk_b"] = np.ascontiguousarray(inputs["L1_attn_bk"])
        m["bv_b"] = np.ascontiguousarray(inputs["L1_attn_bv"])
        m["bo_b"] = np.ascontiguousarray(inputs["L1_attn_bo"])
    if need & {"ffnB0", "ffnB1"}:
        m["w_gate"] = np.ascontiguousarray(inputs["L1_ffn_w_gate"])
        m["w_up"] = np.ascontiguousarray(inputs["L1_ffn_w_up"])
        m["w_down"] = np.ascontiguousarray(inputs["L1_ffn_w_down"])
    c = host_consts()
    m["c_rope_cos"] = c["rope_cos"]; m["c_rope_sin"] = c["rope_sin"]
    m["c_tri"] = c["tri_mask"]; m["c_sel"] = c["selmat"]
    m["c_ones_col"] = c["ones_col"]; m["c_ones_row"] = c["ones_row"]
    m["c_ident"] = c["ident"]
    return m


def run_blocks(x_full, inputs, blocks, n_seq=NSEQ, n_cores=8, sim=False):
    nc = build_program(blocks, n_seq=n_seq)
    wm = host_prep(inputs, blocks)
    rows = n_seq * SEQ
    in_maps = []
    for c in range(n_cores):
        im = dict(wm)
        im["x"] = np.ascontiguousarray(x_full[c * rows:(c + 1) * rows])
        in_maps.append(im)
    if sim:
        from concourse import bass_interp
        s = bass_interp.CoreSim(nc)
        for kk, vv in in_maps[0].items():
            s.tensor(kk)[:] = vv
        s.simulate()
        return np.array(s.tensor("y"))
    res = run_bass_kernel_spmd(nc, in_maps, core_ids=list(range(n_cores)))
    return np.concatenate([r["y"] for r in res.results], axis=0)


_NC_CACHE = {}


def kernel(**inputs):
    x = np.ascontiguousarray(inputs["L0_attn_x"].reshape(32 * SEQ, D))
    n_cores = 8
    rows = SEQ
    for b in ALL_BLOCKS:
        key = b[:4] if b[0] == "f" else b
        if key not in _NC_CACHE:
            _NC_CACHE[key] = build_program((b,), n_seq=1)
        nc = _NC_CACHE[key]
        wm = host_prep(inputs, (b,))
        for seq in range(NSEQ):
            in_maps = []
            for c in range(n_cores):
                im = dict(wm)
                r0 = (c * NSEQ + seq) * SEQ
                im["x"] = np.ascontiguousarray(x[r0:r0 + SEQ])
                in_maps.append(im)
            res = run_bass_kernel_spmd(nc, in_maps, core_ids=list(range(n_cores)))
            for c in range(n_cores):
                r0 = (c * NSEQ + seq) * SEQ
                x[r0:r0 + SEQ] = res.results[c]["y"]
    return x.reshape(32, SEQ, D)


SEXPMASK = int(np.int32(np.uint32(0xFF800000).view(np.int32)))
EXPDOWN2 = 0x01000000
MANTKEEP = int(np.int32(np.uint32(0xFFC00000).view(np.int32)))


FIXC = 6291456.0


def emit_quant(k, e, src, dst, nb, qs, waits):
    W = 16 * nb
    a, b, c = qs["a"], qs["b"], qs["c"]
    bm, scale, rec = qs["bm"], qs["scale"], qs["rec"]
    war = qs.get("war", [])
    src3 = src.rearrange("p (b s) -> p b s", s=16)
    t0 = k.op("vector", lambda v: v.tensor_reduce(
        bm[:, 0:nb].unsqueeze(2), src3, AX, ALU.max, apply_absolute_value=True),
        waits=list(waits) + war)
    t1 = k.op("vector", lambda v: v.tensor_scalar(
        scale[:, 0:nb], bm[:, 0:nb], 1.0 / 6.0, 1e-12, ALU.mult, ALU.max),
        waits=[t0])
    t2 = k.op("vector", lambda v: v.reciprocal(rec[:, 0:nb], scale[:, 0:nb]),
              waits=[t1])
    rec_bc = rec[:, 0:nb].unsqueeze(2).broadcast_to((128, nb, 16))
    sc_bc = scale[:, 0:nb].unsqueeze(2).broadcast_to((128, nb, 16))
    a3 = a[:, 0:W].rearrange("p (b s) -> p b s", s=16)
    t3 = k.op("vector", lambda v: v.tensor_tensor(a3, src3, rec_bc, ALU.mult),
              waits=[t2] + list(waits))
    ai = a[:, 0:W].bitcast(I32)
    bi = b[:, 0:W].bitcast(I32)
    ci = c[:, 0:W].bitcast(I32)
    t4a = k.op("vector", lambda v: v.tensor_scalar(
        bi, ai, SEXPMASK, None, ALU.bitwise_and), waits=[t3])
    t4 = k.op("vector", lambda v: v.tensor_scalar(
        ci, bi, EXPDOWN2, None, ALU.subtract), waits=[t4a])
    t5 = k.op("vector", lambda v: v.tensor_tensor(
        b[:, 0:W], a[:, 0:W], c[:, 0:W], ALU.add), waits=[t4])
    t6 = k.op("vector", lambda v: v.tensor_scalar(
        ci, bi, MANTKEEP, None, ALU.bitwise_and), waits=[t5])
    t6b = k.op("vector", lambda v: v.tensor_scalar(
        b[:, 0:W], a[:, 0:W], FIXC, FIXC, ALU.add, ALU.subtract), waits=[t6])
    t7a1 = k.op("vector", lambda v: v.tensor_scalar(
        ai, ai, 0x7FFFFFFF, None, ALU.bitwise_and), waits=[t6b])
    t7a = k.op("vector", lambda v: v.tensor_scalar(
        ai, ai, 0x40000000, None, ALU.is_ge), waits=[t7a1])
    t7 = k.op("vector", lambda v: v.copy_predicated(
        b[:, 0:W], ai, c[:, 0:W]), waits=[t7a])
    b3 = b[:, 0:W].rearrange("p (b s) -> p b s", s=16)
    t8 = k.op("vector", lambda v: v.tensor_tensor(
        dst.rearrange("p (b s) -> p b s", s=16), b3, sc_bc, ALU.mult),
        waits=[t7])
    qs["war"] = [t8]
    return [t8]


def emit_ffnB_weight_prep(k, e):
    qs = e["wprep_qs"]
    wtmp = e["wprep"]
    wq = e["wprep_q"]
    wqT = e["wprep_qT"]
    ident = e["ident"]
    ps_tp = e["ps_tp"]
    last = list(e["const_tok"])
    for (src_name, dst_name, nrow_ch, W) in (
            ("w_gate", "wgqT", FI_NC, D), ("w_up", "wuqT", FI_NC, D),
            ("w_down", "wdqT", 8, FI)):
        src = e["din"][src_name].rearrange("(t p) n -> t p n", p=128)
        dstv = e["din"][dst_name].rearrange("(c p) n -> p c n", p=128)
        for t in range(nrow_ch):
            ld = k.dma("sync", wtmp[:, 0:W], src[t], waits=qs.get("war", []) + last)
            qt = emit_quant(k, e, wtmp[:, 0:W], wq[:, 0:W], W // 16, qs, [ld])
            nch = W // 128
            for g0 in range(0, nch, 16):
                g1 = min(g0 + 16, nch)
                tt = emit_transpose(
                    k, lambda cc, g0=g0: wq[:, (g0 + cc) * 128:(g0 + cc + 1) * 128],
                    g1 - g0, ps_tp, wqT[:, 0:(g1 - g0) * 128], ident,
                    qt, e.get("wprep_tp_war", []))
                sd = k.dma("sync",
                           dstv[:, g0:g1, t * 128:(t + 1) * 128],
                           wqT[:, 0:(g1 - g0) * 128].rearrange(
                               "p (c j) -> p c j", j=128),
                           waits=tt)
                e["wprep_tp_war"] = [sd]
                last = [sd]
    e["wprep_done"] = last


def emit_ffnB(k, e, seq, call, x_toks):
    lnx32, xq, xqT = e["fb_lnx32"], e["fb_xq"], e["fb_xqT"]
    hsil, hmul, hq, hqT = e["fb_hsil"], e["fb_hmul"], e["fb_hq"], e["fb_hqT"]
    wg, wu, wd = e["fb_wg"], e["fb_wu"], e["fb_wd"]
    qs = e["fb_qs"]
    ps_tp, ps_l, ps_o = e["ps_tp"], e["ps_l"], e["ps_o"]
    lnsc = e["lnsc"]
    x_sb = e["x_sb"]
    ident = e["ident"]
    war_w = e.get("fb_w_war", []) + e["wprep_done"] + e.get("arena_war", [])
    tw = []
    tw.append(k.dma("sync", wg[:].rearrange("p (c n) -> p c n", n=FI),
                    e["din"]["wgqT"].rearrange("(c p) n -> p c n", p=128),
                    waits=war_w))
    tw.append(k.dma("sync", wu[:].rearrange("p (c n) -> p c n", n=FI),
                    e["din"]["wuqT"].rearrange("(c p) n -> p c n", p=128),
                    waits=war_w))
    tw.append(k.dma("sync", wd[:].rearrange("p (c n) -> p c n", n=D),
                    e["din"]["wdqT"].rearrange("(c p) n -> p c n", p=128),
                    waits=war_w))

    JCH = [(0, 512), (512, 1024), (1024, 1536), (1536, 2048), (2048, 2560),
           (2560, 2816)]
    prev = {"hq_used": [], "stt": [], "xqT_used": [], "gate_used": [],
            "hmul_used": []}
    new_x_toks = []
    last_user = list(tw)
    for ti in range(NT):
        x_tile = x_sb[:, ti * D:(ti + 1) * D]
        lntok = emit_ln(k, None, x_tile, x_toks[ti], lnx32[:, 0:D], lnsc)
        qtok = emit_quant(k, e, lnx32[:, 0:D], xq[:, 0:D], D // 16, qs, lntok)
        ttok = emit_transpose(k, lambda cc: xq[:, cc * 128:(cc + 1) * 128], 8,
                              ps_tp, xqT[:, 0:D], ident, qtok,
                              prev["xqT_used"] + prev["hq_used"])
        prev["xqT_used"] = []
        htoks = [None] * len(JCH)
        for ji, (j0, j1) in enumerate(JCH):
            jw = j1 - j0
            gw = ttok + tw + prev["gate_used"]
            gt = ut = None
            for c in range(8):
                gt = k.op("tensor", lambda t, c=c, j0=j0, j1=j1: t.matmul(
                    ps_l[:, 0:j1 - j0], xqT[:, c * 128:(c + 1) * 128],
                    wg[:, c * FI + j0:c * FI + j1],
                    start=(c == 0), stop=(c == 7)), waits=gw)
            for c in range(8):
                ut = k.op("tensor", lambda t, c=c, j0=j0, j1=j1: t.matmul(
                    ps_l[:, 512:512 + j1 - j0], xqT[:, c * 128:(c + 1) * 128],
                    wu[:, c * FI + j0:c * FI + j1],
                    start=(c == 0), stop=(c == 7)), waits=gw)
            prev["xqT_used"] = [gt, ut]
            st = k.op("scalar", lambda s, jw=jw: s.activation(
                hsil[:, 0:jw], ps_l[:, 0:jw], AF.Silu),
                waits=[gt] + prev["hmul_used"])
            ht = k.op("vector", lambda v, jw=jw: v.tensor_tensor(
                hmul[:, 0:jw], hsil[:, 0:jw], ps_l[:, 512:512 + jw], ALU.mult),
                waits=[st, ut])
            prev["gate_used"] = [ht]
            prev["hmul_used"] = [ht]
            qh = emit_quant(k, e, hmul[:, 0:jw], hq[:, j0:j1], jw // 16, qs, [ht])
            prev["hmul_used"] = prev["hmul_used"] + qh
            htoks[ji] = qh
        otoks = []
        ow_war = prev["stt"]
        dtoks = []
        for gi, (g0, g1) in enumerate(((0, 8), (8, 16), (16, 22))):
            need = [t for ji, (j0, j1) in enumerate(JCH)
                    if j0 < g1 * 128 and j1 > g0 * 128 for t in htoks[ji]]
            tt = emit_transpose(
                k, lambda cc, g0=g0: hq[:, (g0 + cc) * 128:(g0 + cc + 1) * 128],
                g1 - g0, ps_tp, hqT[:, 0:(g1 - g0) * 128], ident,
                need + prev["hq_used"], ttok + dtoks)
            prev["hq_used"] = []
            for cc in range(g0, g1):
                for (pso, n0) in ((e["ps_o"], 0), (e["ps_x"], 512)):
                    dt_ = k.op("tensor", lambda t, cc=cc, g0=g0, n0=n0, pso=pso:
                               t.matmul(pso[:],
                                        hqT[:, (cc - g0) * 128:(cc - g0 + 1) * 128],
                                        wd[:, cc * D + n0:cc * D + n0 + 512],
                                        start=(cc == 0), stop=(cc == 21)),
                               waits=tt + tw + ow_war)
                    if cc == g1 - 1:
                        otoks.append(dt_)
                    dtoks.append(dt_)
            prev["hq_used"] = [dtoks[-1]]
        rts = []
        for (pso, n0, dt_) in ((e["ps_o"], 0, otoks[-2]), (e["ps_x"], 512, otoks[-1])):
            rtok = k.op("vector", lambda v, x_tile=x_tile, n0=n0, pso=pso:
                        v.scalar_tensor_tensor(
                            x_tile[:, n0:n0 + 512], pso[:], 0.5,
                            x_tile[:, n0:n0 + 512], ALU.mult, ALU.add),
                        waits=[dt_] + x_toks[ti] + lntok)
            rts.append(rtok)
        prev["stt"] = rts
        new_x_toks.append(rts)
        last_user += rts
    e["fb_w_war"] = tmax(*last_user)
    return new_x_toks


def emit_attnB(k, e, seq, x_toks):
    x_sb = e["x_sb"]
    lnx, lnxT = e["lnx_bf"], e["lnxT"]
    lnsc = e["lnsc"]
    ident = e["ident"]
    ones_row = e["ones_row"]
    ps_tp, ps_l, ps_o, ps_x = e["ps_tp"], e["ps_l"], e["ps_o"], e["ps_x"]
    Qm, Km = e["ab_qm"], e["ab_km"]
    vaug = e["ab_vaug"]
    pt0, pt1 = e["ab_pt0"], e["ab_pt1"]
    recs = e["ab_rec"]
    bc = e["ab_bc"]
    aT = e["ab_aT"]
    onesb = e["ab_ones"]
    onesf = e["ab_onesf"]
    wq, wk, wv, wo = e["ab_wq"], e["ab_wk"], e["ab_wv"], e["ab_wo"]
    brows = e["ab_brows"]

    war_w = e.get("ab_w_war", []) + e.get("arena_war", [])
    tw = []
    for wsb, nm in ((wq, "wq_b"), (wk, "wk_b"), (wv, "wv_b"), (wo, "wo_b")):
        tw.append(k.dma("gpsimd", wsb.rearrange("p (c n) -> p c n", n=D),
                        e["din"][nm].rearrange("(c p) n -> p c n", p=128),
                        waits=war_w))
    for i, nm in enumerate(("bq_b", "bk_b", "bv_b", "bo_b")):
        tw.append(k.dma("gpsimd", brows[:, i * D:(i + 1) * D],
                        e["din"][nm].unsqueeze(0), waits=war_w))

    prev = {"lnxT_used": [], "qk_evac": [], "v_evac": [], "exp": [],
            "pv": [], "stt": [], "aT_used": [], "bc_used": [], "rec_used": []}
    new_x_toks = []
    last_user = list(tw)
    for ti in range(NT):
        x_tile = x_sb[:, ti * D:(ti + 1) * D]
        lntok = emit_ln(k, None, x_tile, x_toks[ti] + prev["lnxT_used"], lnx[:],
                        lnsc)
        ttok = emit_transpose(k, lambda c: lnx[:, c * 128:(c + 1) * 128], 8,
                              ps_tp, lnxT[:], ident, lntok, prev["lnxT_used"])
        prev["lnxT_used"] = [ttok[0]]
        qk_toks = []
        for (dst, wsb, boff, scl, poff) in ((Qm, wq, 0, 1.0 / 1024.0, 0),
                                            (Km, wk, 1, 1.0, 1024)):
            mm = None
            for par in (0, 1):
                for c in range(8):
                    for (n0, n1) in ((0, 512), (512, 1024)):
                        mm = k.op("tensor", lambda t, c=c, n0=n0, n1=n1,
                                  par=par, wsb=wsb, poff=poff: t.matmul(
                                      ps_l[par * 64:(par + 1) * 64,
                                           poff + n0:poff + n1],
                                      lnxT[:, c * 128 + par:(c + 1) * 128:2],
                                      wsb[:, c * D + n0:c * D + n1],
                                      start=(c == 0), stop=False,
                                      tile_position=(0, 64 * par)),
                                  waits=ttok + tw + prev["qk_evac"])
                for (n0, n1) in ((0, 512), (512, 1024)):
                    mm = k.op("tensor", lambda t, n0=n0, n1=n1, par=par,
                              boff=boff, poff=poff: t.matmul(
                                  ps_l[par * 64:(par + 1) * 64,
                                       poff + n0:poff + n1],
                                  ones_row[:, 0:64],
                                  brows[:, boff * D + n0:boff * D + n1],
                                  start=False, stop=True,
                                  tile_position=(0, 64 * par)),
                              waits=ttok + tw + prev["qk_evac"])
            ev0 = k.op("scalar", lambda s, dst=dst, scl=scl, poff=poff:
                       s.activation(dst[:, 0:1024], ps_l[0:64, poff:poff + 1024],
                                    AF.Identity, scale=float(scl)),
                       waits=[mm] + prev["pv"])
            ev1 = k.op("scalar", lambda s, dst=dst, scl=scl, poff=poff:
                       s.activation(dst[:, 1024:2048],
                                    ps_l[64:128, poff:poff + 1024],
                                    AF.Identity, scale=float(scl)),
                       waits=[mm] + prev["pv"])
            qk_toks += [ev0, ev1]
        vmm = []
        vev = []
        for half in (0, 1):
            pso = (ps_o, ps_x)[half]
            hm = []
            for c2 in range(4):
                c = half * 4 + c2
                for par in (0, 1):
                    mm = None
                    for cd in range(8):
                        mm = k.op("tensor", lambda t, c=c, c2=c2, par=par,
                                  cd=cd, pso=pso: t.matmul(
                                      pso[:, c2 * 128 + par * 64:
                                          c2 * 128 + par * 64 + 64],
                                      wv[:, cd * D + c * 128:cd * D + (c + 1) * 128],
                                      lnxT[:, cd * 128 + par:(cd + 1) * 128:2],
                                      start=(cd == 0), stop=False),
                                  waits=ttok + tw + prev["v_evac"])
                    mm = k.op("tensor", lambda t, c=c, c2=c2, par=par, pso=pso:
                              t.matmul(
                                  pso[:, c2 * 128 + par * 64:
                                      c2 * 128 + par * 64 + 64],
                                  brows[:, 2 * D + c * 128:2 * D + (c + 1) * 128],
                                  onesb[0:1, 0:64],
                                  start=False, stop=True),
                              waits=ttok + tw + prev["v_evac"])
                    hm.append(mm)
            va4 = vaug[:].rearrange("p (c q s) -> p c q s", q=2, s=65)
            ev = k.op("vector", lambda v, half=half, pso=pso, va4=va4:
                      v.tensor_copy(
                          va4[:, half * 4:(half + 1) * 4, :, 0:64],
                          pso[:].rearrange("p (c q s) -> p c q s", q=2, s=64)),
                      waits=hm)
            vev.append(ev)
        vmm = vev
        pv_toks = []
        exp_toks = []
        for kt in range(16):
            par, kc = (0, kt) if kt < 8 else (1, kt - 8)
            vslice = vaug[:, kc * 130 + par * 65:kc * 130 + par * 65 + 65]
            for qc in range(4):
                pbuf = (pt0, pt1)[(kt * 4 + qc) % 2]
                pso = (ps_o, ps_x)[(kt * 4 + qc) % 2][:]
                smm = k.op("tensor", lambda t, kt=kt, qc=qc, pso=pso: t.matmul(
                    pso, Km[:, kt * 128:(kt + 1) * 128],
                    Qm[:, qc * 512:(qc + 1) * 512],
                    start=True, stop=True),
                    waits=qk_toks + prev["exp"] + vmm)
                ex = k.op("scalar", lambda s, pbuf=pbuf, pso=pso: s.activation(
                    pbuf[:], pso, AF.Exp), waits=[smm] + prev["pv"])
                exp_toks.append(ex)
                prev["exp"] = exp_toks[-2:]
                pv = k.op("tensor", lambda t, kt=kt, qc=qc, vslice=vslice,
                          pbuf=pbuf: t.matmul(
                              ps_l[0:65, qc * 512:(qc + 1) * 512],
                              vslice, pbuf[:],
                              start=(kt == 0), stop=(kt == 15)),
                          waits=[ex] + prev["aT_used"] + qk_toks)
                pv_toks.append(pv)
                prev["pv"] = pv_toks[-2:]
        rc = k.op("vector", lambda v: v.reciprocal(recs[64:65, :],
                                                   ps_l[64:65, 0:2048]),
                  waits=pv_toks[-4:] + prev["rec_used"])
        bcm = []
        for qc in range(4):
            psb = (ps_o, ps_x)[qc % 2]
            bm_ = k.op("tensor", lambda t, qc=qc, psb=psb: t.matmul(
                psb[0:64, 0:512],
                onesf[64:65, 0:64],
                recs[64:65, qc * 512:(qc + 1) * 512],
                start=True, stop=True),
                waits=[rc] + prev["bc_used"] + prev["exp"] + bcm[-2:])
            bv_ = k.op("vector", lambda v, qc=qc, psb=psb: v.tensor_copy(
                bc[:, qc * 512:(qc + 1) * 512], psb[0:64, 0:512]),
                waits=[bm_])
            bcm.append(bv_)
        prev["rec_used"] = bcm[-1:]
        prev["bc_used"] = []
        aT_toks = []
        for c in range(8):
            for half in (0, 1):
                at = k.op("vector", lambda v, c=c, half=half: v.tensor_tensor(
                    aT[half * 64:half * 64 + 64, c * 128:(c + 1) * 128],
                    ps_l[0:64, 2 * c + half:2048:16],
                    bc[:, 2 * c + half:2048:16], ALU.mult),
                    waits=bcm + pv_toks[-4:]),
                aT_toks.append(at[0])
        prev["bc_used"] = aT_toks
        omm = []
        rts = []
        ow = aT_toks + tw + prev["stt"]
        for (pso, n0) in ((ps_o, 0), (ps_x, 512)):
            mm = None
            for c in range(8):
                mm = k.op("tensor", lambda t, c=c, n0=n0, pso=pso: t.matmul(
                    pso[:], aT[:, c * 128:(c + 1) * 128],
                    wo[:, c * D + n0:c * D + n0 + 512],
                    start=(c == 0), stop=False), waits=ow)
            mm = k.op("tensor", lambda t, n0=n0, pso=pso: t.matmul(
                pso[:], ones_row[:],
                brows[:, 3 * D + n0:3 * D + n0 + 512], start=False, stop=True),
                waits=ow)
            omm.append(mm)
            rtok = k.op("vector", lambda v, x_tile=x_tile, n0=n0, pso=pso:
                        v.tensor_tensor(x_tile[:, n0:n0 + 512], pso[:],
                                        x_tile[:, n0:n0 + 512], ALU.add),
                        waits=[mm] + x_toks[ti] + lntok)
            rts.append(rtok)
        prev["aT_used"] = omm
        prev["stt"] = rts
        prev["qk_evac"] = rts
        prev["v_evac"] = rts
        new_x_toks.append(rts)
        last_user += rts
    e["ab_w_war"] = tmax(*last_user)
    return new_x_toks


def emit_attnA(k, e, seq, x_toks):
    import os
    AA_NPAIR = int(os.environ.get("AA_NPAIR", str(NPAIR)))
    AA_F1 = int(os.environ.get("AA_F1", "1"))
    AA_F2 = int(os.environ.get("AA_F2", "1"))
    x_sb = e["x_sb"]
    lnx = e["lnx_bf"]
    lnsc = e["lnsc"]
    ident, ones_row, tri = e["ident"], e["ones_row"], e["tri"]
    ps_tp, ps_l, ps_o, ps_x = e["ps_tp"], e["ps_l"], e["ps_o"], e["ps_x"]
    lnxT = e["aa_lnxT"]
    yT = e["aa_yT"]
    wq, wk, wv = e["aa_wq"], e["aa_wk"], e["aa_wv"]
    wout = e["aa_wout"]
    cosf, sinf = e["aa_cos"], e["aa_sin"]
    tok4 = e["aa_tok4"]
    pairm = e["aa_pairm"]
    vtok = e["aa_vtok"]
    expscr = e["aa_expscr"]
    ptile = e["aa_pt"]
    sel = e["aa_sel"]
    recT = e["aa_recT"]
    rcb = e["aa_rcb"]
    reccol = e["aa_reccol"]
    denp = e["aa_denp"]
    bcsb = e["aa_bc"]
    bout = e["aa_bout"]
    ropesc = e["aa_ropesc"]

    war_w = e.get("aa_w_war", []) + e.get("arena_war", [])
    tw = []
    for wsb, nm in ((wq, "wq_a"), (wk, "wk_a"), (wv, "wv_a"), (wout, "wo_a")):
        tw.append(k.dma("gpsimd", wsb.rearrange("p (c n) -> p c n", n=D),
                        e["din"][nm].rearrange("(c p) n -> p c n", p=128),
                        waits=war_w))
    tw.append(k.dma("gpsimd", bout[:], e["din"]["bo_a"].unsqueeze(0), waits=war_w))
    tw.append(k.dma("sync", cosf, e["din"]["c_rope_cos"], waits=war_w))
    tw.append(k.dma("sync", sinf, e["din"]["c_rope_sin"], waits=war_w))
    tw.append(k.dma("sync", sel[:], e["din"]["c_sel"], waits=war_w))

    lnxT_toks = []
    prev_tp = []
    ln_last = []
    tp_war = []
    for ti in range(NT):
        x_tile = x_sb[:, ti * D:(ti + 1) * D]
        lntok = emit_ln(k, None, x_tile, x_toks[ti] + war_w + tp_war, lnx[:],
                        lnsc)
        ln_last.append(lntok[0])
        dst = lnxT[:].rearrange("p (c n) -> p c n", n=SEQ)[
            :, :, ti * 128:(ti + 1) * 128]
        tt = emit_transpose(k, lambda c: lnx[:, c * 128:(c + 1) * 128], 8,
                            ps_tp, dst, ident, lntok, prev_tp + tp_war)
        prev_tp = tt
        tp_war = [k.barrier_token("tensor")]
        lnxT_toks += tt

    prev = {"pairm_used": [], "vtok_used": [], "unrot_exp": [], "rot_exp": [],
            "pv": [], "bc_used": [], "yT_used": [], "tok_used": [],
            "denp_used": [], "reccol_used": [], "ropescr_war": []}
    yT_toks = []
    qrT = pairm[:, 0 * SEQ:1 * SEQ]
    qT = pairm[:, 1 * SEQ:2 * SEQ]
    krT = pairm[:, 2 * SEQ:3 * SEQ]
    kT = pairm[:, 3 * SEQ:4 * SEQ]
    for hp in range(AA_NPAIR):
        ptoks = []
        for ti in range(NT):
            pw = lnxT_toks + tw + prev["tok_used"] + prev["unrot_exp"] + \
                prev["rot_exp"] + prev["bc_used"] + prev.get("bcast_used", [])
            mm = {}
            for (slot, wsb) in ((0, wq), (1, wk), (2, wv)):
                m = None
                for c in range(8):
                    m = k.op("tensor", lambda t, c=c, hp=hp, ti=ti, slot=slot,
                             wsb=wsb: t.matmul(
                                 ps_l[:, slot * 512:slot * 512 + 128],
                                 lnxT[:, c * SEQ + ti * 128:c * SEQ + (ti + 1) * 128],
                                 wsb[:, c * D + hp * 128:c * D + (hp + 1) * 128],
                                 start=(c == 0), stop=(c == 7)),
                             waits=pw if slot < 2 else
                             (lnxT_toks + tw + prev["vtok_used"] + pw))
                mm[slot] = m
            evq = k.op("scalar", lambda s: s.copy(tok4[:, 0:128], ps_l[:, 0:128]),
                       waits=[mm[2]] + prev["tok_used"])
            evk = k.op("scalar", lambda s: s.copy(tok4[:, 256:384],
                                                  ps_l[:, 512:640]),
                       waits=[mm[2]] + prev["tok_used"])
            evv = k.op("scalar", lambda s, ti=ti: s.copy(
                vtok[:, ti * 128:(ti + 1) * 128], ps_l[:, 1024:1152]),
                waits=[mm[2]] + prev["vtok_used"])
            rtoks = []
            AA_ROPE = int(os.environ.get("AA_ROPE", "1"))
            for (src0, out0, src_ev) in (((0, 128, evq), (256, 384, evk))
                                         if AA_ROPE else ()):
                pv3 = tok4[:, src0:src0 + 128].rearrange("p (h i) -> p h i", i=64)
                te, to = pv3[:, :, 0:32], pv3[:, :, 32:64]
                cosb = cosf[:, ti * 32:(ti + 1) * 32].unsqueeze(1).broadcast_to(
                    (128, 2, 32))
                sinb = sinf[:, ti * 32:(ti + 1) * 32].unsqueeze(1).broadcast_to(
                    (128, 2, 32))
                sc3 = ropesc[:].rearrange("p (h i) -> p h i", i=64)
                sc_a, sc_b = sc3[:, :, 0:32], sc3[:, :, 32:64]
                o3 = tok4[:, out0:out0 + 128].rearrange("p (h i) -> p h i", i=64)
                ro_re, ro_ro = o3[:, :, 0:32], o3[:, :, 32:64]
                w0 = [src_ev] + prev["tok_used"] + prev["ropescr_war"] + rtoks
                RN = int(os.environ.get("AA_RN", "6"))
                r1 = k.op("vector", lambda v, te=te, cosb=cosb, sc_a=sc_a:
                          v.tensor_tensor(sc_a, te, cosb, ALU.mult), waits=w0)
                last = r1
                if RN >= 2:
                    r2 = k.op("vector", lambda v, to=to, sinb=sinb, sc_b=sc_b:
                              v.tensor_tensor(sc_b, to, sinb, ALU.mult), waits=w0)
                    last = r2
                if RN >= 3:
                    r3 = k.op("vector", lambda v, sc_a=sc_a, sc_b=sc_b, ro_re=ro_re:
                              v.tensor_tensor(ro_re, sc_a, sc_b, ALU.subtract),
                              waits=[r1, r2])
                    last = r3
                if RN >= 4:
                    r4 = k.op("vector", lambda v, te=te, sinb=sinb, sc_a=sc_a:
                              v.tensor_tensor(sc_a, te, sinb, ALU.mult), waits=[r3])
                    last = r4
                if RN >= 5:
                    r5 = k.op("vector", lambda v, to=to, cosb=cosb, sc_b=sc_b:
                              v.tensor_tensor(sc_b, to, cosb, ALU.mult), waits=[r3])
                    last = r5
                if RN >= 6:
                    r6 = k.op("vector", lambda v, sc_a=sc_a, sc_b=sc_b, ro_ro=ro_ro:
                              v.tensor_tensor(ro_ro, sc_a, sc_b, ALU.add),
                              waits=[r4, r5])
                    last = r6
                rtoks += ([r3, r6] if RN >= 6 else [last])
            prev["ropescr_war"] = rtoks[-2:]
            ttw = [evq, evk] + rtoks + prev["pairm_used"]
            tts = []
            for (src0, m) in ((128, 0), (0, 1), (384, 2), (256, 3)):
                tt = k.op("tensor", lambda t, src0=src0, m=m: t.transpose(
                    ps_tp[:, m * 128:(m + 1) * 128],
                    tok4[:, src0:src0 + 128], ident[:]), waits=ttw)
                tts.append(tt)
            pm = pairm[:].rearrange("p (m n) -> p m n", n=SEQ)
            ev = k.op("vector", lambda v, ti=ti, pm=pm: v.tensor_copy(
                pm[:, :, ti * 128:(ti + 1) * 128],
                ps_tp[:, 0:512].rearrange("p (m n) -> p m n", n=128)),
                waits=tts)
            ptoks.append(ev)
            ptoks.append(evv)
            prev["tok_used"] = [tts[-1], ev]
            prev["vtok_used"] = [evv]
        rc_toks = []
        for qt in range(NT if AA_F1 else 0):
            etoks = []
            for h in (0, 1):
                for kh in (0, 1):
                    m2 = []
                    for nn in (0, 512):
                        m2.append(k.op(
                            "tensor", lambda t, h=h, kh=kh, nn=nn, qt=qt:
                            t.matmul(
                                ps_l[:, h * 1024 + nn:h * 1024 + nn + 512],
                                qT[h * 64:(h + 1) * 64, qt * 128:(qt + 1) * 128],
                                kT[h * 64:(h + 1) * 64,
                                   kh * 1024 + nn:kh * 1024 + nn + 512],
                                start=True, stop=True),
                            waits=ptoks + etoks + prev["unrot_exp"]))
                    ex = k.op("scalar", lambda s, h=h, kh=kh: s.activation(
                        expscr[:], ps_l[:, h * 1024:h * 1024 + 1024],
                        AF.Exp, scale=0.125,
                        accum_out=denp[:, h * 2 + kh:h * 2 + kh + 1]),
                        waits=m2 + prev["denp_used"])
                    etoks.append(ex)
            prev["unrot_exp"] = etoks[-2:]
            d0 = k.op("vector", lambda v, qt=qt: v.tensor_tensor(
                reccol[:, qt:qt + 1], denp[:, 0:1], denp[:, 1:2], ALU.add),
                waits=etoks + prev["reccol_used"])
            d1 = k.op("vector", lambda v, qt=qt: v.tensor_tensor(
                reccol[:, 16 + qt:16 + qt + 1], denp[:, 2:3], denp[:, 3:4],
                ALU.add), waits=etoks)
            prev["denp_used"] = [d0, d1]
            r0 = k.op("vector", lambda v, qt=qt: v.reciprocal(
                reccol[:, qt:qt + 1], reccol[:, qt:qt + 1]), waits=[d0])
            r1 = k.op("vector", lambda v, qt=qt: v.reciprocal(
                reccol[:, 16 + qt:16 + qt + 1], reccol[:, 16 + qt:16 + qt + 1]),
                waits=[d1])
            rc_toks += [r0, r1]
        cb = k.op("vector", lambda v: v.tensor_copy(rcb[:], reccol[:, 0:32]),
                  waits=rc_toks + prev["reccol_used"])
        tt = k.op("tensor", lambda t: t.transpose(
            ps_tp[0:32, 0:128], rcb[:], ident[:]),
            waits=[cb] + prev["tok_used"])
        rtT = k.op("vector", lambda v: v.tensor_copy(recT[:], ps_tp[0:32, 0:128]),
                   waits=[tt])
        prev["reccol_used"] = [rtT]
        prev["tok_used"] = prev["tok_used"] + [rtT]
        yevs = []
        for qc in range(4 if AA_F2 else 0):
            pv_last = []
            nkt = 4 * qc + 4
            for kt in range(nkt):
                pp = kt % 2
                smm = {}
                for h in (0, 1):
                    smm[h] = k.op("tensor", lambda t, h=h, kt=kt, qc=qc, pp=pp:
                                  t.matmul(
                                      ps_l[:, pp * 1024 + h * 512:
                                           pp * 1024 + h * 512 + 512],
                                      krT[h * 64:(h + 1) * 64,
                                          kt * 128:(kt + 1) * 128],
                                      qrT[h * 64:(h + 1) * 64,
                                          qc * 512:(qc + 1) * 512],
                                      start=True, stop=True),
                                  waits=ptoks + prev["rot_exp"] +
                                        prev["unrot_exp"] +
                                        prev.get("bcast_used", []))
                extoks = []
                j = kt - 4 * qc
                for h in (0, 1):
                    ex = k.op("scalar", lambda s, h=h, pp=pp: s.activation(
                        ptile[:, h * 512:(h + 1) * 512],
                        ps_l[:, pp * 1024 + h * 512:pp * 1024 + h * 512 + 512],
                        AF.Exp, scale=0.125),
                        waits=[smm[h]] + prev["pv"])
                    extoks.append(ex)
                prev["rot_exp"] = extoks
                masks = []
                if j >= 0:
                    for h in (0, 1):
                        mk = k.op("vector", lambda v, h=h, j=j: v.tensor_tensor(
                            ptile[:, h * 512 + j * 128:h * 512 + (j + 1) * 128],
                            ptile[:, h * 512 + j * 128:h * 512 + (j + 1) * 128],
                            tri[:], ALU.mult), waits=extoks)
                        masks.append(mk)
                        if j > 0:
                            mz = k.op("vector", lambda v, h=h, j=j: v.memset(
                                ptile[:, h * 512:h * 512 + j * 128], 0.0),
                                waits=extoks)
                            masks.append(mz)
                pvw = extoks + masks + ptoks
                pv2 = []
                for h in (0, 1):
                    pso = (ps_o, ps_x)[h]
                    pv = k.op("tensor", lambda t, h=h, kt=kt, qc=qc,
                              pso=pso, nkt=nkt: t.matmul(
                                  pso[h * 64:(h + 1) * 64, 0:512],
                                  vtok[:, kt * 128 + h * 64:
                                       kt * 128 + h * 64 + 64],
                                  ptile[:, h * 512:(h + 1) * 512],
                                  start=(kt == 0), stop=(kt == nkt - 1),
                                  tile_position=(0, 64 * h)),
                              waits=pvw + prev["yT_used"])
                    pv2.append(pv)
                prev["pv"] = pv2
                pv_last = pv2
            bmm = []
            for j in range(4):
                bm_ = k.op("tensor", lambda t, j=j, qc=qc: t.matmul(
                    ps_l[:, j * 128:(j + 1) * 128],
                    sel[:, (qc * 4 + j) * 128:(qc * 4 + j + 1) * 128],
                    recT[:], start=True, stop=True),
                    waits=[rtT] + prev["bc_used"] + prev["rot_exp"])
                bmm.append(bm_)
            bev = k.op("vector", lambda v: v.tensor_copy(bcsb[:], ps_l[:, 0:512]),
                       waits=bmm)
            prev["bcast_used"] = [bev]
            yevs = []
            for h in (0, 1):
                pso = (ps_o, ps_x)[h]
                yev = k.op("vector", lambda v, hp=hp, qc=qc, h=h, pso=pso:
                           v.tensor_tensor(
                               yT[h * 64:(h + 1) * 64,
                                  hp * SEQ + qc * 512:hp * SEQ + (qc + 1) * 512],
                               pso[h * 64:(h + 1) * 64, 0:512],
                               bcsb[h * 64:(h + 1) * 64, :], ALU.mult),
                           waits=[bev] + pv_last + prev["yT_used"])
                yevs.append(yev)
            prev["bc_used"] = yevs
            prev["yT_used"] = yevs
            yT_toks += yevs
        prev["pairm_used"] = yevs if yevs else prev["tok_used"]
        prev["vtok_used"] = prev["pv"] if prev["pv"] else prev["tok_used"]
    new_x_toks = []
    stt_prev = []
    last_user = []
    for ti in range(NT):
        x_tile = x_sb[:, ti * D:(ti + 1) * D]
        rs = []
        for (pso, n0) in ((ps_o, 0), (ps_x, 512)):
            mm = None
            for c in range(8):
                mm = k.op("tensor", lambda t, c=c, ti=ti, n0=n0, pso=pso:
                          t.matmul(
                              pso[:],
                              yT[:, c * SEQ + ti * 128:c * SEQ + (ti + 1) * 128],
                              wout[:, c * D + n0:c * D + n0 + 512],
                              start=(c == 0), stop=False),
                          waits=yT_toks + tw + stt_prev + prev["pv"] +
                                prev["bc_used"])
            mm = k.op("tensor", lambda t, ti=ti, n0=n0, pso=pso: t.matmul(
                pso[:], ones_row[:], bout[:, n0:n0 + 512],
                start=False, stop=True),
                waits=yT_toks + tw + stt_prev)
            rtok = k.op("vector", lambda v, x_tile=x_tile, n0=n0, pso=pso:
                        v.tensor_tensor(x_tile[:, n0:n0 + 512], pso[:],
                                        x_tile[:, n0:n0 + 512], ALU.add),
                        waits=[mm] + x_toks[ti] + ln_last)
            rs.append(rtok)
        stt_prev = rs
        new_x_toks.append(rs)
        last_user += rs
    e["aa_w_war"] = tmax(*last_user)
    return new_x_toks


# version 06
# speedup vs baseline: 1.1254x; 1.1254x over previous
import math
import numpy as np
import ml_dtypes
import concourse.bass as bass
import concourse.mybir as mybir
from concourse.bass_utils import run_bass_kernel_spmd
from concourse.alu_op_type import AluOpType as ALU
from concourse import dve_ops
from concourse.dve_op import DveOp
from concourse.dve_spec import Spec, Src0, Src1, C0, C1, Zero, maxx, minn, select

F32 = mybir.dt.float32
BF16 = mybir.dt.bfloat16
I32 = mybir.dt.int32
AF = mybir.ActivationFunctionType
AX = mybir.AxisListType.X
BF = ml_dtypes.bfloat16

D = 1024
SEQ = 2048
NSEQ = 4
NT = SEQ // 128
NH = 16
DH = 64
NPAIR = NH // 2

FF_PAR = 8
FF_NODES = 255
FF_W = FF_PAR * FF_NODES
FF_WP = 2048
FF_DEPTH = 7

FI = 2816
FI_NC = FI // 128

def _register_qops():
    if any(o.name == "QMX_SEL" for o in dve_ops.OPS):
        return
    qmx_sel = DveOp(
        "QMX_SEL",
        Spec(body=select(maxx(Src0, Zero - Src0) < C0, Src0, Src1)),
        subdim=False,
        uops_sha={},
        reference=lambda in0, in1, s0, s1, imm2: np.where(
            np.abs(in0) < s0, in0, in1).astype(np.float32),
    )
    from concourse.dve_spec import lower as _dve_lower
    for op in (qmx_sel,):
        for ver in ("v3", "v4"):
            try:
                r = _dve_lower(op.spec, ver=ver, perf_en=False, name=op.name)
                op.uops_sha[ver] = r.sha(ver)
            except Exception:
                pass
        dve_ops.OPS.append(op)
        dve_ops._SUB_OPCODE_FOR_NAME[op.name] = (
            dve_ops._CUSTOM_DVE_ROW_BASE + len(dve_ops.OPS) - 1
        )
        dve_ops.CUSTOM_DVE_REFS[op.name] = op.ref
    assert max(dve_ops._SUB_OPCODE_FOR_NAME.values()) < 0x20


ENGS = ("tensor", "vector", "scalar", "gpsimd", "sync")


class Sem:
    def __init__(self, k, name):
        self.h = k.ctx.enter_context(k.nc.semaphore(name))
        self.name = name
        self.n = 0


class K:
    def __init__(self, nc, ctx):
        self.nc = nc
        self.ctx = ctx
        self.plan = []
        self.s = {e: Sem(self, "s_" + e) for e in ENGS}
        NP = 8
        self.sdma_pool = {q: [Sem(self, f"d_{q}{i}") for i in range(NP)]
                          for q in ("sync", "gpsimd")}
        self.sdma_idx = {q: 0 for q in ("sync", "gpsimd")}
        self.last_wait = {e: {} for e in ENGS}
        self.n_emitted = {e: 0 for e in ENGS}

    def sb(self, name, shape, dt=F32):
        return self.ctx.enter_context(self.nc.sbuf_tensor(name, shape, dt))

    def ps(self, name, shape, dt=F32):
        return self.ctx.enter_context(self.nc.psum_tensor(name, shape, dt))

    def _coalesce(self, eng, waits):
        byname = {}
        for (sem, n) in waits:
            if n <= 0:
                continue
            if sem.name not in byname or byname[sem.name][1] < n:
                byname[sem.name] = (sem, n)
        wl = []
        for (sem, n) in byname.values():
            prev = self.last_wait[eng].get(sem.name, -1)
            if n > prev:
                wl.append((sem.h, n))
                self.last_wait[eng][sem.name] = n
        return wl

    def op(self, eng, f, waits=(), inc=True):
        wl = self._coalesce(eng, waits)
        psem = self.s[eng]
        tok = None
        if inc:
            psem.n += 1
            tok = (psem, psem.n)
            nn = psem.n

            def fn(e, wl=wl, f=f, nn=nn, h=psem.h):
                for (sh, n) in wl:
                    e.wait_ge(sh, n)
                f(e).then_inc(h, 1)
        else:
            def fn(e, wl=wl, f=f):
                for (sh, n) in wl:
                    e.wait_ge(sh, n)
                f(e)
        self.plan.append((eng, fn))
        self.n_emitted[eng] += 1
        return tok

    def dma(self, queue, out, in_, waits=()):
        pool = self.sdma_pool[queue]
        dsem = pool[self.sdma_idx[queue] % len(pool)]
        self.sdma_idx[queue] += 1
        wl = self._coalesce(queue, list(waits) + [(dsem, dsem.n)])
        dsem.n += 16
        tok = (dsem, dsem.n)

        def fn(e, wl=wl, out=out, in_=in_, h=dsem.h):
            for (sh, n) in wl:
                e.wait_ge(sh, n)
            e.dma_start(out=out, in_=in_).then_inc(h, 16)

        self.plan.append((queue, fn))
        self.n_emitted[queue] += 1
        return tok

    def barrier_token(self, eng):
        return (self.s[eng], self.s[eng].n)

    def replay(self):
        with self.nc.Block() as block:
            for engname in ENGS:
                items = [fn for e, fn in self.plan if e == engname]
                if not items:
                    continue
                dec = getattr(block, engname)

                def section(eng, items=items):
                    for fn in items:
                        fn(eng)

                dec(section)


def tmax(*toks):
    best = {}
    for t in toks:
        if t is None:
            continue
        sem, n = t
        if sem.name not in best or best[sem.name][1] < n:
            best[sem.name] = (sem, n)
    return list(best.values())


def host_consts():
    c = {}
    i = np.arange(DH // 2, dtype=np.float32)
    theta = np.power(np.float32(10000.0), -2.0 * (i - 1.0) / DH).astype(np.float32)
    ang = np.arange(SEQ, dtype=np.float32)[:, None] * theta[None, :]
    cos = np.cos(ang).astype(np.float32)
    sin = np.sin(ang).astype(np.float32)
    c["rope_cos"] = cos.reshape(NT, 128, 32).transpose(1, 0, 2).reshape(128, NT * 32).copy()
    c["rope_sin"] = sin.reshape(NT, 128, 32).transpose(1, 0, 2).reshape(128, NT * 32).copy()
    tri = np.tril(np.ones((128, 128), dtype=np.float32)).T
    c["tri_mask"] = tri.astype(BF)
    sel = np.zeros((4, 4, 32, 128), dtype=np.float32)
    for qc in range(4):
        for j in range(4):
            for d in range(128):
                sel[qc, j, (d // 64) * 16 + 4 * qc + j, d] = 1.0
    c["selmat"] = np.ascontiguousarray(sel.reshape(16, 32, 128).transpose(1, 0, 2).reshape(32, 16 * 128)).astype(BF)
    c["ones_col"] = np.ones((128, 1), dtype=BF)
    c["ones_row"] = np.ones((1, 128), dtype=BF)
    c["ident"] = np.eye(128, dtype=BF)
    return c


def emit_ln(k, bufs, x_ap, x_tok, out_ap, sc):
    junk, ssq, s1, mean, var, rstd, nmr = (sc["junk"], sc["ssq"], sc["s1"],
                                           sc["mean"], sc["var"], sc["rstd"],
                                           sc["nmr"])
    war = sc.get("war", [])
    t1 = k.op("scalar", lambda s: s.activation(junk[:], x_ap, AF.Square,
                                               accum_out=ssq[:]),
              waits=list(x_tok) + war)
    t2 = k.op("vector", lambda v: v.tensor_reduce(s1[:], x_ap, AX, ALU.add),
              waits=list(x_tok) + war)
    t3 = k.op("vector", lambda v: v.tensor_scalar(mean[:], s1[:], 1.0 / D, None,
                                                  ALU.mult), waits=[t2])
    t4 = k.op("vector", lambda v: v.tensor_tensor(var[:], mean[:], mean[:],
                                                  ALU.mult), waits=[t3])
    t5 = k.op("vector", lambda v: v.scalar_tensor_tensor(var[:], ssq[:], 1.0 / D,
                                                         var[:], ALU.mult,
                                                         ALU.subtract),
              waits=[t1, t4])
    t5b = k.op("vector", lambda v: v.tensor_scalar(var[:], var[:], 1e-5, None,
                                                   ALU.add), waits=[t5])
    t6 = k.op("scalar", lambda s: s.activation(rstd[:], var[:], AF.Sqrt),
              waits=[t5b])
    t7 = k.op("vector", lambda v: v.reciprocal(rstd[:], rstd[:]), waits=[t6])
    t8 = k.op("vector", lambda v: v.scalar_tensor_tensor(nmr[:], mean[:], -1.0,
                                                         rstd[:], ALU.mult,
                                                         ALU.mult),
              waits=[t7])
    t9 = k.op("scalar", lambda s: s.activation(out_ap, x_ap, AF.Identity,
                                               bias=nmr[:], scale=rstd[:]),
              waits=[t8] + list(x_tok))
    sc["war"] = [t9]
    return [t9]


def make_ln_scratch(k, tag):
    return {
        "junk": k.nc_junk_view,
        "ssq": k.sb(f"ln_sq_{tag}", [128, 1], F32),
        "s1": k.sb(f"ln_s1_{tag}", [128, 1], F32),
        "mean": k.sb(f"ln_m_{tag}", [128, 1], F32),
        "var": k.sb(f"ln_v_{tag}", [128, 1], F32),
        "rstd": k.sb(f"ln_rs_{tag}", [128, 1], F32),
        "nmr": k.sb(f"ln_nm_{tag}", [128, 1], F32),
    }


def emit_transpose(k, src_ap, nchunks, psum, dst_ap, ident, waits, war):
    ttoks = []
    for c in range(nchunks):
        tt = k.op("tensor",
                  lambda t, c=c: t.transpose(psum[:, c * 128:(c + 1) * 128],
                                             src_ap(c), ident[:]),
                  waits=list(waits) + list(war))
        ttoks.append(tt)
    ev = k.op("vector",
              lambda v: v.tensor_copy(dst_ap, psum[:, 0:nchunks * 128]),
              waits=ttoks)
    return [ev]


def emit_ffnA(k, env, seq, call, x_toks):
    nc = k.nc
    e = env
    wtag = f"fa{seq}_{call}"
    war_w = e["ffnA_w_war"] + e.get("arena_war", [])
    tw1 = k.dma("gpsimd", e["w_inT"][:].rearrange("p (c n) -> p c n", n=FF_W),
                e["in_w_inT"].rearrange("(c p) n -> p c n", p=128), waits=war_w)
    tw2 = k.dma("gpsimd", e["b_in"][:], e["in_b_in"].unsqueeze(0), waits=war_w)
    tw3 = k.dma("gpsimd", e["w_outT"][:].rearrange("p (c n) -> p c n", n=D),
                e["in_w_outT"].rearrange("(c p) n -> p c n", p=128), waits=war_w)
    wtok = [tw1, tw2, tw3]

    lnx, lnxT = e["lnx_bf"], e["lnxT"]
    acts, dec, vtree = e["acts"], e["dec"], e["vtree"]
    masked, maskedT = e["masked"], e["maskedT"]
    ps_tp, ps_l, ps_o = e["ps_tp"], e["ps_l"], e["ps_o"]
    lnsc = e["lnsc"]
    x_sb = e["x_sb"]
    ones_row, b_in = e["ones_row"], e["b_in"]
    ident = e["ident"]

    prev = {"ev2": [], "silu": [], "dec": [], "stt": [], "lnxT_used": [],
            "ln": [], "mask": []}
    new_x_toks = []
    last_user = []
    for ti in range(NT):
        x_tile = x_sb[:, ti * D:(ti + 1) * D]
        lntok = emit_ln(k, None, x_tile, x_toks[ti] + prev["lnxT_used"],
                        lnx[:], lnsc)
        ttok = emit_transpose(
            k, lambda c: lnx[:, c * 128:(c + 1) * 128], 8, ps_tp,
            lnxT[:], ident, lntok, prev["ev2"] + prev["lnxT_used"])
        prev["lnxT_used"] = []
        mm_waits = ttok + wtok + prev["silu"] + prev["dec"]
        ltoks = []
        nch = [(0, 512), (512, 1024), (1024, 1536), (1536, 2040)]
        for (n0, n1) in nch:
            for c in range(8):
                tok = k.op("tensor",
                           lambda t, c=c, n0=n0, n1=n1: t.matmul(
                               ps_l[:, n0:n1],
                               lnxT[:, c * 128:(c + 1) * 128],
                               e["w_inT"][:, c * FF_W + n0:c * FF_W + n1],
                               start=(c == 0), stop=False),
                           waits=mm_waits)
            tok = k.op("tensor",
                       lambda t, n0=n0, n1=n1: t.matmul(
                           ps_l[:, n0:n1], ones_row[:],
                           b_in[:, n0:n1], start=False, stop=True),
                       waits=mm_waits)
            ltoks.append(tok)
            prev["lnxT_used"].append(tok)
        stok = k.op("scalar", lambda s: s.activation(acts[:, 0:FF_W], ps_l[:, 0:FF_W],
                                                     AF.Silu),
                    waits=ltoks + prev["mask"])
        prev["silu"] = [stok]
        acts_t = acts[:].rearrange("p (t n) -> p t n", t=FF_PAR)
        dec_t = dec[:].rearrange("p (t n) -> p t n", t=FF_PAR)
        dtok = k.op("vector", lambda v: v.tensor_scalar(
            dec_t[:, :, 0:127], acts_t[:, :, 0:127], 0.0, None, ALU.is_gt),
            waits=[stok])
        prev["dec"] = [dtok]
        v_t = vtree[:].rearrange("p (t n) -> p t n", t=FF_PAR)
        ttk = k.op("vector", lambda v: v.memset(v_t[:, :, 0:1], 1.0),
                   waits=prev["stt"])
        cur = [ttk, dtok]
        for dlev in range(FF_DEPTH):
            pd = (1 << dlev) - 1
            cd = 1 << dlev
            c0 = (1 << (dlev + 1)) - 1
            tr = k.op("vector", lambda v, pd=pd, cd=cd, c0=c0: v.tensor_tensor(
                v_t[:, :, c0 + 1:c0 + 2 * cd:2], v_t[:, :, pd:pd + cd],
                dec_t[:, :, pd:pd + cd], ALU.mult), waits=cur)
            tl = k.op("vector", lambda v, pd=pd, cd=cd, c0=c0: v.tensor_tensor(
                v_t[:, :, c0:c0 + 2 * cd:2], v_t[:, :, pd:pd + cd],
                v_t[:, :, c0 + 1:c0 + 2 * cd:2], ALU.subtract), waits=[tr])
            cur = [tl, tr]
        mtok = k.op("vector", lambda v: v.tensor_tensor(
            masked[:, 0:FF_W], acts[:, 0:FF_W], vtree[:, 0:FF_W], ALU.mult),
            waits=cur + [stok] + prev["ev2"])
        prev["mask"] = [mtok]
        prev["stt"] = prev["stt"]
        ttok2 = emit_transpose(
            k, lambda c: masked[:, c * 128:(c + 1) * 128], 16, ps_tp,
            maskedT[:], ident, [mtok], ttok)
        prev["ev2"] = ttok2
        ow = ttok2 + wtok + prev["stt"]
        rts = []
        for (pso, n0) in ((e["ps_o"], 0), (e["ps_x"], 512)):
            for c in range(16):
                tok = k.op("tensor", lambda t, c=c, n0=n0, pso=pso: t.matmul(
                    pso[:], maskedT[:, c * 128:(c + 1) * 128],
                    e["w_outT"][:, c * D + n0:c * D + n0 + 512],
                    start=(c == 0), stop=(c == 15)), waits=ow)
            rtok = k.op("vector", lambda v, x_tile=x_tile, n0=n0, pso=pso:
                        v.scalar_tensor_tensor(
                            x_tile[:, n0:n0 + 512], pso[:], 0.5,
                            x_tile[:, n0:n0 + 512], ALU.mult, ALU.add),
                        waits=[tok] + x_toks[ti] + lntok)
            rts.append(rtok)
        prev["stt"] = rts
        new_x_toks.append(rts)
        last_user += rts
    env["ffnA_w_war"] = tmax(*last_user)
    return new_x_toks


ALL_BLOCKS = ("ffnA0", "attnA", "ffnA1", "ffnB0", "attnB", "ffnB1")


def build_program(blocks, n_seq=NSEQ):
    from contextlib import ExitStack
    _register_qops()
    nc = bass.Bass("TRN2", target_bir_lowering=False)
    din = {}
    def dram(name, shape, dt=F32, kind="ExternalInput"):
        din[name] = nc.dram_tensor(name, shape, dt, kind=kind).ap()
        return din[name]

    x_in = dram("x", [n_seq * SEQ, D])
    y_out = dram("y", [n_seq * SEQ, D], kind="ExternalOutput")
    dram("c_rope_cos", [128, NT * 32]); dram("c_rope_sin", [128, NT * 32])
    dram("c_tri", [128, 128], BF16)
    dram("c_sel", [32, 16 * 128], BF16)
    dram("c_ones_col", [128, 1], BF16); dram("c_ones_row", [1, 128], BF16)
    dram("c_ident", [128, 128], BF16)
    need = set(blocks)
    if need & {"ffnA0", "ffnA1"}:
        dram("w_inT_f", [D, FF_W]); dram("b_in_f", [FF_W]); dram("w_outT_f", [FF_WP, D])
    if "attnA" in need:
        dram("wq_a", [D, D]); dram("wk_a", [D, D]); dram("wv_a", [D, D])
        dram("wo_a", [D, D]); dram("bo_a", [D])
    if "attnB" in need:
        dram("wq_b", [D, D]); dram("wk_b", [D, D]); dram("wv_b", [D, D])
        dram("wo_b", [D, D])
        dram("bq_b", [D]); dram("bk_b", [D]); dram("bv_b", [D]); dram("bo_b", [D])
    if need & {"ffnB0", "ffnB1"}:
        dram("w_gate", [FI, D]); dram("w_up", [FI, D]); dram("w_down", [D, FI])
        din["wgqT"] = nc.dram_tensor("wgqT", [D, FI], BF16, kind="Internal").ap()
        din["wuqT"] = nc.dram_tensor("wuqT", [D, FI], BF16, kind="Internal").ap()
        din["wdqT"] = nc.dram_tensor("wdqT", [FI, D], BF16, kind="Internal").ap()

    with ExitStack() as ctx:
        k = K(nc, ctx)
        e = {"din": din}
        e["x_sb"] = k.sb("x_sb", [128, NT * D], BF16)
        e["stage"] = k.sb("stage", [128, D], F32)
        e["ident"] = k.sb("ident_sb", [128, 128], BF16)
        e["ones_row"] = k.sb("ones_row", [1, 128], BF16)
        e["ones_col"] = k.sb("ones_col", [128, 1], BF16)
        e["tri"] = k.sb("tri", [128, 128], BF16)
        e["lnx_bf"] = k.sb("lnx_bf", [128, D], BF16)
        e["lnxT"] = k.sb("lnxT", [128, D], BF16)
        k.nc_junk_view = e["lnx_bf"][:]
        e["lnsc"] = make_ln_scratch(k, "g")
        WA = 67584
        SA = 17280
        e["warena"] = k.sb("warena", [128, WA], BF16)
        e["sarena"] = k.sb("sarena", [128, SA], BF16)
        wa, sa = e["warena"], e["sarena"]
        if need & {"ffnA0", "ffnA1"}:
            e["in_w_inT"] = din["w_inT_f"]
            e["in_b_in"] = din["b_in_f"]
            e["in_w_outT"] = din["w_outT_f"]
            e["w_inT"] = wa[:, 0:8 * FF_W]
            e["b_in"] = wa[0:1, 32768:32768 + FF_W]
            e["w_outT"] = wa[:, 16320:16320 + 16 * D]
            e["acts"] = sa[:, 0:FF_W]
            e["dec"] = sa[:, 2048:2048 + FF_W]
            e["vtree"] = sa[:, 4096:4096 + FF_W]
            e["masked"] = sa[:, 6144:6144 + FF_WP]
            e["maskedT"] = sa[:, 8192:8192 + FF_WP]
            e["ffnA_w_war"] = []
        if need & {"ffnB0", "ffnB1"}:
            e["fb_wg"] = wa[:, 0:8 * FI]
            e["fb_wu"] = wa[:, 22528:45056]
            e["fb_wd"] = wa[:, 45056:45056 + FI_NC * D]
            e["fb_lnx32"] = sa[:, 0:2048].bitcast(F32)
            e["fb_xq"] = sa[:, 2048:3072]
            e["fb_xqT"] = sa[:, 3072:4096]
            qa = sa[:, 4096:6144].bitcast(F32)
            qb = sa[:, 6144:8192].bitcast(F32)
            qc = sa[:, 8192:10240].bitcast(F32)
            qbm = sa[:, 10240:10624].bitcast(F32)
            qsc = sa[:, 10624:11008].bitcast(F32)
            qrc = sa[:, 11008:11392].bitcast(F32)
            e["fb_qs"] = {"a": qa, "b": qb, "c": qc, "bm": qbm, "scale": qsc,
                          "rec": qrc}
            e["fb_hq"] = sa[:, 11392:11392 + FI]
            e["fb_hqT"] = sa[:, 14208:15232]
            e["fb_hsil"] = sa[:, 15232:16256].bitcast(F32)
            e["fb_hmul"] = sa[:, 16256:17280].bitcast(F32)
            e["wprep"] = wa[:, 0:5632].bitcast(F32)
            e["wprep_q"] = wa[:, 5632:8448]
            e["wprep_qT"] = wa[:, 8448:10496]
            e["wprep_qs"] = {"a": wa[:, 10496:16128].bitcast(F32),
                             "b": wa[:, 16128:21760].bitcast(F32),
                             "c": wa[:, 21760:27392].bitcast(F32),
                             "bm": wa[:, 27392:27744].bitcast(F32),
                             "scale": wa[:, 27744:28096].bitcast(F32),
                             "rec": wa[:, 28096:28448].bitcast(F32)}
        if "attnB" in need:
            e["ab_wq"] = wa[:, 0:8192]
            e["ab_wk"] = wa[:, 8192:16384]
            e["ab_wv"] = wa[:, 16384:24576]
            e["ab_wo"] = wa[:, 24576:32768]
            e["ab_brows"] = wa[0:1, 32768:32768 + 4 * D]
            e["ab_qm"] = sa[0:64, 0:2048]
            e["ab_km"] = sa[0:64, 2048:4096]
            e["ab_vaug"] = sa[:, 4096:4096 + 8 * 130]
            e["ab_pt0"] = sa[:, 5440:5952]
            e["ab_pt1"] = sa[:, 5952:6464]
            e["ab_aT"] = sa[:, 6464:7488]
            e["ab_rec"] = sa[:, 7488:11584].bitcast(F32)
            e["ab_bc"] = sa[0:64, 11584:15680].bitcast(F32)
            e["ab_ones"] = k.sb("ab_ones", [128, 64], BF16)
            e["ab_onesf"] = k.sb("ab_onesf", [128, 64], F32)
        if "attnA" in need:
            e["aa_lnxT"] = wa[:, 0:16384]
            e["aa_yT"] = wa[:, 16384:32768]
            e["aa_wq"] = wa[:, 32768:40960]
            e["aa_wk"] = wa[:, 40960:49152]
            e["aa_wv"] = wa[:, 49152:57344]
            e["aa_wout"] = wa[:, 57344:65536]
            e["aa_cos"] = wa[:, 65536:66560].bitcast(F32)
            e["aa_sin"] = wa[:, 66560:67584].bitcast(F32)
            e["aa_pairm"] = sa[:, 0:8192]
            e["aa_vtok"] = sa[:, 8192:10240]
            e["aa_tok4"] = sa[:, 10240:10752]
            e["aa_expscr"] = sa[:, 10752:11776]
            e["aa_ropesc"] = k.sb("aa_ropesc", [128, 128], F32)[:]
            e["aa_pt"] = sa[:, 11776:12800]
            e["aa_sel"] = sa[0:32, 12800:14848]
            e["aa_recT"] = sa[0:32, 14848:14976]
            e["aa_rcb"] = sa[:, 14976:15008]
            e["aa_reccol"] = sa[:, 15040:15104].bitcast(F32)
            e["aa_denp"] = sa[:, 15104:15112].bitcast(F32)
            e["aa_bc"] = sa[:, 15112:16136].bitcast(F32)
            e["aa_bout"] = sa[0:1, 16136:17160]
        e["ps_tp"] = k.ps("ps_tp", [128, 2048], BF16)
        e["ps_l"] = k.ps("ps_l", [128, 2048], F32)
        e["ps_o"] = k.ps("ps_o", [128, 512], F32)
        e["ps_x"] = k.ps("ps_x", [128, 512], F32)

        ct = []
        ct.append(k.dma("sync", e["ident"][:], din["c_ident"]))
        ct.append(k.dma("sync", e["ones_row"][:], din["c_ones_row"]))
        ct.append(k.dma("sync", e["ones_col"][:], din["c_ones_col"]))
        ct.append(k.dma("sync", e["tri"][:], din["c_tri"]))
        e["const_tok"] = ct
        if need & {"ffnA0", "ffnA1"}:
            k.op("vector", lambda v: v.memset(e["masked"][:, FF_W:FF_WP], 0.0))
        if "attnB" in need:
            k.op("vector", lambda v: v.memset(e["ab_ones"][:], 1.0))
            k.op("vector", lambda v: v.memset(e["ab_onesf"][:], 1.0))
            va3 = e["ab_vaug"].rearrange("p (c s) -> p c s", s=130)
            k.op("vector", lambda v: v.memset(va3[:, :, 64:65], 1.0))
            k.op("vector", lambda v: v.memset(va3[:, :, 129:130], 1.0))
        e["wprep_done"] = []
        if need & {"ffnB0", "ffnB1"}:
            emit_ffnB_weight_prep(k, e)
            e["arena_war"] = list(e["wprep_done"])

        for seq in range(n_seq):
            xv = x_in.rearrange("(s t p) d -> s p t d", s=n_seq, p=128)[seq]
            tld = k.dma("gpsimd", e["x_sb"][:].rearrange("p (t d) -> p t d", d=D),
                        xv, waits=e.get("x_war", []))
            x_toks = [[tld] + ct for _ in range(NT)]
            for b in blocks:
                if b.startswith("ffnA"):
                    x_toks = emit_ffnA(k, e, seq, b[-1], x_toks)
                elif b == "attnA":
                    x_toks = emit_attnA(k, e, seq, x_toks)
                elif b.startswith("ffnB"):
                    x_toks = emit_ffnB(k, e, seq, b[-1], x_toks)
                elif b == "attnB":
                    x_toks = emit_attnB(k, e, seq, x_toks)
                bar = tmax(*[t for toks in x_toks for t in toks])
                e["arena_war"] = bar
                x_toks = [toks + bar for toks in x_toks]
            yv = y_out.rearrange("(s t p) d -> s p t d", s=n_seq, p=128)[seq]
            st_done = []
            for ti in range(NT):
                tok = k.op("scalar", lambda s, ti=ti: s.copy(
                    e["stage"][:], e["x_sb"][:, ti * D:(ti + 1) * D]),
                    waits=x_toks[ti] + e.get("stage_war", []))
                sd = k.dma("sync", yv[:, ti, :], e["stage"][:], waits=[tok])
                e["stage_war"] = [sd]
                st_done.append(sd)
            e["x_war"] = tmax(*(t for toks in x_toks for t in toks)) + st_done
            e["final_store"] = st_done
        fw = e["final_store"]
        k.op("gpsimd", lambda g: g.engine_nop(), waits=fw, inc=False)
        k.replay()
    return nc


def _rope_deinterleave_cols(w):
    idx = np.zeros(D, dtype=np.int64)
    for h in range(NH):
        for i in range(32):
            idx[h * 64 + i] = h * 64 + 2 * i
            idx[h * 64 + 32 + i] = h * 64 + 2 * i + 1
    return np.ascontiguousarray(w[:, idx])


def host_prep(inputs, blocks):
    need = set(blocks)
    m = {}
    if need & {"ffnA0", "ffnA1"}:
        m["w_inT_f"] = np.ascontiguousarray(inputs["L0_ffn_W_in"].T)
        m["b_in_f"] = np.ascontiguousarray(inputs["L0_ffn_b_in"])
        wo = np.zeros((FF_WP, D), dtype=np.float32)
        wo[:FF_W] = inputs["L0_ffn_W_out"].T
        m["w_outT_f"] = wo
    if "attnA" in need:
        wqkv = inputs["L0_attn_Wqkv"]
        m["wq_a"] = _rope_deinterleave_cols(np.ascontiguousarray(wqkv[:, 0:D]))
        m["wk_a"] = _rope_deinterleave_cols(np.ascontiguousarray(wqkv[:, D:2 * D]))
        m["wv_a"] = np.ascontiguousarray(wqkv[:, 2 * D:3 * D])
        m["wo_a"] = np.ascontiguousarray(inputs["L0_attn_Wout"])
        m["bo_a"] = np.ascontiguousarray(inputs["L0_attn_bout"])
    if "attnB" in need:
        m["wq_b"] = np.ascontiguousarray(inputs["L1_attn_Wq"])
        m["wk_b"] = np.ascontiguousarray(inputs["L1_attn_Wk"])
        m["wv_b"] = np.ascontiguousarray(inputs["L1_attn_Wv"])
        m["wo_b"] = np.ascontiguousarray(inputs["L1_attn_Wo"])
        m["bq_b"] = np.ascontiguousarray(inputs["L1_attn_bq"])
        m["bk_b"] = np.ascontiguousarray(inputs["L1_attn_bk"])
        m["bv_b"] = np.ascontiguousarray(inputs["L1_attn_bv"])
        m["bo_b"] = np.ascontiguousarray(inputs["L1_attn_bo"])
    if need & {"ffnB0", "ffnB1"}:
        m["w_gate"] = np.ascontiguousarray(inputs["L1_ffn_w_gate"])
        m["w_up"] = np.ascontiguousarray(inputs["L1_ffn_w_up"])
        m["w_down"] = np.ascontiguousarray(inputs["L1_ffn_w_down"])
    c = host_consts()
    m["c_rope_cos"] = c["rope_cos"]; m["c_rope_sin"] = c["rope_sin"]
    m["c_tri"] = c["tri_mask"]; m["c_sel"] = c["selmat"]
    m["c_ones_col"] = c["ones_col"]; m["c_ones_row"] = c["ones_row"]
    m["c_ident"] = c["ident"]
    return m


def run_blocks(x_full, inputs, blocks, n_seq=NSEQ, n_cores=8, sim=False):
    nc = build_program(blocks, n_seq=n_seq)
    wm = host_prep(inputs, blocks)
    rows = n_seq * SEQ
    in_maps = []
    for c in range(n_cores):
        im = dict(wm)
        im["x"] = np.ascontiguousarray(x_full[c * rows:(c + 1) * rows])
        in_maps.append(im)
    if sim:
        from concourse import bass_interp
        s = bass_interp.CoreSim(nc)
        for kk, vv in in_maps[0].items():
            s.tensor(kk)[:] = vv
        s.simulate()
        return np.array(s.tensor("y"))
    res = run_bass_kernel_spmd(nc, in_maps, core_ids=list(range(n_cores)))
    return np.concatenate([r["y"] for r in res.results], axis=0)


_NC_CACHE = {}


def kernel(**inputs):
    x = np.ascontiguousarray(inputs["L0_attn_x"].reshape(32 * SEQ, D))
    n_cores = 8
    if "full1" not in _NC_CACHE:
        _NC_CACHE["full1"] = build_program(ALL_BLOCKS, n_seq=1)
    nc = _NC_CACHE["full1"]
    wm = host_prep(inputs, ALL_BLOCKS)
    for seq in range(NSEQ):
        in_maps = []
        for c in range(n_cores):
            im = dict(wm)
            r0 = (c * NSEQ + seq) * SEQ
            im["x"] = np.ascontiguousarray(x[r0:r0 + SEQ])
            in_maps.append(im)
        res = run_bass_kernel_spmd(nc, in_maps, core_ids=list(range(n_cores)))
        for c in range(n_cores):
            r0 = (c * NSEQ + seq) * SEQ
            x[r0:r0 + SEQ] = res.results[c]["y"]
    return x.reshape(32, SEQ, D)


SEXPMASK = int(np.int32(np.uint32(0xFF800000).view(np.int32)))
EXPDOWN2 = 0x01000000
MANTKEEP = int(np.int32(np.uint32(0xFFC00000).view(np.int32)))


FIXC = 6291456.0


def emit_quant(k, e, src, dst, nb, qs, waits):
    W = 16 * nb
    a, b, c = qs["a"], qs["b"], qs["c"]
    bm, scale, rec = qs["bm"], qs["scale"], qs["rec"]
    war = qs.get("war", [])
    src3 = src.rearrange("p (b s) -> p b s", s=16)
    t0 = k.op("vector", lambda v: v.tensor_reduce(
        bm[:, 0:nb].unsqueeze(2), src3, AX, ALU.max, apply_absolute_value=True),
        waits=list(waits) + war)
    t1 = k.op("vector", lambda v: v.tensor_scalar(
        scale[:, 0:nb], bm[:, 0:nb], 1.0 / 6.0, 1e-12, ALU.mult, ALU.max),
        waits=[t0])
    t2 = k.op("vector", lambda v: v.reciprocal(rec[:, 0:nb], scale[:, 0:nb]),
              waits=[t1])
    rec_bc = rec[:, 0:nb].unsqueeze(2).broadcast_to((128, nb, 16))
    sc_bc = scale[:, 0:nb].unsqueeze(2).broadcast_to((128, nb, 16))
    a3 = a[:, 0:W].rearrange("p (b s) -> p b s", s=16)
    t3 = k.op("vector", lambda v: v.tensor_tensor(a3, src3, rec_bc, ALU.mult),
              waits=[t2] + list(waits))
    ai = a[:, 0:W].bitcast(I32)
    bi = b[:, 0:W].bitcast(I32)
    ci = c[:, 0:W].bitcast(I32)
    t4a = k.op("vector", lambda v: v.tensor_scalar(
        bi, ai, SEXPMASK, None, ALU.bitwise_and), waits=[t3])
    t4 = k.op("vector", lambda v: v.tensor_scalar(
        ci, bi, EXPDOWN2, None, ALU.subtract), waits=[t4a])
    t5 = k.op("vector", lambda v: v.tensor_tensor(
        b[:, 0:W], a[:, 0:W], c[:, 0:W], ALU.add), waits=[t4])
    t6 = k.op("vector", lambda v: v.tensor_scalar(
        ci, bi, MANTKEEP, None, ALU.bitwise_and), waits=[t5])
    t6b = k.op("vector", lambda v: v.tensor_scalar(
        b[:, 0:W], a[:, 0:W], FIXC, FIXC, ALU.add, ALU.subtract), waits=[t6])
    t7a1 = k.op("vector", lambda v: v.tensor_scalar(
        ai, ai, 0x7FFFFFFF, None, ALU.bitwise_and), waits=[t6b])
    t7a = k.op("vector", lambda v: v.tensor_scalar(
        ai, ai, 0x40000000, None, ALU.is_ge), waits=[t7a1])
    t7 = k.op("vector", lambda v: v.copy_predicated(
        b[:, 0:W], ai, c[:, 0:W]), waits=[t7a])
    b3 = b[:, 0:W].rearrange("p (b s) -> p b s", s=16)
    t8 = k.op("vector", lambda v: v.tensor_tensor(
        dst.rearrange("p (b s) -> p b s", s=16), b3, sc_bc, ALU.mult),
        waits=[t7])
    qs["war"] = [t8]
    return [t8]


def emit_ffnB_weight_prep(k, e):
    qs = e["wprep_qs"]
    wtmp = e["wprep"]
    wq = e["wprep_q"]
    wqT = e["wprep_qT"]
    ident = e["ident"]
    ps_tp = e["ps_tp"]
    last = list(e["const_tok"])
    for (src_name, dst_name, nrow_ch, W) in (
            ("w_gate", "wgqT", FI_NC, D), ("w_up", "wuqT", FI_NC, D),
            ("w_down", "wdqT", 8, FI)):
        src = e["din"][src_name].rearrange("(t p) n -> t p n", p=128)
        dstv = e["din"][dst_name].rearrange("(c p) n -> p c n", p=128)
        for t in range(nrow_ch):
            ld = k.dma("sync", wtmp[:, 0:W], src[t], waits=qs.get("war", []) + last)
            qt = emit_quant(k, e, wtmp[:, 0:W], wq[:, 0:W], W // 16, qs, [ld])
            nch = W // 128
            for g0 in range(0, nch, 16):
                g1 = min(g0 + 16, nch)
                tt = emit_transpose(
                    k, lambda cc, g0=g0: wq[:, (g0 + cc) * 128:(g0 + cc + 1) * 128],
                    g1 - g0, ps_tp, wqT[:, 0:(g1 - g0) * 128], ident,
                    qt, e.get("wprep_tp_war", []))
                sd = k.dma("sync",
                           dstv[:, g0:g1, t * 128:(t + 1) * 128],
                           wqT[:, 0:(g1 - g0) * 128].rearrange(
                               "p (c j) -> p c j", j=128),
                           waits=tt)
                e["wprep_tp_war"] = [sd]
                last = [sd]
    e["wprep_done"] = last


def emit_ffnB(k, e, seq, call, x_toks):
    lnx32, xq, xqT = e["fb_lnx32"], e["fb_xq"], e["fb_xqT"]
    hsil, hmul, hq, hqT = e["fb_hsil"], e["fb_hmul"], e["fb_hq"], e["fb_hqT"]
    wg, wu, wd = e["fb_wg"], e["fb_wu"], e["fb_wd"]
    qs = e["fb_qs"]
    ps_tp, ps_l, ps_o = e["ps_tp"], e["ps_l"], e["ps_o"]
    lnsc = e["lnsc"]
    x_sb = e["x_sb"]
    ident = e["ident"]
    war_w = e.get("fb_w_war", []) + e["wprep_done"] + e.get("arena_war", [])
    tw = []
    tw.append(k.dma("sync", wg[:].rearrange("p (c n) -> p c n", n=FI),
                    e["din"]["wgqT"].rearrange("(c p) n -> p c n", p=128),
                    waits=war_w))
    tw.append(k.dma("sync", wu[:].rearrange("p (c n) -> p c n", n=FI),
                    e["din"]["wuqT"].rearrange("(c p) n -> p c n", p=128),
                    waits=war_w))
    tw.append(k.dma("sync", wd[:].rearrange("p (c n) -> p c n", n=D),
                    e["din"]["wdqT"].rearrange("(c p) n -> p c n", p=128),
                    waits=war_w))

    JCH = [(0, 512), (512, 1024), (1024, 1536), (1536, 2048), (2048, 2560),
           (2560, 2816)]
    prev = {"hq_used": [], "stt": [], "xqT_used": [], "gate_used": [],
            "hmul_used": []}
    new_x_toks = []
    last_user = list(tw)
    for ti in range(NT):
        x_tile = x_sb[:, ti * D:(ti + 1) * D]
        lntok = emit_ln(k, None, x_tile, x_toks[ti], lnx32[:, 0:D], lnsc)
        qtok = emit_quant(k, e, lnx32[:, 0:D], xq[:, 0:D], D // 16, qs, lntok)
        ttok = emit_transpose(k, lambda cc: xq[:, cc * 128:(cc + 1) * 128], 8,
                              ps_tp, xqT[:, 0:D], ident, qtok,
                              prev["xqT_used"] + prev["hq_used"])
        prev["xqT_used"] = []
        htoks = [None] * len(JCH)
        for ji, (j0, j1) in enumerate(JCH):
            jw = j1 - j0
            gw = ttok + tw + prev["gate_used"]
            gt = ut = None
            for c in range(8):
                gt = k.op("tensor", lambda t, c=c, j0=j0, j1=j1: t.matmul(
                    ps_l[:, 0:j1 - j0], xqT[:, c * 128:(c + 1) * 128],
                    wg[:, c * FI + j0:c * FI + j1],
                    start=(c == 0), stop=(c == 7)), waits=gw)
            for c in range(8):
                ut = k.op("tensor", lambda t, c=c, j0=j0, j1=j1: t.matmul(
                    ps_l[:, 512:512 + j1 - j0], xqT[:, c * 128:(c + 1) * 128],
                    wu[:, c * FI + j0:c * FI + j1],
                    start=(c == 0), stop=(c == 7)), waits=gw)
            prev["xqT_used"] = [gt, ut]
            st = k.op("scalar", lambda s, jw=jw: s.activation(
                hsil[:, 0:jw], ps_l[:, 0:jw], AF.Silu),
                waits=[gt] + prev["hmul_used"])
            ht = k.op("vector", lambda v, jw=jw: v.tensor_tensor(
                hmul[:, 0:jw], hsil[:, 0:jw], ps_l[:, 512:512 + jw], ALU.mult),
                waits=[st, ut])
            prev["gate_used"] = [ht]
            prev["hmul_used"] = [ht]
            qh = emit_quant(k, e, hmul[:, 0:jw], hq[:, j0:j1], jw // 16, qs, [ht])
            prev["hmul_used"] = prev["hmul_used"] + qh
            htoks[ji] = qh
        otoks = []
        ow_war = prev["stt"]
        dtoks = []
        for gi, (g0, g1) in enumerate(((0, 8), (8, 16), (16, 22))):
            need = [t for ji, (j0, j1) in enumerate(JCH)
                    if j0 < g1 * 128 and j1 > g0 * 128 for t in htoks[ji]]
            tt = emit_transpose(
                k, lambda cc, g0=g0: hq[:, (g0 + cc) * 128:(g0 + cc + 1) * 128],
                g1 - g0, ps_tp, hqT[:, 0:(g1 - g0) * 128], ident,
                need + prev["hq_used"], ttok + dtoks)
            prev["hq_used"] = []
            for cc in range(g0, g1):
                for (pso, n0) in ((e["ps_o"], 0), (e["ps_x"], 512)):
                    dt_ = k.op("tensor", lambda t, cc=cc, g0=g0, n0=n0, pso=pso:
                               t.matmul(pso[:],
                                        hqT[:, (cc - g0) * 128:(cc - g0 + 1) * 128],
                                        wd[:, cc * D + n0:cc * D + n0 + 512],
                                        start=(cc == 0), stop=(cc == 21)),
                               waits=tt + tw + ow_war)
                    if cc == g1 - 1:
                        otoks.append(dt_)
                    dtoks.append(dt_)
            prev["hq_used"] = [dtoks[-1]]
        rts = []
        for (pso, n0, dt_) in ((e["ps_o"], 0, otoks[-2]), (e["ps_x"], 512, otoks[-1])):
            rtok = k.op("vector", lambda v, x_tile=x_tile, n0=n0, pso=pso:
                        v.scalar_tensor_tensor(
                            x_tile[:, n0:n0 + 512], pso[:], 0.5,
                            x_tile[:, n0:n0 + 512], ALU.mult, ALU.add),
                        waits=[dt_] + x_toks[ti] + lntok)
            rts.append(rtok)
        prev["stt"] = rts
        new_x_toks.append(rts)
        last_user += rts
    e["fb_w_war"] = tmax(*last_user)
    return new_x_toks


def emit_attnB(k, e, seq, x_toks):
    x_sb = e["x_sb"]
    lnx, lnxT = e["lnx_bf"], e["lnxT"]
    lnsc = e["lnsc"]
    ident = e["ident"]
    ones_row = e["ones_row"]
    ps_tp, ps_l, ps_o, ps_x = e["ps_tp"], e["ps_l"], e["ps_o"], e["ps_x"]
    Qm, Km = e["ab_qm"], e["ab_km"]
    vaug = e["ab_vaug"]
    pt0, pt1 = e["ab_pt0"], e["ab_pt1"]
    recs = e["ab_rec"]
    bc = e["ab_bc"]
    aT = e["ab_aT"]
    onesb = e["ab_ones"]
    onesf = e["ab_onesf"]
    wq, wk, wv, wo = e["ab_wq"], e["ab_wk"], e["ab_wv"], e["ab_wo"]
    brows = e["ab_brows"]

    war_w = e.get("ab_w_war", []) + e.get("arena_war", [])
    tw = []
    for wsb, nm in ((wq, "wq_b"), (wk, "wk_b"), (wv, "wv_b"), (wo, "wo_b")):
        tw.append(k.dma("gpsimd", wsb.rearrange("p (c n) -> p c n", n=D),
                        e["din"][nm].rearrange("(c p) n -> p c n", p=128),
                        waits=war_w))
    for i, nm in enumerate(("bq_b", "bk_b", "bv_b", "bo_b")):
        tw.append(k.dma("gpsimd", brows[:, i * D:(i + 1) * D],
                        e["din"][nm].unsqueeze(0), waits=war_w))
    va3m = vaug[:].rearrange("p (c s) -> p c s", s=130)
    mz1 = k.op("vector", lambda v: v.memset(va3m[:, :, 64:65], 1.0), waits=war_w)
    mz2 = k.op("vector", lambda v: v.memset(va3m[:, :, 129:130], 1.0),
               waits=war_w)
    tw = tw + [mz1, mz2]

    prev = {"lnxT_used": [], "qk_evac": [], "v_evac": [], "exp": [],
            "pv": [], "stt": [], "aT_used": [], "bc_used": [], "rec_used": []}
    new_x_toks = []
    last_user = list(tw)
    for ti in range(NT):
        x_tile = x_sb[:, ti * D:(ti + 1) * D]
        lntok = emit_ln(k, None, x_tile, x_toks[ti] + prev["lnxT_used"], lnx[:],
                        lnsc)
        ttok = emit_transpose(k, lambda c: lnx[:, c * 128:(c + 1) * 128], 8,
                              ps_tp, lnxT[:], ident, lntok, prev["lnxT_used"])
        prev["lnxT_used"] = [ttok[0]]
        qk_toks = []
        for (dst, wsb, boff, scl, poff) in ((Qm, wq, 0, 1.0 / 1024.0, 0),
                                            (Km, wk, 1, 1.0, 1024)):
            mm = None
            for par in (0, 1):
                for c in range(8):
                    for (n0, n1) in ((0, 512), (512, 1024)):
                        mm = k.op("tensor", lambda t, c=c, n0=n0, n1=n1,
                                  par=par, wsb=wsb, poff=poff: t.matmul(
                                      ps_l[par * 64:(par + 1) * 64,
                                           poff + n0:poff + n1],
                                      lnxT[:, c * 128 + par:(c + 1) * 128:2],
                                      wsb[:, c * D + n0:c * D + n1],
                                      start=(c == 0), stop=False,
                                      tile_position=(0, 64 * par)),
                                  waits=ttok + tw + prev["qk_evac"])
                for (n0, n1) in ((0, 512), (512, 1024)):
                    mm = k.op("tensor", lambda t, n0=n0, n1=n1, par=par,
                              boff=boff, poff=poff: t.matmul(
                                  ps_l[par * 64:(par + 1) * 64,
                                       poff + n0:poff + n1],
                                  ones_row[:, 0:64],
                                  brows[:, boff * D + n0:boff * D + n1],
                                  start=False, stop=True,
                                  tile_position=(0, 64 * par)),
                              waits=ttok + tw + prev["qk_evac"])
            ev0 = k.op("scalar", lambda s, dst=dst, scl=scl, poff=poff:
                       s.activation(dst[:, 0:1024], ps_l[0:64, poff:poff + 1024],
                                    AF.Identity, scale=float(scl)),
                       waits=[mm] + prev["pv"])
            ev1 = k.op("scalar", lambda s, dst=dst, scl=scl, poff=poff:
                       s.activation(dst[:, 1024:2048],
                                    ps_l[64:128, poff:poff + 1024],
                                    AF.Identity, scale=float(scl)),
                       waits=[mm] + prev["pv"])
            qk_toks += [ev0, ev1]
        vmm = []
        vev = []
        for half in (0, 1):
            pso = (ps_o, ps_x)[half]
            hm = []
            for c2 in range(4):
                c = half * 4 + c2
                for par in (0, 1):
                    mm = None
                    for cd in range(8):
                        mm = k.op("tensor", lambda t, c=c, c2=c2, par=par,
                                  cd=cd, pso=pso: t.matmul(
                                      pso[:, c2 * 128 + par * 64:
                                          c2 * 128 + par * 64 + 64],
                                      wv[:, cd * D + c * 128:cd * D + (c + 1) * 128],
                                      lnxT[:, cd * 128 + par:(cd + 1) * 128:2],
                                      start=(cd == 0), stop=False),
                                  waits=ttok + tw + prev["v_evac"])
                    mm = k.op("tensor", lambda t, c=c, c2=c2, par=par, pso=pso:
                              t.matmul(
                                  pso[:, c2 * 128 + par * 64:
                                      c2 * 128 + par * 64 + 64],
                                  brows[:, 2 * D + c * 128:2 * D + (c + 1) * 128],
                                  onesb[0:1, 0:64],
                                  start=False, stop=True),
                              waits=ttok + tw + prev["v_evac"])
                    hm.append(mm)
            va4 = vaug[:].rearrange("p (c q s) -> p c q s", q=2, s=65)
            ev = k.op("vector", lambda v, half=half, pso=pso, va4=va4:
                      v.tensor_copy(
                          va4[:, half * 4:(half + 1) * 4, :, 0:64],
                          pso[:].rearrange("p (c q s) -> p c q s", q=2, s=64)),
                      waits=hm)
            vev.append(ev)
        vmm = vev
        pv_toks = []
        exp_toks = []
        for kt in range(16):
            par, kc = (0, kt) if kt < 8 else (1, kt - 8)
            vslice = vaug[:, kc * 130 + par * 65:kc * 130 + par * 65 + 65]
            for qc in range(4):
                pbuf = (pt0, pt1)[(kt * 4 + qc) % 2]
                pso = (ps_o, ps_x)[(kt * 4 + qc) % 2][:]
                smm = k.op("tensor", lambda t, kt=kt, qc=qc, pso=pso: t.matmul(
                    pso, Km[:, kt * 128:(kt + 1) * 128],
                    Qm[:, qc * 512:(qc + 1) * 512],
                    start=True, stop=True),
                    waits=qk_toks + prev["exp"] + vmm)
                ex = k.op("scalar", lambda s, pbuf=pbuf, pso=pso: s.activation(
                    pbuf[:], pso, AF.Exp), waits=[smm] + prev["pv"])
                exp_toks.append(ex)
                prev["exp"] = exp_toks[-2:]
                pv = k.op("tensor", lambda t, kt=kt, qc=qc, vslice=vslice,
                          pbuf=pbuf: t.matmul(
                              ps_l[0:65, qc * 512:(qc + 1) * 512],
                              vslice, pbuf[:],
                              start=(kt == 0), stop=(kt == 15)),
                          waits=[ex] + prev["aT_used"] + qk_toks)
                pv_toks.append(pv)
                prev["pv"] = pv_toks[-2:]
        rc = k.op("vector", lambda v: v.reciprocal(recs[64:65, :],
                                                   ps_l[64:65, 0:2048]),
                  waits=pv_toks[-4:] + prev["rec_used"])
        bcm = []
        for qc in range(4):
            psb = (ps_o, ps_x)[qc % 2]
            bm_ = k.op("tensor", lambda t, qc=qc, psb=psb: t.matmul(
                psb[0:64, 0:512],
                onesf[64:65, 0:64],
                recs[64:65, qc * 512:(qc + 1) * 512],
                start=True, stop=True),
                waits=[rc] + prev["bc_used"] + prev["exp"] + bcm[-2:])
            bv_ = k.op("vector", lambda v, qc=qc, psb=psb: v.tensor_copy(
                bc[:, qc * 512:(qc + 1) * 512], psb[0:64, 0:512]),
                waits=[bm_])
            bcm.append(bv_)
        prev["rec_used"] = bcm[-1:]
        prev["bc_used"] = []
        aT_toks = []
        for c in range(8):
            for half in (0, 1):
                at = k.op("vector", lambda v, c=c, half=half: v.tensor_tensor(
                    aT[half * 64:half * 64 + 64, c * 128:(c + 1) * 128],
                    ps_l[0:64, 2 * c + half:2048:16],
                    bc[:, 2 * c + half:2048:16], ALU.mult),
                    waits=bcm + pv_toks[-4:]),
                aT_toks.append(at[0])
        prev["bc_used"] = aT_toks
        omm = []
        rts = []
        ow = aT_toks + tw + prev["stt"]
        for (pso, n0) in ((ps_o, 0), (ps_x, 512)):
            mm = None
            for c in range(8):
                mm = k.op("tensor", lambda t, c=c, n0=n0, pso=pso: t.matmul(
                    pso[:], aT[:, c * 128:(c + 1) * 128],
                    wo[:, c * D + n0:c * D + n0 + 512],
                    start=(c == 0), stop=False), waits=ow)
            mm = k.op("tensor", lambda t, n0=n0, pso=pso: t.matmul(
                pso[:], ones_row[:],
                brows[:, 3 * D + n0:3 * D + n0 + 512], start=False, stop=True),
                waits=ow)
            omm.append(mm)
            rtok = k.op("vector", lambda v, x_tile=x_tile, n0=n0, pso=pso:
                        v.tensor_tensor(x_tile[:, n0:n0 + 512], pso[:],
                                        x_tile[:, n0:n0 + 512], ALU.add),
                        waits=[mm] + x_toks[ti] + lntok)
            rts.append(rtok)
        prev["aT_used"] = omm
        prev["stt"] = rts
        prev["qk_evac"] = rts
        prev["v_evac"] = rts
        new_x_toks.append(rts)
        last_user += rts
    e["ab_w_war"] = tmax(*last_user)
    return new_x_toks


def emit_attnA(k, e, seq, x_toks):
    import os
    AA_NPAIR = int(os.environ.get("AA_NPAIR", str(NPAIR)))
    AA_F1 = int(os.environ.get("AA_F1", "1"))
    AA_F2 = int(os.environ.get("AA_F2", "1"))
    x_sb = e["x_sb"]
    lnx = e["lnx_bf"]
    lnsc = e["lnsc"]
    ident, ones_row, tri = e["ident"], e["ones_row"], e["tri"]
    ps_tp, ps_l, ps_o, ps_x = e["ps_tp"], e["ps_l"], e["ps_o"], e["ps_x"]
    lnxT = e["aa_lnxT"]
    yT = e["aa_yT"]
    wq, wk, wv = e["aa_wq"], e["aa_wk"], e["aa_wv"]
    wout = e["aa_wout"]
    cosf, sinf = e["aa_cos"], e["aa_sin"]
    tok4 = e["aa_tok4"]
    pairm = e["aa_pairm"]
    vtok = e["aa_vtok"]
    expscr = e["aa_expscr"]
    ptile = e["aa_pt"]
    sel = e["aa_sel"]
    recT = e["aa_recT"]
    rcb = e["aa_rcb"]
    reccol = e["aa_reccol"]
    denp = e["aa_denp"]
    bcsb = e["aa_bc"]
    bout = e["aa_bout"]
    ropesc = e["aa_ropesc"]

    war_w = e.get("aa_w_war", []) + e.get("arena_war", [])
    tw = []
    for wsb, nm in ((wq, "wq_a"), (wk, "wk_a"), (wv, "wv_a"), (wout, "wo_a")):
        tw.append(k.dma("gpsimd", wsb.rearrange("p (c n) -> p c n", n=D),
                        e["din"][nm].rearrange("(c p) n -> p c n", p=128),
                        waits=war_w))
    tw.append(k.dma("gpsimd", bout[:], e["din"]["bo_a"].unsqueeze(0), waits=war_w))
    tw.append(k.dma("sync", cosf, e["din"]["c_rope_cos"], waits=war_w))
    tw.append(k.dma("sync", sinf, e["din"]["c_rope_sin"], waits=war_w))
    tw.append(k.dma("sync", sel[:], e["din"]["c_sel"], waits=war_w))

    lnxT_toks = []
    prev_tp = []
    ln_last = []
    tp_war = []
    for ti in range(NT):
        x_tile = x_sb[:, ti * D:(ti + 1) * D]
        lntok = emit_ln(k, None, x_tile, x_toks[ti] + war_w + tp_war, lnx[:],
                        lnsc)
        ln_last.append(lntok[0])
        dst = lnxT[:].rearrange("p (c n) -> p c n", n=SEQ)[
            :, :, ti * 128:(ti + 1) * 128]
        tt = emit_transpose(k, lambda c: lnx[:, c * 128:(c + 1) * 128], 8,
                            ps_tp, dst, ident, lntok, prev_tp + tp_war)
        prev_tp = tt
        tp_war = [k.barrier_token("tensor")]
        lnxT_toks += tt

    prev = {"pairm_used": [], "vtok_used": [], "unrot_exp": [], "rot_exp": [],
            "pv": [], "bc_used": [], "yT_used": [], "tok_used": [],
            "denp_used": [], "reccol_used": [], "ropescr_war": []}
    yT_toks = []
    qrT = pairm[:, 0 * SEQ:1 * SEQ]
    qT = pairm[:, 1 * SEQ:2 * SEQ]
    krT = pairm[:, 2 * SEQ:3 * SEQ]
    kT = pairm[:, 3 * SEQ:4 * SEQ]
    for hp in range(AA_NPAIR):
        ptoks = []
        for ti in range(NT):
            pw = lnxT_toks + tw + prev["tok_used"] + prev["unrot_exp"] + \
                prev["rot_exp"] + prev["bc_used"] + prev.get("bcast_used", [])
            mm = {}
            for (slot, wsb) in ((0, wq), (1, wk), (2, wv)):
                m = None
                for c in range(8):
                    m = k.op("tensor", lambda t, c=c, hp=hp, ti=ti, slot=slot,
                             wsb=wsb: t.matmul(
                                 ps_l[:, slot * 512:slot * 512 + 128],
                                 lnxT[:, c * SEQ + ti * 128:c * SEQ + (ti + 1) * 128],
                                 wsb[:, c * D + hp * 128:c * D + (hp + 1) * 128],
                                 start=(c == 0), stop=(c == 7)),
                             waits=pw if slot < 2 else
                             (lnxT_toks + tw + prev["vtok_used"] + pw))
                mm[slot] = m
            evq = k.op("scalar", lambda s: s.copy(tok4[:, 0:128], ps_l[:, 0:128]),
                       waits=[mm[2]] + prev["tok_used"])
            evk = k.op("scalar", lambda s: s.copy(tok4[:, 256:384],
                                                  ps_l[:, 512:640]),
                       waits=[mm[2]] + prev["tok_used"])
            evv = k.op("scalar", lambda s, ti=ti: s.copy(
                vtok[:, ti * 128:(ti + 1) * 128], ps_l[:, 1024:1152]),
                waits=[mm[2]] + prev["vtok_used"])
            rtoks = []
            AA_ROPE = int(os.environ.get("AA_ROPE", "1"))
            for (src0, out0, src_ev) in (((0, 128, evq), (256, 384, evk))
                                         if AA_ROPE else ()):
                pv3 = tok4[:, src0:src0 + 128].rearrange("p (h i) -> p h i", i=64)
                te, to = pv3[:, :, 0:32], pv3[:, :, 32:64]
                cosb = cosf[:, ti * 32:(ti + 1) * 32].unsqueeze(1).broadcast_to(
                    (128, 2, 32))
                sinb = sinf[:, ti * 32:(ti + 1) * 32].unsqueeze(1).broadcast_to(
                    (128, 2, 32))
                sc3 = ropesc[:].rearrange("p (h i) -> p h i", i=64)
                sc_a, sc_b = sc3[:, :, 0:32], sc3[:, :, 32:64]
                o3 = tok4[:, out0:out0 + 128].rearrange("p (h i) -> p h i", i=64)
                ro_re, ro_ro = o3[:, :, 0:32], o3[:, :, 32:64]
                w0 = [src_ev] + prev["tok_used"] + prev["ropescr_war"] + rtoks
                RN = int(os.environ.get("AA_RN", "6"))
                r1 = k.op("vector", lambda v, te=te, cosb=cosb, sc_a=sc_a:
                          v.tensor_tensor(sc_a, te, cosb, ALU.mult), waits=w0)
                last = r1
                if RN >= 2:
                    r2 = k.op("vector", lambda v, to=to, sinb=sinb, sc_b=sc_b:
                              v.tensor_tensor(sc_b, to, sinb, ALU.mult), waits=w0)
                    last = r2
                if RN >= 3:
                    r3 = k.op("vector", lambda v, sc_a=sc_a, sc_b=sc_b, ro_re=ro_re:
                              v.tensor_tensor(ro_re, sc_a, sc_b, ALU.subtract),
                              waits=[r1, r2])
                    last = r3
                if RN >= 4:
                    r4 = k.op("vector", lambda v, te=te, sinb=sinb, sc_a=sc_a:
                              v.tensor_tensor(sc_a, te, sinb, ALU.mult), waits=[r3])
                    last = r4
                if RN >= 5:
                    r5 = k.op("vector", lambda v, to=to, cosb=cosb, sc_b=sc_b:
                              v.tensor_tensor(sc_b, to, cosb, ALU.mult), waits=[r3])
                    last = r5
                if RN >= 6:
                    r6 = k.op("vector", lambda v, sc_a=sc_a, sc_b=sc_b, ro_ro=ro_ro:
                              v.tensor_tensor(ro_ro, sc_a, sc_b, ALU.add),
                              waits=[r4, r5])
                    last = r6
                rtoks += ([r3, r6] if RN >= 6 else [last])
            prev["ropescr_war"] = rtoks[-2:]
            ttw = [evq, evk] + rtoks + prev["pairm_used"]
            tts = []
            for (src0, m) in ((128, 0), (0, 1), (384, 2), (256, 3)):
                tt = k.op("tensor", lambda t, src0=src0, m=m: t.transpose(
                    ps_tp[:, m * 128:(m + 1) * 128],
                    tok4[:, src0:src0 + 128], ident[:]), waits=ttw)
                tts.append(tt)
            pm = pairm[:].rearrange("p (m n) -> p m n", n=SEQ)
            ev = k.op("vector", lambda v, ti=ti, pm=pm: v.tensor_copy(
                pm[:, :, ti * 128:(ti + 1) * 128],
                ps_tp[:, 0:512].rearrange("p (m n) -> p m n", n=128)),
                waits=tts)
            ptoks.append(ev)
            ptoks.append(evv)
            prev["tok_used"] = [tts[-1], ev]
            prev["vtok_used"] = [evv]
        rc_toks = []
        for qt in range(NT if AA_F1 else 0):
            etoks = []
            for h in (0, 1):
                for kh in (0, 1):
                    m2 = []
                    for nn in (0, 512):
                        m2.append(k.op(
                            "tensor", lambda t, h=h, kh=kh, nn=nn, qt=qt:
                            t.matmul(
                                ps_l[:, h * 1024 + nn:h * 1024 + nn + 512],
                                qT[h * 64:(h + 1) * 64, qt * 128:(qt + 1) * 128],
                                kT[h * 64:(h + 1) * 64,
                                   kh * 1024 + nn:kh * 1024 + nn + 512],
                                start=True, stop=True),
                            waits=ptoks + etoks + prev["unrot_exp"]))
                    ex = k.op("scalar", lambda s, h=h, kh=kh: s.activation(
                        expscr[:], ps_l[:, h * 1024:h * 1024 + 1024],
                        AF.Exp, scale=0.125,
                        accum_out=denp[:, h * 2 + kh:h * 2 + kh + 1]),
                        waits=m2 + prev["denp_used"])
                    etoks.append(ex)
            prev["unrot_exp"] = etoks[-2:]
            d0 = k.op("vector", lambda v, qt=qt: v.tensor_tensor(
                reccol[:, qt:qt + 1], denp[:, 0:1], denp[:, 1:2], ALU.add),
                waits=etoks + prev["reccol_used"])
            d1 = k.op("vector", lambda v, qt=qt: v.tensor_tensor(
                reccol[:, 16 + qt:16 + qt + 1], denp[:, 2:3], denp[:, 3:4],
                ALU.add), waits=etoks)
            prev["denp_used"] = [d0, d1]
            r0 = k.op("vector", lambda v, qt=qt: v.reciprocal(
                reccol[:, qt:qt + 1], reccol[:, qt:qt + 1]), waits=[d0])
            r1 = k.op("vector", lambda v, qt=qt: v.reciprocal(
                reccol[:, 16 + qt:16 + qt + 1], reccol[:, 16 + qt:16 + qt + 1]),
                waits=[d1])
            rc_toks += [r0, r1]
        cb = k.op("vector", lambda v: v.tensor_copy(rcb[:], reccol[:, 0:32]),
                  waits=rc_toks + prev["reccol_used"])
        tt = k.op("tensor", lambda t: t.transpose(
            ps_tp[0:32, 0:128], rcb[:], ident[:]),
            waits=[cb] + prev["tok_used"])
        rtT = k.op("vector", lambda v: v.tensor_copy(recT[:], ps_tp[0:32, 0:128]),
                   waits=[tt])
        prev["reccol_used"] = [rtT]
        prev["tok_used"] = prev["tok_used"] + [rtT]
        yevs = []
        for qc in range(4 if AA_F2 else 0):
            pv_last = []
            nkt = 4 * qc + 4
            for kt in range(nkt):
                pp = kt % 2
                smm = {}
                for h in (0, 1):
                    smm[h] = k.op("tensor", lambda t, h=h, kt=kt, qc=qc, pp=pp:
                                  t.matmul(
                                      ps_l[:, pp * 1024 + h * 512:
                                           pp * 1024 + h * 512 + 512],
                                      krT[h * 64:(h + 1) * 64,
                                          kt * 128:(kt + 1) * 128],
                                      qrT[h * 64:(h + 1) * 64,
                                          qc * 512:(qc + 1) * 512],
                                      start=True, stop=True),
                                  waits=ptoks + prev["rot_exp"] +
                                        prev["unrot_exp"] +
                                        prev.get("bcast_used", []))
                extoks = []
                j = kt - 4 * qc
                for h in (0, 1):
                    ex = k.op("scalar", lambda s, h=h, pp=pp: s.activation(
                        ptile[:, h * 512:(h + 1) * 512],
                        ps_l[:, pp * 1024 + h * 512:pp * 1024 + h * 512 + 512],
                        AF.Exp, scale=0.125),
                        waits=[smm[h]] + prev["pv"])
                    extoks.append(ex)
                prev["rot_exp"] = extoks
                masks = []
                if j >= 0:
                    for h in (0, 1):
                        mk = k.op("vector", lambda v, h=h, j=j: v.tensor_tensor(
                            ptile[:, h * 512 + j * 128:h * 512 + (j + 1) * 128],
                            ptile[:, h * 512 + j * 128:h * 512 + (j + 1) * 128],
                            tri[:], ALU.mult), waits=extoks)
                        masks.append(mk)
                        if j > 0:
                            mz = k.op("vector", lambda v, h=h, j=j: v.memset(
                                ptile[:, h * 512:h * 512 + j * 128], 0.0),
                                waits=extoks)
                            masks.append(mz)
                pvw = extoks + masks + ptoks
                pv2 = []
                for h in (0, 1):
                    pso = (ps_o, ps_x)[h]
                    pv = k.op("tensor", lambda t, h=h, kt=kt, qc=qc,
                              pso=pso, nkt=nkt: t.matmul(
                                  pso[h * 64:(h + 1) * 64, 0:512],
                                  vtok[:, kt * 128 + h * 64:
                                       kt * 128 + h * 64 + 64],
                                  ptile[:, h * 512:(h + 1) * 512],
                                  start=(kt == 0), stop=(kt == nkt - 1),
                                  tile_position=(0, 64 * h)),
                              waits=pvw + prev["yT_used"])
                    pv2.append(pv)
                prev["pv"] = pv2
                pv_last = pv2
            bmm = []
            for j in range(4):
                bm_ = k.op("tensor", lambda t, j=j, qc=qc: t.matmul(
                    ps_l[:, j * 128:(j + 1) * 128],
                    sel[:, (qc * 4 + j) * 128:(qc * 4 + j + 1) * 128],
                    recT[:], start=True, stop=True),
                    waits=[rtT] + prev["bc_used"] + prev["rot_exp"])
                bmm.append(bm_)
            bev = k.op("vector", lambda v: v.tensor_copy(bcsb[:], ps_l[:, 0:512]),
                       waits=bmm)
            prev["bcast_used"] = [bev]
            yevs = []
            for h in (0, 1):
                pso = (ps_o, ps_x)[h]
                yev = k.op("vector", lambda v, hp=hp, qc=qc, h=h, pso=pso:
                           v.tensor_tensor(
                               yT[h * 64:(h + 1) * 64,
                                  hp * SEQ + qc * 512:hp * SEQ + (qc + 1) * 512],
                               pso[h * 64:(h + 1) * 64, 0:512],
                               bcsb[h * 64:(h + 1) * 64, :], ALU.mult),
                           waits=[bev] + pv_last + prev["yT_used"])
                yevs.append(yev)
            prev["bc_used"] = yevs
            prev["yT_used"] = yevs
            yT_toks += yevs
        prev["pairm_used"] = yevs if yevs else prev["tok_used"]
        prev["vtok_used"] = prev["pv"] if prev["pv"] else prev["tok_used"]
    new_x_toks = []
    stt_prev = []
    last_user = []
    for ti in range(NT):
        x_tile = x_sb[:, ti * D:(ti + 1) * D]
        rs = []
        for (pso, n0) in ((ps_o, 0), (ps_x, 512)):
            mm = None
            for c in range(8):
                mm = k.op("tensor", lambda t, c=c, ti=ti, n0=n0, pso=pso:
                          t.matmul(
                              pso[:],
                              yT[:, c * SEQ + ti * 128:c * SEQ + (ti + 1) * 128],
                              wout[:, c * D + n0:c * D + n0 + 512],
                              start=(c == 0), stop=False),
                          waits=yT_toks + tw + stt_prev + prev["pv"] +
                                prev["bc_used"])
            mm = k.op("tensor", lambda t, ti=ti, n0=n0, pso=pso: t.matmul(
                pso[:], ones_row[:], bout[:, n0:n0 + 512],
                start=False, stop=True),
                waits=yT_toks + tw + stt_prev)
            rtok = k.op("vector", lambda v, x_tile=x_tile, n0=n0, pso=pso:
                        v.tensor_tensor(x_tile[:, n0:n0 + 512], pso[:],
                                        x_tile[:, n0:n0 + 512], ALU.add),
                        waits=[mm] + x_toks[ti] + ln_last)
            rs.append(rtok)
        stt_prev = rs
        new_x_toks.append(rs)
        last_user += rs
    e["aa_w_war"] = tmax(*last_user)
    return new_x_toks


# version 08
# speedup vs baseline: 1.2162x; 1.0806x over previous
import math
import numpy as np
import ml_dtypes
import concourse.bass as bass
import concourse.mybir as mybir
from concourse.bass_utils import run_bass_kernel_spmd
from concourse.alu_op_type import AluOpType as ALU
from concourse import dve_ops
from concourse.dve_op import DveOp
from concourse.dve_spec import Spec, Src0, Src1, C0, C1, Zero, maxx, minn, select

F32 = mybir.dt.float32
BF16 = mybir.dt.bfloat16
I32 = mybir.dt.int32
AF = mybir.ActivationFunctionType
AX = mybir.AxisListType.X
BF = ml_dtypes.bfloat16

D = 1024
SEQ = 2048
NSEQ = 4
NT = SEQ // 128
NH = 16
DH = 64
NPAIR = NH // 2

FF_PAR = 8
FF_NODES = 255
FF_W = FF_PAR * FF_NODES
FF_WP = 2048
FF_DEPTH = 7

FI = 2816
FI_NC = FI // 128

def _register_qops():
    if any(o.name == "QMX_SEL" for o in dve_ops.OPS):
        return
    qmx_sel = DveOp(
        "QMX_SEL",
        Spec(body=select(maxx(Src0, Zero - Src0) < C0, Src0, Src1)),
        subdim=False,
        uops_sha={},
        reference=lambda in0, in1, s0, s1, imm2: np.where(
            np.abs(in0) < s0, in0, in1).astype(np.float32),
    )
    from concourse.dve_spec import lower as _dve_lower
    for op in (qmx_sel,):
        for ver in ("v3", "v4"):
            try:
                r = _dve_lower(op.spec, ver=ver, perf_en=False, name=op.name)
                op.uops_sha[ver] = r.sha(ver)
            except Exception:
                pass
        dve_ops.OPS.append(op)
        dve_ops._SUB_OPCODE_FOR_NAME[op.name] = (
            dve_ops._CUSTOM_DVE_ROW_BASE + len(dve_ops.OPS) - 1
        )
        dve_ops.CUSTOM_DVE_REFS[op.name] = op.ref
    assert max(dve_ops._SUB_OPCODE_FOR_NAME.values()) < 0x20


ENGS = ("tensor", "vector", "scalar", "gpsimd", "sync")


class Sem:
    def __init__(self, k, name):
        self.h = k.ctx.enter_context(k.nc.semaphore(name))
        self.name = name
        self.n = 0


class K:
    def __init__(self, nc, ctx):
        self.nc = nc
        self.ctx = ctx
        self.plan = []
        self.s = {e: Sem(self, "s_" + e) for e in ENGS}
        NP = 8
        self.sdma_pool = {q: [Sem(self, f"d_{q}{i}") for i in range(NP)]
                          for q in ("sync", "gpsimd")}
        self.sdma_idx = {q: 0 for q in ("sync", "gpsimd")}
        self.last_wait = {e: {} for e in ENGS}
        self.n_emitted = {e: 0 for e in ENGS}
        self.phase = "pre"
        self.suppress = False
        self.P = {}
        self.Kper = {}
        self.allsems = list(self.s.values()) + [x for q in self.sdma_pool
                                                for x in self.sdma_pool[q]]
        self.body_waits = {e: set() for e in ENGS}
        self.semmap = {sem.name: sem for sem in self.allsems}
        self.cur_i = None
        self.cur_regs = None
        self.n_iter = NSEQ

    def begin_loop(self):
        self.P = {sem.name: sem.n for sem in self.allsems}
        self._lw_snap = {e: dict(self.last_wait[e]) for e in ENGS}
        self._idx_snap = dict(self.sdma_idx)
        self._semn_snap = {sem.name: sem.n for sem in self.allsems}
        self.suppress = True
        self.phase = "body"

    def mid_loop(self):
        self.Kper = {sem.name: sem.n - self.P[sem.name] for sem in self.allsems}
        self.last_wait = {e: dict(self._lw_snap[e]) for e in ENGS}
        self.sdma_idx = dict(self._idx_snap)
        self.suppress = False

    def end_loop(self):
        self.phase = "post"

    def _classify(self, sem, c):
        if self.phase == "pre" or not self.P:
            return ("imm", c)
        P = self.P[sem.name]
        if c <= P:
            return ("imm", c)
        K_ = self.Kper.get(sem.name)
        if K_ is None:
            return ("rel", c - P)
        if self.phase == "post":
            return ("imm", c + (self.n_iter - 2) * K_)
        return ("rel", c - P - K_)

    def sb(self, name, shape, dt=F32):
        return self.ctx.enter_context(self.nc.sbuf_tensor(name, shape, dt))

    def ps(self, name, shape, dt=F32):
        return self.ctx.enter_context(self.nc.psum_tensor(name, shape, dt))

    def _coalesce(self, eng, waits):
        byname = {}
        for (sem, n) in waits:
            if n <= 0:
                continue
            if sem.name not in byname or byname[sem.name][1] < n:
                byname[sem.name] = (sem, n)
        wl = []
        for (sem, n) in byname.values():
            kind, v = self._classify(sem, n)
            key = (sem.name, kind)
            prev = self.last_wait[eng].get(key, None)
            if prev is None or v > prev:
                wl.append((sem, kind, v))
                self.last_wait[eng][key] = v
                if kind == "rel":
                    self.body_waits[eng].add(sem.name)
        return wl

    def _emit_waits(self, e, eng, wl):
        for (sem, kind, v) in wl:
            if kind == "imm":
                e.wait_ge(sem.h, v)
            else:
                regs = self.cur_regs
                base, scr = regs["base"][sem.name], regs["scr"]
                if v >= 0:
                    e.reg_alu(scr, base, v, ALU.add)
                else:
                    e.reg_alu(scr, base, -v, ALU.subtract)
                    if self.P[sem.name] + v < 0:
                        e.reg_alu(scr, scr, 0, ALU.max)
                e.wait_ge(sem.h, scr)

    def op(self, eng, f, waits=(), inc=True):
        wl = self._coalesce(eng, waits)
        psem = self.s[eng]
        tok = None
        if inc:
            psem.n += 1
            tok = (psem, psem.n)

            def fn(e, wl=wl, f=f, h=psem.h, eng=eng):
                self._emit_waits(e, eng, wl)
                f(e).then_inc(h, 1)
        else:
            def fn(e, wl=wl, f=f, eng=eng):
                self._emit_waits(e, eng, wl)
                f(e)
        if not self.suppress:
            self.plan.append((eng, self.phase, fn))
        self.n_emitted[eng] += 1
        return tok

    def dma(self, queue, out, in_, waits=()):
        pool = self.sdma_pool[queue]
        dsem = pool[self.sdma_idx[queue] % len(pool)]
        self.sdma_idx[queue] += 1
        wl = self._coalesce(queue, list(waits) + [(dsem, dsem.n)])
        dsem.n += 16
        tok = (dsem, dsem.n)

        def fn(e, wl=wl, out=out, in_=in_, h=dsem.h, eng=queue):
            self._emit_waits(e, eng, wl)
            o = out(self) if callable(out) else out
            i = in_(self) if callable(in_) else in_
            e.dma_start(out=o, in_=i).then_inc(h, 16)

        if not self.suppress:
            self.plan.append((queue, self.phase, fn))
        self.n_emitted[queue] += 1
        return tok

    def barrier_token(self, eng):
        return (self.s[eng], self.s[eng].n)

    def replay(self):
        looped = bool(self.Kper)
        with self.nc.Block() as block:
            for engname in ENGS:
                pre = [fn for e, ph, fn in self.plan if e == engname and ph == "pre"]
                body = [fn for e, ph, fn in self.plan if e == engname and ph == "body"]
                post = [fn for e, ph, fn in self.plan if e == engname and ph == "post"]
                if not (pre or body or post):
                    continue
                dec = getattr(block, engname)

                def section(eng, pre=pre, body=body, post=post, engname=engname):
                    for fn in pre:
                        fn(eng)
                    if body:
                        if looped:
                            regs = {"scr": eng.alloc_register(f"scr_{engname}"),
                                    "base": {}}
                            for sname in sorted(self.body_waits[engname]):
                                regs["base"][sname] = eng.alloc_register(
                                    f"b_{engname}_{sname}")
                            with eng.Fori(0, self.n_iter) as i:
                                self.cur_i = i
                                self.cur_regs = regs
                                for sname, r in regs["base"].items():
                                    eng.reg_alu(r, i, self.Kper[sname], ALU.mult)
                                    eng.reg_alu(r, r, self.P[sname], ALU.add)
                                for fn in body:
                                    fn(eng)
                        else:
                            for fn in body:
                                fn(eng)
                    for fn in post:
                        fn(eng)

                dec(section)


def tmax(*toks):
    best = {}
    for t in toks:
        if t is None:
            continue
        sem, n = t
        if sem.name not in best or best[sem.name][1] < n:
            best[sem.name] = (sem, n)
    return list(best.values())


def host_consts():
    c = {}
    i = np.arange(DH // 2, dtype=np.float32)
    theta = np.power(np.float32(10000.0), -2.0 * (i - 1.0) / DH).astype(np.float32)
    ang = np.arange(SEQ, dtype=np.float32)[:, None] * theta[None, :]
    cos = np.cos(ang).astype(np.float32)
    sin = np.sin(ang).astype(np.float32)
    c["rope_cos"] = cos.reshape(NT, 128, 32).transpose(1, 0, 2).reshape(128, NT * 32).copy()
    c["rope_sin"] = sin.reshape(NT, 128, 32).transpose(1, 0, 2).reshape(128, NT * 32).copy()
    tri = np.tril(np.ones((128, 128), dtype=np.float32)).T
    c["tri_mask"] = tri.astype(BF)
    sel = np.zeros((4, 4, 32, 128), dtype=np.float32)
    for qc in range(4):
        for j in range(4):
            for d in range(128):
                sel[qc, j, (d // 64) * 16 + 4 * qc + j, d] = 1.0
    c["selmat"] = np.ascontiguousarray(sel.reshape(16, 32, 128).transpose(1, 0, 2).reshape(32, 16 * 128)).astype(BF)
    c["ones_col"] = np.ones((128, 1), dtype=BF)
    c["ones_row"] = np.ones((1, 128), dtype=BF)
    c["ident"] = np.eye(128, dtype=BF)
    return c


def emit_ln(k, bufs, x_ap, x_tok, out_ap, sc):
    junk, ssq, s1, mean, var, rstd, nmr = (sc["junk"], sc["ssq"], sc["s1"],
                                           sc["mean"], sc["var"], sc["rstd"],
                                           sc["nmr"])
    war = sc.get("war", [])
    t1 = k.op("scalar", lambda s: s.activation(junk[:], x_ap, AF.Square,
                                               accum_out=ssq[:]),
              waits=list(x_tok) + war)
    t2 = k.op("vector", lambda v: v.tensor_reduce(s1[:], x_ap, AX, ALU.add),
              waits=list(x_tok) + war)
    t3 = k.op("vector", lambda v: v.tensor_scalar(mean[:], s1[:], 1.0 / D, None,
                                                  ALU.mult), waits=[t2])
    t4 = k.op("vector", lambda v: v.tensor_tensor(var[:], mean[:], mean[:],
                                                  ALU.mult), waits=[t3])
    t5 = k.op("vector", lambda v: v.scalar_tensor_tensor(var[:], ssq[:], 1.0 / D,
                                                         var[:], ALU.mult,
                                                         ALU.subtract),
              waits=[t1, t4])
    t5b = k.op("vector", lambda v: v.tensor_scalar(var[:], var[:], 1e-5, None,
                                                   ALU.add), waits=[t5])
    t6 = k.op("scalar", lambda s: s.activation(rstd[:], var[:], AF.Sqrt),
              waits=[t5b])
    t7 = k.op("vector", lambda v: v.reciprocal(rstd[:], rstd[:]), waits=[t6])
    t8 = k.op("vector", lambda v: v.scalar_tensor_tensor(nmr[:], mean[:], -1.0,
                                                         rstd[:], ALU.mult,
                                                         ALU.mult),
              waits=[t7])
    t9 = k.op("scalar", lambda s: s.activation(out_ap, x_ap, AF.Identity,
                                               bias=nmr[:], scale=rstd[:]),
              waits=[t8] + list(x_tok))
    sc["war"] = [t9]
    return [t9]


def make_ln_scratch(k, tag):
    return {
        "junk": k.nc_junk_view,
        "ssq": k.sb(f"ln_sq_{tag}", [128, 1], F32),
        "s1": k.sb(f"ln_s1_{tag}", [128, 1], F32),
        "mean": k.sb(f"ln_m_{tag}", [128, 1], F32),
        "var": k.sb(f"ln_v_{tag}", [128, 1], F32),
        "rstd": k.sb(f"ln_rs_{tag}", [128, 1], F32),
        "nmr": k.sb(f"ln_nm_{tag}", [128, 1], F32),
    }


def emit_transpose(k, src_ap, nchunks, psum, dst_ap, ident, waits, war):
    ttoks = []
    for c in range(nchunks):
        tt = k.op("tensor",
                  lambda t, c=c: t.transpose(psum[:, c * 128:(c + 1) * 128],
                                             src_ap(c), ident[:]),
                  waits=list(waits) + list(war))
        ttoks.append(tt)
    ev = k.op("vector",
              lambda v: v.tensor_copy(dst_ap, psum[:, 0:nchunks * 128]),
              waits=ttoks)
    return [ev]


def emit_ffnA(k, env, seq, call, x_toks):
    nc = k.nc
    e = env
    wtag = f"fa{seq}_{call}"
    war_w = e["ffnA_w_war"] + e.get("arena_war", [])
    tw1 = k.dma("gpsimd", e["w_inT"][:].rearrange("p (c n) -> p c n", n=FF_W),
                e["in_w_inT"].rearrange("(c p) n -> p c n", p=128), waits=war_w)
    tw2 = k.dma("gpsimd", e["b_in"][:], e["in_b_in"].unsqueeze(0), waits=war_w)
    tw3 = k.dma("gpsimd", e["w_outT"][:].rearrange("p (c n) -> p c n", n=D),
                e["in_w_outT"].rearrange("(c p) n -> p c n", p=128), waits=war_w)
    wtok = [tw1, tw2, tw3]

    lnx, lnxT = e["lnx_bf"], e["lnxT"]
    vtree = e["vtree"]
    masked, maskedT = e["masked"], e["maskedT"]
    ps_tp, ps_l = e["ps_tp"], e["ps_l"]
    lnsc = e["lnsc"]
    x_sb = e["x_sb"]
    ones_row, b_in = e["ones_row"], e["b_in"]
    ident = e["ident"]

    nch = [(0, 512), (512, 1024), (1024, 1536), (1536, 2040)]
    st = {
        "ln": [[] for _ in range(NT)],
        "lg_last": [],
        "evA": [],
        "silu": [[], []],
        "dec": [[], []],
        "mask": [[], []],
        "treelast": [[], []],
        "psl_free": [],
        "evB": [],
        "stt": [],
        "maskprev": [],
    }
    acts2, dec2 = e["acts2"], e["dec2"]
    new_x_toks = [None] * NT
    last_user = []

    def stageA(ti):
        s = ti % 2
        x_tile = x_sb[:, ti * D:(ti + 1) * D]
        lntok = emit_ln(k, None, x_tile, x_toks[ti] + st["lg_last"],
                        lnx[:], lnsc)
        st["ln"][ti] = lntok
        ttok = emit_transpose(
            k, lambda c: lnx[:, c * 128:(c + 1) * 128], 8, ps_tp[:, 0:1024],
            lnxT[:], ident, lntok, st["evA"] + st["lg_last"])
        st["evA"] = ttok
        mm_waits = ttok + wtok + st["psl_free"]
        ltoks = []
        for (n0, n1) in nch:
            for c in range(8):
                k.op("tensor",
                     lambda t, c=c, n0=n0, n1=n1: t.matmul(
                         ps_l[:, n0:n1],
                         lnxT[:, c * 128:(c + 1) * 128],
                         e["w_inT"][:, c * FF_W + n0:c * FF_W + n1],
                         start=(c == 0), stop=False),
                     waits=mm_waits)
            tok = k.op("tensor",
                       lambda t, n0=n0, n1=n1: t.matmul(
                           ps_l[:, n0:n1], ones_row[:],
                           b_in[:, n0:n1], start=False, stop=True),
                       waits=mm_waits)
            ltoks.append(tok)
        st["lg_last"] = [ltoks[-1]]
        acts = acts2[s]
        stok = k.op("scalar", lambda sc_, acts=acts: sc_.activation(
            acts[:, 0:FF_W], ps_l[:, 0:FF_W], AF.Silu),
            waits=ltoks + st["mask"][s])
        st["silu"][s] = [stok]
        st["psl_free"] = [stok]
        acts_t = acts[:].rearrange("p (t n) -> p t n", t=FF_PAR)
        dec_t = dec2[s][:].rearrange("p (t n) -> p t n", t=FF_PAR)
        dtok = k.op("vector", lambda v, acts_t=acts_t, dec_t=dec_t:
                    v.tensor_scalar(
                        dec_t[:, :, 0:127], acts_t[:, :, 0:127], 0.0, None,
                        ALU.is_gt),
                    waits=[stok] + st["treelast"][s])
        st["dec"][s] = [dtok]

    def stageB(ti):
        s = ti % 2
        x_tile = x_sb[:, ti * D:(ti + 1) * D]
        acts = acts2[s]
        dec_t = dec2[s][:].rearrange("p (t n) -> p t n", t=FF_PAR)
        v_t = vtree[:].rearrange("p (t n) -> p t n", t=FF_PAR)
        ttk = k.op("vector", lambda v: v.memset(v_t[:, :, 0:1], 1.0),
                   waits=st["maskprev"])
        cur = [ttk] + st["dec"][s]
        tl = tr = None
        for dlev in range(FF_DEPTH):
            pd = (1 << dlev) - 1
            cd = 1 << dlev
            c0 = (1 << (dlev + 1)) - 1
            tr = k.op("vector", lambda v, pd=pd, cd=cd, c0=c0, dec_t=dec_t:
                      v.tensor_tensor(
                          v_t[:, :, c0 + 1:c0 + 2 * cd:2],
                          v_t[:, :, pd:pd + cd],
                          dec_t[:, :, pd:pd + cd], ALU.mult), waits=cur)
            tl = k.op("vector", lambda v, pd=pd, cd=cd, c0=c0: v.tensor_tensor(
                v_t[:, :, c0:c0 + 2 * cd:2], v_t[:, :, pd:pd + cd],
                v_t[:, :, c0 + 1:c0 + 2 * cd:2], ALU.subtract), waits=[tr])
            cur = [tl, tr]
        st["treelast"][s] = [tl, tr]
        mtok = k.op("vector", lambda v, acts=acts: v.tensor_tensor(
            masked[:, 0:FF_W], acts[:, 0:FF_W], vtree[:, 0:FF_W], ALU.mult),
            waits=cur + st["silu"][s] + st["evB"])
        st["mask"][s] = [mtok]
        st["maskprev"] = [mtok]
        ev1 = emit_transpose(
            k, lambda c: masked[:, c * 128:(c + 1) * 128], 8,
            ps_tp[:, 1024:2048], maskedT[:, 0:1024], ident, [mtok], st["evB"])
        ev2 = emit_transpose(
            k, lambda c: masked[:, (8 + c) * 128:(9 + c) * 128], 8,
            ps_tp[:, 1024:2048], maskedT[:, 1024:2048], ident, [mtok], ev1)
        st["evB"] = ev1 + ev2
        ow = ev1 + ev2 + wtok + st["stt"]
        rts = []
        for (pso, n0) in ((e["ps_o"], 0), (e["ps_x"], 512)):
            tok = None
            for c in range(16):
                tok = k.op("tensor", lambda t, c=c, n0=n0, pso=pso: t.matmul(
                    pso[:], maskedT[:, c * 128:(c + 1) * 128],
                    e["w_outT"][:, c * D + n0:c * D + n0 + 512],
                    start=(c == 0), stop=(c == 15)), waits=ow)
            rtok = k.op("vector", lambda v, x_tile=x_tile, n0=n0, pso=pso:
                        v.scalar_tensor_tensor(
                            x_tile[:, n0:n0 + 512], pso[:], 0.5,
                            x_tile[:, n0:n0 + 512], ALU.mult, ALU.add),
                        waits=[tok] + x_toks[ti] + st["ln"][ti])
            rts.append(rtok)
        st["stt"] = rts
        new_x_toks[ti] = rts
        last_user.extend(rts)

    stageA(0)
    for ti in range(NT):
        if ti + 1 < NT:
            stageA(ti + 1)
        stageB(ti)
    env["ffnA_w_war"] = tmax(*last_user)
    return new_x_toks


ALL_BLOCKS = ("ffnA0", "attnA", "ffnA1", "ffnB0", "attnB", "ffnB1")


def build_program(blocks, n_seq=NSEQ, fused=False):
    from contextlib import ExitStack
    _register_qops()
    nc = bass.Bass("TRN2", target_bir_lowering=False)
    din = {}
    def dram(name, shape, dt=F32, kind="ExternalInput"):
        din[name] = nc.dram_tensor(name, shape, dt, kind=kind).ap()
        return din[name]

    x_in = dram("x", [n_seq * SEQ, D])
    y_out = dram("y", [n_seq * SEQ, D], kind="ExternalOutput")
    dram("c_rope_cos", [128, NT * 32]); dram("c_rope_sin", [128, NT * 32])
    dram("c_tri", [128, 128], BF16)
    dram("c_sel", [32, 16 * 128], BF16)
    dram("c_ones_col", [128, 1], BF16); dram("c_ones_row", [1, 128], BF16)
    dram("c_ident", [128, 128], BF16)
    need = set(blocks)
    if need & {"ffnA0", "ffnA1"}:
        dram("w_inT_f", [D, FF_W]); dram("b_in_f", [FF_W]); dram("w_outT_f", [FF_WP, D])
    if "attnA" in need:
        dram("wf_a", [D, 3 * D]); dram("wo_a", [D, D]); dram("bo_a", [D])
    if "attnB" in need:
        dram("wq_b", [D, D]); dram("wk_b", [D, D]); dram("wv_b", [D, D])
        dram("wo_b", [D, D])
        dram("bq_b", [D]); dram("bk_b", [D]); dram("bv_b", [D]); dram("bo_b", [D])
    if need & {"ffnB0", "ffnB1"}:
        dram("w_gate", [FI, D]); dram("w_up", [FI, D]); dram("w_down", [D, FI])
        din["wgqT"] = nc.dram_tensor("wgqT", [D, FI], BF16, kind="Internal").ap()
        din["wuqT"] = nc.dram_tensor("wuqT", [D, FI], BF16, kind="Internal").ap()
        din["wdqT"] = nc.dram_tensor("wdqT", [FI, D], BF16, kind="Internal").ap()

    with ExitStack() as ctx:
        k = K(nc, ctx)
        e = {"din": din}
        e["x_sb"] = k.sb("x_sb", [128, NT * D], BF16)
        e["stage"] = k.sb("stage", [128, D], F32)
        e["ident"] = k.sb("ident_sb", [128, 128], BF16)
        e["ones_row"] = k.sb("ones_row", [1, 128], BF16)
        e["ones_col"] = k.sb("ones_col", [128, 1], BF16)
        e["tri"] = k.sb("tri", [128, 128], BF16)
        e["lnx_bf"] = k.sb("lnx_bf", [128, D], BF16)
        e["lnxT"] = k.sb("lnxT", [128, D], BF16)
        k.nc_junk_view = e["lnx_bf"][:]
        e["lnsc"] = make_ln_scratch(k, "g")
        WA = 67584
        SA = 17280
        e["warena"] = k.sb("warena", [128, WA], BF16)
        e["sarena"] = k.sb("sarena", [128, SA], BF16)
        wa, sa = e["warena"], e["sarena"]
        if need & {"ffnA0", "ffnA1"}:
            e["in_w_inT"] = din["w_inT_f"]
            e["in_b_in"] = din["b_in_f"]
            e["in_w_outT"] = din["w_outT_f"]
            e["w_inT"] = wa[:, 0:8 * FF_W]
            e["b_in"] = wa[0:1, 32768:32768 + FF_W]
            e["w_outT"] = wa[:, 16320:16320 + 16 * D]
            e["acts2"] = [sa[:, 0:FF_W], sa[:, 10240:10240 + FF_W]]
            e["dec2"] = [sa[:, 2048:2048 + FF_W], sa[:, 12288:12288 + FF_W]]
            e["vtree"] = sa[:, 4096:4096 + FF_W]
            e["masked"] = sa[:, 6144:6144 + FF_WP]
            e["maskedT"] = sa[:, 8192:8192 + FF_WP]
            e["ffnA_w_war"] = []
        if need & {"ffnB0", "ffnB1"}:
            e["fb_wg"] = wa[:, 0:8 * FI]
            e["fb_wu"] = wa[:, 22528:45056]
            e["fb_wd"] = wa[:, 45056:45056 + FI_NC * D]
            e["fb_lnx32"] = sa[:, 0:2048].bitcast(F32)
            e["fb_xq"] = sa[:, 2048:3072]
            e["fb_xqT"] = sa[:, 3072:4096]
            qa = sa[:, 4096:6144].bitcast(F32)
            qb = sa[:, 6144:8192].bitcast(F32)
            qc = sa[:, 8192:10240].bitcast(F32)
            qbm = sa[:, 10240:10624].bitcast(F32)
            qsc = sa[:, 10624:11008].bitcast(F32)
            qrc = sa[:, 11008:11392].bitcast(F32)
            e["fb_qs"] = {"a": qa, "b": qb, "c": qc, "bm": qbm, "scale": qsc,
                          "rec": qrc}
            e["fb_hq"] = sa[:, 11392:11392 + FI]
            e["fb_hqT"] = sa[:, 14208:15232]
            e["fb_hsil"] = sa[:, 15232:16256].bitcast(F32)
            e["fb_hmul"] = sa[:, 16256:17280].bitcast(F32)
            e["wprep"] = wa[:, 0:5632].bitcast(F32)
            e["wprep_q"] = wa[:, 5632:8448]
            e["wprep_qT"] = wa[:, 8448:10496]
            e["wprep_qs"] = {"a": wa[:, 10496:16128].bitcast(F32),
                             "b": wa[:, 16128:21760].bitcast(F32),
                             "c": wa[:, 21760:27392].bitcast(F32),
                             "bm": wa[:, 27392:27744].bitcast(F32),
                             "scale": wa[:, 27744:28096].bitcast(F32),
                             "rec": wa[:, 28096:28448].bitcast(F32)}
        if "attnB" in need:
            e["ab_wq"] = wa[:, 0:8192]
            e["ab_wk"] = wa[:, 8192:16384]
            e["ab_wv"] = wa[:, 16384:24576]
            e["ab_wo"] = wa[:, 24576:32768]
            e["ab_brows"] = wa[0:1, 32768:32768 + 4 * D]
            e["ab_qm"] = sa[0:64, 0:2048]
            e["ab_km"] = sa[0:64, 2048:4096]
            e["ab_vaug"] = sa[:, 4096:4096 + 8 * 130]
            e["ab_pt0"] = sa[:, 5440:5952]
            e["ab_pt1"] = sa[:, 5952:6464]
            e["ab_aT"] = sa[:, 6464:7488]
            e["ab_rec"] = sa[:, 7488:11584].bitcast(F32)
            e["ab_bc"] = sa[0:64, 11584:15680].bitcast(F32)
            e["ab_ones"] = k.sb("ab_ones", [128, 64], BF16)
            e["ab_onesf"] = k.sb("ab_onesf", [128, 64], F32)
        if "attnA" in need:
            e["aa_lnxT"] = wa[:, 0:16384]
            e["aa_yT"] = wa[:, 16384:32768]
            e["aa_wf"] = wa[:, 32768:57344]
            e["aa_wout"] = wa[:, 57344:65536]
            e["aa_cos"] = wa[:, 65536:66560].bitcast(F32)
            e["aa_sin"] = wa[:, 66560:67584].bitcast(F32)
            e["aa_pairm"] = sa[:, 0:8192]
            e["aa_vtok"] = sa[:, 8192:10240]
            e["aa_tok4"] = sa[:, 10240:10752]
            e["aa_expscr"] = sa[:, 10752:11776]
            e["aa_ropesc"] = k.sb("aa_ropesc", [128, 128], F32)[:]
            e["aa_pt"] = sa[:, 11776:12800]
            e["aa_sel"] = sa[0:32, 12800:14848]
            e["aa_recT"] = sa[0:32, 14848:14976]
            e["aa_rcb"] = sa[:, 14976:15008]
            e["aa_reccol"] = sa[:, 15040:15104].bitcast(F32)
            e["aa_denp"] = sa[:, 15104:15112].bitcast(F32)
            e["aa_bc"] = sa[:, 15112:16136].bitcast(F32)
            e["aa_bout"] = sa[0:1, 16136:17160]
        e["ps_tp"] = k.ps("ps_tp", [128, 2048], BF16)
        e["ps_l"] = k.ps("ps_l", [128, 2048], F32)
        e["ps_o"] = k.ps("ps_o", [128, 512], F32)
        e["ps_x"] = k.ps("ps_x", [128, 512], F32)

        ct = []
        ct.append(k.dma("sync", e["ident"][:], din["c_ident"]))
        ct.append(k.dma("sync", e["ones_row"][:], din["c_ones_row"]))
        ct.append(k.dma("sync", e["ones_col"][:], din["c_ones_col"]))
        ct.append(k.dma("sync", e["tri"][:], din["c_tri"]))
        e["const_tok"] = ct
        if need & {"ffnA0", "ffnA1"}:
            k.op("vector", lambda v: v.memset(e["masked"][:, FF_W:FF_WP], 0.0))
        if "attnB" in need:
            k.op("vector", lambda v: v.memset(e["ab_ones"][:], 1.0))
            k.op("vector", lambda v: v.memset(e["ab_onesf"][:], 1.0))
        e["wprep_done"] = []
        if need & {"ffnB0", "ffnB1"}:
            emit_ffnB_weight_prep(k, e)
            e["arena_war"] = list(e["wprep_done"])

        x4 = x_in.rearrange("(s t p) d -> s p t d", s=n_seq, p=128)
        y4 = y_out.rearrange("(s t p) d -> s p t d", s=n_seq, p=128)

        def seq_body(seq):
            if fused:
                xin = lambda kk: x4[bass.ds(kk.cur_i, 1)].squeeze(0)
            else:
                xin = x4[seq]
            tld = k.dma("gpsimd", e["x_sb"][:].rearrange("p (t d) -> p t d", d=D),
                        xin, waits=e.get("x_war", []))
            x_toks = [[tld] + ct for _ in range(NT)]
            for b in blocks:
                if b.startswith("ffnA"):
                    x_toks = emit_ffnA(k, e, seq, b[-1], x_toks)
                elif b == "attnA":
                    x_toks = emit_attnA(k, e, seq, x_toks)
                elif b.startswith("ffnB"):
                    x_toks = emit_ffnB(k, e, seq, b[-1], x_toks)
                elif b == "attnB":
                    x_toks = emit_attnB(k, e, seq, x_toks)
                bar = tmax(*[t for toks in x_toks for t in toks])
                e["arena_war"] = bar
                x_toks = [toks + bar for toks in x_toks]
            st_done = []
            for ti in range(NT):
                tok = k.op("scalar", lambda s, ti=ti: s.copy(
                    e["stage"][:], e["x_sb"][:, ti * D:(ti + 1) * D]),
                    waits=x_toks[ti] + e.get("stage_war", []))
                if fused:
                    yv = lambda kk, ti=ti: y4[bass.ds(kk.cur_i, 1)].squeeze(0)[:, ti, :]
                else:
                    yv = y4[seq][:, ti, :]
                sd = k.dma("sync", yv, e["stage"][:], waits=[tok])
                e["stage_war"] = [sd]
                st_done.append(sd)
            e["x_war"] = tmax(*(t for toks in x_toks for t in toks)) + st_done
            e["final_store"] = st_done

        if fused:
            k.n_iter = n_seq
            k.begin_loop()
            seq_body(0)
            k.mid_loop()
            seq_body(1)
            k.end_loop()
        else:
            for seq in range(n_seq):
                seq_body(seq)
        fw = e["final_store"]
        k.op("gpsimd", lambda g: g.engine_nop(), waits=fw, inc=False)
        k.replay()
    return nc


def _rope_deinterleave_cols(w):
    idx = np.zeros(D, dtype=np.int64)
    for h in range(NH):
        for i in range(32):
            idx[h * 64 + i] = h * 64 + 2 * i
            idx[h * 64 + 32 + i] = h * 64 + 2 * i + 1
    return np.ascontiguousarray(w[:, idx])


def host_prep(inputs, blocks):
    need = set(blocks)
    m = {}
    if need & {"ffnA0", "ffnA1"}:
        m["w_inT_f"] = np.ascontiguousarray(inputs["L0_ffn_W_in"].T)
        m["b_in_f"] = np.ascontiguousarray(inputs["L0_ffn_b_in"])
        wo = np.zeros((FF_WP, D), dtype=np.float32)
        wo[:FF_W] = inputs["L0_ffn_W_out"].T
        m["w_outT_f"] = wo
    if "attnA" in need:
        wqkv = inputs["L0_attn_Wqkv"]
        wq = _rope_deinterleave_cols(np.ascontiguousarray(wqkv[:, 0:D]))
        wk = _rope_deinterleave_cols(np.ascontiguousarray(wqkv[:, D:2 * D]))
        wv = np.ascontiguousarray(wqkv[:, 2 * D:3 * D])
        wf = np.empty((D, 3 * D), dtype=np.float32)
        for hp in range(NPAIR):
            wf[:, hp * 384:hp * 384 + 128] = wq[:, hp * 128:(hp + 1) * 128]
            wf[:, hp * 384 + 128:hp * 384 + 256] = wk[:, hp * 128:(hp + 1) * 128]
            wf[:, hp * 384 + 256:hp * 384 + 384] = wv[:, hp * 128:(hp + 1) * 128]
        m["wf_a"] = wf
        m["wo_a"] = np.ascontiguousarray(inputs["L0_attn_Wout"])
        m["bo_a"] = np.ascontiguousarray(inputs["L0_attn_bout"])
    if "attnB" in need:
        m["wq_b"] = np.ascontiguousarray(inputs["L1_attn_Wq"])
        m["wk_b"] = np.ascontiguousarray(inputs["L1_attn_Wk"])
        m["wv_b"] = np.ascontiguousarray(inputs["L1_attn_Wv"])
        m["wo_b"] = np.ascontiguousarray(inputs["L1_attn_Wo"])
        m["bq_b"] = np.ascontiguousarray(inputs["L1_attn_bq"])
        m["bk_b"] = np.ascontiguousarray(inputs["L1_attn_bk"])
        m["bv_b"] = np.ascontiguousarray(inputs["L1_attn_bv"])
        m["bo_b"] = np.ascontiguousarray(inputs["L1_attn_bo"])
    if need & {"ffnB0", "ffnB1"}:
        m["w_gate"] = np.ascontiguousarray(inputs["L1_ffn_w_gate"])
        m["w_up"] = np.ascontiguousarray(inputs["L1_ffn_w_up"])
        m["w_down"] = np.ascontiguousarray(inputs["L1_ffn_w_down"])
    c = host_consts()
    m["c_rope_cos"] = c["rope_cos"]; m["c_rope_sin"] = c["rope_sin"]
    m["c_tri"] = c["tri_mask"]; m["c_sel"] = c["selmat"]
    m["c_ones_col"] = c["ones_col"]; m["c_ones_row"] = c["ones_row"]
    m["c_ident"] = c["ident"]
    return m


def run_blocks(x_full, inputs, blocks, n_seq=NSEQ, n_cores=8, sim=False,
               fused=False):
    nc = build_program(blocks, n_seq=n_seq, fused=fused)
    wm = host_prep(inputs, blocks)
    rows = n_seq * SEQ
    in_maps = []
    for c in range(n_cores):
        im = dict(wm)
        im["x"] = np.ascontiguousarray(x_full[c * rows:(c + 1) * rows])
        in_maps.append(im)
    if sim:
        from concourse import bass_interp
        s = bass_interp.CoreSim(nc)
        for kk, vv in in_maps[0].items():
            s.tensor(kk)[:] = vv
        s.simulate()
        return np.array(s.tensor("y"))
    res = run_bass_kernel_spmd(nc, in_maps, core_ids=list(range(n_cores)))
    return np.concatenate([r["y"] for r in res.results], axis=0)


_NC_CACHE = {}


def kernel(**inputs):
    x = np.ascontiguousarray(inputs["L0_attn_x"].reshape(32 * SEQ, D))
    n_cores = 8
    if "fused" not in _NC_CACHE:
        _NC_CACHE["fused"] = build_program(ALL_BLOCKS, n_seq=NSEQ, fused=True)
    nc = _NC_CACHE["fused"]
    wm = host_prep(inputs, ALL_BLOCKS)
    rows = NSEQ * SEQ
    in_maps = []
    for c in range(n_cores):
        im = dict(wm)
        im["x"] = np.ascontiguousarray(x[c * rows:(c + 1) * rows])
        in_maps.append(im)
    res = run_bass_kernel_spmd(nc, in_maps, core_ids=list(range(n_cores)))
    return np.concatenate([r["y"] for r in res.results],
                          axis=0).reshape(32, SEQ, D)


SEXPMASK = int(np.int32(np.uint32(0xFF800000).view(np.int32)))
EXPDOWN2 = 0x01000000
MANTKEEP = int(np.int32(np.uint32(0xFFC00000).view(np.int32)))


FIXC = 6291456.0


def emit_quant(k, e, src, dst, nb, qs, waits):
    W = 16 * nb
    a, b, c = qs["a"], qs["b"], qs["c"]
    bm, scale, rec = qs["bm"], qs["scale"], qs["rec"]
    war = qs.get("war", [])
    src3 = src.rearrange("p (b s) -> p b s", s=16)
    t0 = k.op("vector", lambda v: v.tensor_reduce(
        bm[:, 0:nb].unsqueeze(2), src3, AX, ALU.max, apply_absolute_value=True),
        waits=list(waits) + war)
    t1 = k.op("vector", lambda v: v.tensor_scalar(
        scale[:, 0:nb], bm[:, 0:nb], 1.0 / 6.0, 1e-12, ALU.mult, ALU.max),
        waits=[t0])
    t2 = k.op("vector", lambda v: v.reciprocal(rec[:, 0:nb], scale[:, 0:nb]),
              waits=[t1])
    rec_bc = rec[:, 0:nb].unsqueeze(2).broadcast_to((128, nb, 16))
    sc_bc = scale[:, 0:nb].unsqueeze(2).broadcast_to((128, nb, 16))
    a3 = a[:, 0:W].rearrange("p (b s) -> p b s", s=16)
    t3 = k.op("vector", lambda v: v.tensor_tensor(a3, src3, rec_bc, ALU.mult),
              waits=[t2] + list(waits))
    ai = a[:, 0:W].bitcast(I32)
    bi = b[:, 0:W].bitcast(I32)
    ci = c[:, 0:W].bitcast(I32)
    t4a = k.op("vector", lambda v: v.tensor_scalar(
        bi, ai, SEXPMASK, None, ALU.bitwise_and), waits=[t3])
    t4 = k.op("vector", lambda v: v.tensor_scalar(
        ci, bi, EXPDOWN2, None, ALU.subtract), waits=[t4a])
    t5 = k.op("vector", lambda v: v.tensor_tensor(
        b[:, 0:W], a[:, 0:W], c[:, 0:W], ALU.add), waits=[t4])
    t6 = k.op("vector", lambda v: v.tensor_scalar(
        ci, bi, MANTKEEP, None, ALU.bitwise_and), waits=[t5])
    t6b = k.op("vector", lambda v: v.tensor_scalar(
        b[:, 0:W], a[:, 0:W], FIXC, FIXC, ALU.add, ALU.subtract), waits=[t6])
    t7a1 = k.op("vector", lambda v: v.tensor_scalar(
        ai, ai, 0x7FFFFFFF, None, ALU.bitwise_and), waits=[t6b])
    t7a = k.op("vector", lambda v: v.tensor_scalar(
        ai, ai, 0x40000000, None, ALU.is_ge), waits=[t7a1])
    t7 = k.op("vector", lambda v: v.copy_predicated(
        b[:, 0:W], ai, c[:, 0:W]), waits=[t7a])
    b3 = b[:, 0:W].rearrange("p (b s) -> p b s", s=16)
    t8 = k.op("vector", lambda v: v.tensor_tensor(
        dst.rearrange("p (b s) -> p b s", s=16), b3, sc_bc, ALU.mult),
        waits=[t7])
    qs["war"] = [t8]
    return [t8]


def emit_ffnB_weight_prep(k, e):
    qs = e["wprep_qs"]
    wtmp = e["wprep"]
    wq = e["wprep_q"]
    wqT = e["wprep_qT"]
    ident = e["ident"]
    ps_tp = e["ps_tp"]
    last = list(e["const_tok"])
    for (src_name, dst_name, nrow_ch, W) in (
            ("w_gate", "wgqT", FI_NC, D), ("w_up", "wuqT", FI_NC, D),
            ("w_down", "wdqT", 8, FI)):
        src = e["din"][src_name].rearrange("(t p) n -> t p n", p=128)
        dstv = e["din"][dst_name].rearrange("(c p) n -> p c n", p=128)
        for t in range(nrow_ch):
            ld = k.dma("sync", wtmp[:, 0:W], src[t], waits=qs.get("war", []) + last)
            qt = emit_quant(k, e, wtmp[:, 0:W], wq[:, 0:W], W // 16, qs, [ld])
            nch = W // 128
            for g0 in range(0, nch, 16):
                g1 = min(g0 + 16, nch)
                tt = emit_transpose(
                    k, lambda cc, g0=g0: wq[:, (g0 + cc) * 128:(g0 + cc + 1) * 128],
                    g1 - g0, ps_tp, wqT[:, 0:(g1 - g0) * 128], ident,
                    qt, e.get("wprep_tp_war", []))
                sd = k.dma("sync",
                           dstv[:, g0:g1, t * 128:(t + 1) * 128],
                           wqT[:, 0:(g1 - g0) * 128].rearrange(
                               "p (c j) -> p c j", j=128),
                           waits=tt)
                e["wprep_tp_war"] = [sd]
                last = [sd]
    e["wprep_done"] = last


def emit_ffnB(k, e, seq, call, x_toks):
    lnx32, xq, xqT = e["fb_lnx32"], e["fb_xq"], e["fb_xqT"]
    hsil, hmul, hq, hqT = e["fb_hsil"], e["fb_hmul"], e["fb_hq"], e["fb_hqT"]
    wg, wu, wd = e["fb_wg"], e["fb_wu"], e["fb_wd"]
    qs = e["fb_qs"]
    ps_tp, ps_l, ps_o = e["ps_tp"], e["ps_l"], e["ps_o"]
    lnsc = e["lnsc"]
    x_sb = e["x_sb"]
    ident = e["ident"]
    war_w = e.get("fb_w_war", []) + e["wprep_done"] + e.get("arena_war", [])
    tw = []
    tw.append(k.dma("sync", wg[:].rearrange("p (c n) -> p c n", n=FI),
                    e["din"]["wgqT"].rearrange("(c p) n -> p c n", p=128),
                    waits=war_w))
    tw.append(k.dma("sync", wu[:].rearrange("p (c n) -> p c n", n=FI),
                    e["din"]["wuqT"].rearrange("(c p) n -> p c n", p=128),
                    waits=war_w))
    tw.append(k.dma("sync", wd[:].rearrange("p (c n) -> p c n", n=D),
                    e["din"]["wdqT"].rearrange("(c p) n -> p c n", p=128),
                    waits=war_w))

    JCH = [(0, 512), (512, 1024), (1024, 1536), (1536, 2048), (2048, 2560),
           (2560, 2816)]
    prev = {"hq_used": [], "stt": [], "xqT_used": [], "gate_used": [],
            "hmul_used": []}
    new_x_toks = []
    last_user = list(tw)
    for ti in range(NT):
        x_tile = x_sb[:, ti * D:(ti + 1) * D]
        lntok = emit_ln(k, None, x_tile, x_toks[ti], lnx32[:, 0:D], lnsc)
        qtok = emit_quant(k, e, lnx32[:, 0:D], xq[:, 0:D], D // 16, qs, lntok)
        ttok = emit_transpose(k, lambda cc: xq[:, cc * 128:(cc + 1) * 128], 8,
                              ps_tp, xqT[:, 0:D], ident, qtok,
                              prev["xqT_used"] + prev["hq_used"])
        prev["xqT_used"] = []
        htoks = [None] * len(JCH)
        for ji, (j0, j1) in enumerate(JCH):
            jw = j1 - j0
            gw = ttok + tw + prev["gate_used"]
            gt = ut = None
            for c in range(8):
                gt = k.op("tensor", lambda t, c=c, j0=j0, j1=j1: t.matmul(
                    ps_l[:, 0:j1 - j0], xqT[:, c * 128:(c + 1) * 128],
                    wg[:, c * FI + j0:c * FI + j1],
                    start=(c == 0), stop=(c == 7)), waits=gw)
            for c in range(8):
                ut = k.op("tensor", lambda t, c=c, j0=j0, j1=j1: t.matmul(
                    ps_l[:, 512:512 + j1 - j0], xqT[:, c * 128:(c + 1) * 128],
                    wu[:, c * FI + j0:c * FI + j1],
                    start=(c == 0), stop=(c == 7)), waits=gw)
            prev["xqT_used"] = [gt, ut]
            st = k.op("scalar", lambda s, jw=jw: s.activation(
                hsil[:, 0:jw], ps_l[:, 0:jw], AF.Silu),
                waits=[gt] + prev["hmul_used"])
            ht = k.op("vector", lambda v, jw=jw: v.tensor_tensor(
                hmul[:, 0:jw], hsil[:, 0:jw], ps_l[:, 512:512 + jw], ALU.mult),
                waits=[st, ut])
            prev["gate_used"] = [ht]
            prev["hmul_used"] = [ht]
            qh = emit_quant(k, e, hmul[:, 0:jw], hq[:, j0:j1], jw // 16, qs, [ht])
            prev["hmul_used"] = prev["hmul_used"] + qh
            htoks[ji] = qh
        otoks = []
        ow_war = prev["stt"]
        dtoks = []
        for gi, (g0, g1) in enumerate(((0, 8), (8, 16), (16, 22))):
            need = [t for ji, (j0, j1) in enumerate(JCH)
                    if j0 < g1 * 128 and j1 > g0 * 128 for t in htoks[ji]]
            tt = emit_transpose(
                k, lambda cc, g0=g0: hq[:, (g0 + cc) * 128:(g0 + cc + 1) * 128],
                g1 - g0, ps_tp, hqT[:, 0:(g1 - g0) * 128], ident,
                need + prev["hq_used"], ttok + dtoks)
            prev["hq_used"] = []
            for cc in range(g0, g1):
                for (pso, n0) in ((e["ps_o"], 0), (e["ps_x"], 512)):
                    dt_ = k.op("tensor", lambda t, cc=cc, g0=g0, n0=n0, pso=pso:
                               t.matmul(pso[:],
                                        hqT[:, (cc - g0) * 128:(cc - g0 + 1) * 128],
                                        wd[:, cc * D + n0:cc * D + n0 + 512],
                                        start=(cc == 0), stop=(cc == 21)),
                               waits=tt + tw + ow_war)
                    if cc == g1 - 1:
                        otoks.append(dt_)
                    dtoks.append(dt_)
            prev["hq_used"] = [dtoks[-1]]
        rts = []
        for (pso, n0, dt_) in ((e["ps_o"], 0, otoks[-2]), (e["ps_x"], 512, otoks[-1])):
            rtok = k.op("vector", lambda v, x_tile=x_tile, n0=n0, pso=pso:
                        v.scalar_tensor_tensor(
                            x_tile[:, n0:n0 + 512], pso[:], 0.5,
                            x_tile[:, n0:n0 + 512], ALU.mult, ALU.add),
                        waits=[dt_] + x_toks[ti] + lntok)
            rts.append(rtok)
        prev["stt"] = rts
        new_x_toks.append(rts)
        last_user += rts
    e["fb_w_war"] = tmax(*last_user)
    return new_x_toks


def emit_attnB(k, e, seq, x_toks):
    x_sb = e["x_sb"]
    lnx, lnxT = e["lnx_bf"], e["lnxT"]
    lnsc = e["lnsc"]
    ident = e["ident"]
    ones_row = e["ones_row"]
    ps_tp, ps_l, ps_o, ps_x = e["ps_tp"], e["ps_l"], e["ps_o"], e["ps_x"]
    Qm, Km = e["ab_qm"], e["ab_km"]
    vaug = e["ab_vaug"]
    pt0, pt1 = e["ab_pt0"], e["ab_pt1"]
    recs = e["ab_rec"]
    bc = e["ab_bc"]
    aT = e["ab_aT"]
    onesb = e["ab_ones"]
    onesf = e["ab_onesf"]
    wq, wk, wv, wo = e["ab_wq"], e["ab_wk"], e["ab_wv"], e["ab_wo"]
    brows = e["ab_brows"]

    war_w = e.get("ab_w_war", []) + e.get("arena_war", [])
    tw = []
    for wsb, nm in ((wq, "wq_b"), (wk, "wk_b"), (wv, "wv_b"), (wo, "wo_b")):
        tw.append(k.dma("gpsimd", wsb.rearrange("p (c n) -> p c n", n=D),
                        e["din"][nm].rearrange("(c p) n -> p c n", p=128),
                        waits=war_w))
    for i, nm in enumerate(("bq_b", "bk_b", "bv_b", "bo_b")):
        tw.append(k.dma("gpsimd", brows[:, i * D:(i + 1) * D],
                        e["din"][nm].unsqueeze(0), waits=war_w))
    va3m = vaug[:].rearrange("p (c s) -> p c s", s=130)
    mz1 = k.op("vector", lambda v: v.memset(va3m[:, :, 64:65], 1.0), waits=war_w)
    mz2 = k.op("vector", lambda v: v.memset(va3m[:, :, 129:130], 1.0),
               waits=war_w + [mz1])
    tw = tw + [mz1, mz2]

    prev = {"lnxT_used": [], "qk_evac": [], "v_evac": [], "exp": [],
            "pv": [], "stt": [], "aT_used": [], "bc_used": [], "rec_used": []}
    new_x_toks = []
    last_user = list(tw)
    for ti in range(NT):
        x_tile = x_sb[:, ti * D:(ti + 1) * D]
        lntok = emit_ln(k, None, x_tile, x_toks[ti] + prev["lnxT_used"], lnx[:],
                        lnsc)
        ttok = emit_transpose(k, lambda c: lnx[:, c * 128:(c + 1) * 128], 8,
                              ps_tp, lnxT[:], ident, lntok, prev["lnxT_used"])
        prev["lnxT_used"] = [ttok[0]]
        qk_toks = []
        for (dst, wsb, boff, scl, poff) in ((Qm, wq, 0, 1.0 / 1024.0, 0),
                                            (Km, wk, 1, 1.0, 1024)):
            mm = None
            for par in (0, 1):
                for c in range(8):
                    for (n0, n1) in ((0, 512), (512, 1024)):
                        mm = k.op("tensor", lambda t, c=c, n0=n0, n1=n1,
                                  par=par, wsb=wsb, poff=poff: t.matmul(
                                      ps_l[par * 64:(par + 1) * 64,
                                           poff + n0:poff + n1],
                                      lnxT[:, c * 128 + par:(c + 1) * 128:2],
                                      wsb[:, c * D + n0:c * D + n1],
                                      start=(c == 0), stop=False,
                                      tile_position=(0, 64 * par)),
                                  waits=ttok + tw + prev["qk_evac"])
                for (n0, n1) in ((0, 512), (512, 1024)):
                    mm = k.op("tensor", lambda t, n0=n0, n1=n1, par=par,
                              boff=boff, poff=poff: t.matmul(
                                  ps_l[par * 64:(par + 1) * 64,
                                       poff + n0:poff + n1],
                                  ones_row[:, 0:64],
                                  brows[:, boff * D + n0:boff * D + n1],
                                  start=False, stop=True,
                                  tile_position=(0, 64 * par)),
                              waits=ttok + tw + prev["qk_evac"])
            ev0 = k.op("scalar", lambda s, dst=dst, scl=scl, poff=poff:
                       s.activation(dst[:, 0:1024], ps_l[0:64, poff:poff + 1024],
                                    AF.Identity, scale=float(scl)),
                       waits=[mm] + prev["pv"])
            ev1 = k.op("scalar", lambda s, dst=dst, scl=scl, poff=poff:
                       s.activation(dst[:, 1024:2048],
                                    ps_l[64:128, poff:poff + 1024],
                                    AF.Identity, scale=float(scl)),
                       waits=[mm] + prev["pv"])
            qk_toks += [ev0, ev1]
        vev = []
        for half in (0, 1):
            pso = (ps_o, ps_x)[half]
            hm = []
            for c2 in range(4):
                c = half * 4 + c2
                mm = None
                for cd in range(8):
                    mm = k.op("tensor", lambda t, c=c, c2=c2, cd=cd, pso=pso:
                              t.matmul(
                                  pso[:, c2 * 128:(c2 + 1) * 128],
                                  wv[:, cd * D + c * 128:cd * D + (c + 1) * 128],
                                  lnxT[:, cd * 128:(cd + 1) * 128],
                                  start=(cd == 0), stop=False),
                              waits=ttok + tw + prev["v_evac"])
                mm = k.op("tensor", lambda t, c=c, c2=c2, pso=pso: t.matmul(
                    pso[:, c2 * 128:(c2 + 1) * 128],
                    brows[:, 2 * D + c * 128:2 * D + (c + 1) * 128],
                    ones_row[:],
                    start=False, stop=True),
                    waits=ttok + tw + prev["v_evac"])
                hm.append(mm)
            va4 = vaug[:].rearrange("p (c q s) -> p c q s", q=2, s=65)
            psv = pso[:].rearrange("p (c q s) -> p c s q", q=64, s=2)
            ev = k.op("vector", lambda v, half=half, psv=psv, va4=va4:
                      v.tensor_copy(
                          va4[:, half * 4:(half + 1) * 4, :, 0:64], psv),
                      waits=hm)
            vev.append(ev)
        vmm = vev
        pv_toks = []
        exp_toks = []
        for kt in range(16):
            par, kc = (0, kt) if kt < 8 else (1, kt - 8)
            vslice = vaug[:, kc * 130 + par * 65:kc * 130 + par * 65 + 65]
            for qc in range(4):
                pbuf = (pt0, pt1)[(kt * 4 + qc) % 2]
                pso = (ps_o, ps_x)[(kt * 4 + qc) % 2][:]
                smm = k.op("tensor", lambda t, kt=kt, qc=qc, pso=pso: t.matmul(
                    pso, Km[:, kt * 128:(kt + 1) * 128],
                    Qm[:, qc * 512:(qc + 1) * 512],
                    start=True, stop=True),
                    waits=qk_toks + prev["exp"] + vmm)
                ex = k.op("scalar", lambda s, pbuf=pbuf, pso=pso: s.activation(
                    pbuf[:], pso, AF.Exp), waits=[smm] + prev["pv"])
                exp_toks.append(ex)
                prev["exp"] = exp_toks[-2:]
                pv = k.op("tensor", lambda t, kt=kt, qc=qc, vslice=vslice,
                          pbuf=pbuf: t.matmul(
                              ps_l[0:65, qc * 512:(qc + 1) * 512],
                              vslice, pbuf[:],
                              start=(kt == 0), stop=(kt == 15)),
                          waits=[ex] + prev["aT_used"] + qk_toks)
                pv_toks.append(pv)
                prev["pv"] = pv_toks[-2:]
        rc = k.op("vector", lambda v: v.reciprocal(recs[64:65, :],
                                                   ps_l[64:65, 0:2048]),
                  waits=pv_toks[-4:] + prev["rec_used"])
        bcm = []
        for qc in range(4):
            psb = (ps_o, ps_x)[qc % 2]
            bm_ = k.op("tensor", lambda t, qc=qc, psb=psb: t.matmul(
                psb[0:64, 0:512],
                onesf[64:65, 0:64],
                recs[64:65, qc * 512:(qc + 1) * 512],
                start=True, stop=True),
                waits=[rc] + prev["bc_used"] + prev["exp"] + bcm[-2:])
            bv_ = k.op("vector", lambda v, qc=qc, psb=psb: v.tensor_copy(
                bc[:, qc * 512:(qc + 1) * 512], psb[0:64, 0:512]),
                waits=[bm_])
            bcm.append(bv_)
        prev["rec_used"] = bcm[-1:]
        prev["bc_used"] = []
        aT_toks = []
        for c in range(8):
            for half in (0, 1):
                at = k.op("vector", lambda v, c=c, half=half: v.tensor_tensor(
                    aT[half * 64:half * 64 + 64, c * 128:(c + 1) * 128],
                    ps_l[0:64, 2 * c + half:2048:16],
                    bc[:, 2 * c + half:2048:16], ALU.mult),
                    waits=bcm + pv_toks[-4:]),
                aT_toks.append(at[0])
        prev["bc_used"] = aT_toks
        omm = []
        rts = []
        ow = aT_toks + tw + prev["stt"]
        for (pso, n0) in ((ps_o, 0), (ps_x, 512)):
            mm = None
            for c in range(8):
                mm = k.op("tensor", lambda t, c=c, n0=n0, pso=pso: t.matmul(
                    pso[:], aT[:, c * 128:(c + 1) * 128],
                    wo[:, c * D + n0:c * D + n0 + 512],
                    start=(c == 0), stop=False), waits=ow)
            mm = k.op("tensor", lambda t, n0=n0, pso=pso: t.matmul(
                pso[:], ones_row[:],
                brows[:, 3 * D + n0:3 * D + n0 + 512], start=False, stop=True),
                waits=ow)
            omm.append(mm)
            rtok = k.op("vector", lambda v, x_tile=x_tile, n0=n0, pso=pso:
                        v.tensor_tensor(x_tile[:, n0:n0 + 512], pso[:],
                                        x_tile[:, n0:n0 + 512], ALU.add),
                        waits=[mm] + x_toks[ti] + lntok)
            rts.append(rtok)
        prev["aT_used"] = omm
        prev["stt"] = rts
        prev["qk_evac"] = rts
        prev["v_evac"] = rts
        new_x_toks.append(rts)
        last_user += rts
    e["ab_w_war"] = tmax(*last_user)
    return new_x_toks


def emit_attnA(k, e, seq, x_toks):
    import os
    AA_NPAIR = int(os.environ.get("AA_NPAIR", str(NPAIR)))
    AA_F1 = int(os.environ.get("AA_F1", "1"))
    AA_F2 = int(os.environ.get("AA_F2", "1"))
    x_sb = e["x_sb"]
    lnx = e["lnx_bf"]
    lnsc = e["lnsc"]
    ident, ones_row, tri = e["ident"], e["ones_row"], e["tri"]
    ps_tp, ps_l, ps_o, ps_x = e["ps_tp"], e["ps_l"], e["ps_o"], e["ps_x"]
    lnxT = e["aa_lnxT"]
    yT = e["aa_yT"]
    wf = e["aa_wf"]
    wout = e["aa_wout"]
    cosf, sinf = e["aa_cos"], e["aa_sin"]
    tok4 = e["aa_tok4"]
    pairm = e["aa_pairm"]
    vtok = e["aa_vtok"]
    expscr = e["aa_expscr"]
    ptile = e["aa_pt"]
    sel = e["aa_sel"]
    recT = e["aa_recT"]
    rcb = e["aa_rcb"]
    reccol = e["aa_reccol"]
    denp = e["aa_denp"]
    bcsb = e["aa_bc"]
    bout = e["aa_bout"]
    ropesc = e["aa_ropesc"]

    war_w = e.get("aa_w_war", []) + e.get("arena_war", [])
    tw = []
    for wsb, nm, nn in ((wf, "wf_a", 3 * D), (wout, "wo_a", D)):
        tw.append(k.dma("gpsimd", wsb.rearrange("p (c n) -> p c n", n=nn),
                        e["din"][nm].rearrange("(c p) n -> p c n", p=128),
                        waits=war_w))
    tw.append(k.dma("gpsimd", bout[:], e["din"]["bo_a"].unsqueeze(0), waits=war_w))
    tw.append(k.dma("sync", cosf, e["din"]["c_rope_cos"], waits=war_w))
    tw.append(k.dma("sync", sinf, e["din"]["c_rope_sin"], waits=war_w))
    tw.append(k.dma("sync", sel[:], e["din"]["c_sel"], waits=war_w))

    lnxT_toks = []
    prev_tp = []
    ln_last = []
    tp_war = []
    for ti in range(NT):
        x_tile = x_sb[:, ti * D:(ti + 1) * D]
        lntok = emit_ln(k, None, x_tile, x_toks[ti] + war_w + tp_war, lnx[:],
                        lnsc)
        ln_last.append(lntok[0])
        dst = lnxT[:].rearrange("p (c n) -> p c n", n=SEQ)[
            :, :, ti * 128:(ti + 1) * 128]
        tt = emit_transpose(k, lambda c: lnx[:, c * 128:(c + 1) * 128], 8,
                            ps_tp, dst, ident, lntok, prev_tp + tp_war)
        prev_tp = tt
        tp_war = [k.barrier_token("tensor")]
        lnxT_toks += tt

    prev = {"pairm_used": [], "vtok_used": [], "unrot_exp": [], "rot_exp": [],
            "pv": [], "bc_used": [], "yT_used": [], "tok_used": [],
            "denp_used": [], "reccol_used": [], "ropescr_war": []}
    yT_toks = []
    qrT = pairm[:, 0 * SEQ:1 * SEQ]
    qT = pairm[:, 1 * SEQ:2 * SEQ]
    krT = pairm[:, 2 * SEQ:3 * SEQ]
    kT = pairm[:, 3 * SEQ:4 * SEQ]
    for hp in range(AA_NPAIR):
        ptoks = []
        for ti in range(NT):
            pw = lnxT_toks + tw + prev["tok_used"] + prev["unrot_exp"] + \
                prev["rot_exp"] + prev["bc_used"] + prev.get("bcast_used", [])
            m = None
            for c in range(8):
                m = k.op("tensor", lambda t, c=c, hp=hp, ti=ti: t.matmul(
                    ps_l[:, 0:384],
                    lnxT[:, c * SEQ + ti * 128:c * SEQ + (ti + 1) * 128],
                    wf[:, c * 3 * D + hp * 384:c * 3 * D + (hp + 1) * 384],
                    start=(c == 0), stop=(c == 7)),
                    waits=pw + prev["vtok_used"])
            mm = {0: m, 1: m, 2: m}
            evq = k.op("scalar", lambda s: s.copy(tok4[:, 0:128], ps_l[:, 0:128]),
                       waits=[m] + prev["tok_used"])
            evk = k.op("scalar", lambda s: s.copy(tok4[:, 256:384],
                                                  ps_l[:, 128:256]),
                       waits=[m] + prev["tok_used"])
            evv = k.op("scalar", lambda s, ti=ti: s.copy(
                vtok[:, ti * 128:(ti + 1) * 128], ps_l[:, 256:384]),
                waits=[m] + prev["vtok_used"])
            rtoks = []
            AA_ROPE = int(os.environ.get("AA_ROPE", "1"))
            for (src0, out0, src_ev) in (((0, 128, evq), (256, 384, evk))
                                         if AA_ROPE else ()):
                pv3 = tok4[:, src0:src0 + 128].rearrange("p (h i) -> p h i", i=64)
                te, to = pv3[:, :, 0:32], pv3[:, :, 32:64]
                cosb = cosf[:, ti * 32:(ti + 1) * 32].unsqueeze(1).broadcast_to(
                    (128, 2, 32))
                sinb = sinf[:, ti * 32:(ti + 1) * 32].unsqueeze(1).broadcast_to(
                    (128, 2, 32))
                sc3 = ropesc[:].rearrange("p (h i) -> p h i", i=64)
                sc_a, sc_b = sc3[:, :, 0:32], sc3[:, :, 32:64]
                o3 = tok4[:, out0:out0 + 128].rearrange("p (h i) -> p h i", i=64)
                ro_re, ro_ro = o3[:, :, 0:32], o3[:, :, 32:64]
                w0 = [src_ev] + prev["tok_used"] + prev["ropescr_war"] + rtoks
                RN = int(os.environ.get("AA_RN", "6"))
                r1 = k.op("vector", lambda v, te=te, cosb=cosb, sc_a=sc_a:
                          v.tensor_tensor(sc_a, te, cosb, ALU.mult), waits=w0)
                last = r1
                if RN >= 2:
                    r2 = k.op("vector", lambda v, to=to, sinb=sinb, sc_b=sc_b:
                              v.tensor_tensor(sc_b, to, sinb, ALU.mult), waits=w0)
                    last = r2
                if RN >= 3:
                    r3 = k.op("vector", lambda v, sc_a=sc_a, sc_b=sc_b, ro_re=ro_re:
                              v.tensor_tensor(ro_re, sc_a, sc_b, ALU.subtract),
                              waits=[r1, r2])
                    last = r3
                if RN >= 4:
                    r4 = k.op("vector", lambda v, te=te, sinb=sinb, sc_a=sc_a:
                              v.tensor_tensor(sc_a, te, sinb, ALU.mult), waits=[r3])
                    last = r4
                if RN >= 5:
                    r5 = k.op("vector", lambda v, to=to, cosb=cosb, sc_b=sc_b:
                              v.tensor_tensor(sc_b, to, cosb, ALU.mult), waits=[r3])
                    last = r5
                if RN >= 6:
                    r6 = k.op("vector", lambda v, sc_a=sc_a, sc_b=sc_b, ro_ro=ro_ro:
                              v.tensor_tensor(ro_ro, sc_a, sc_b, ALU.add),
                              waits=[r4, r5])
                    last = r6
                rtoks += ([r3, r6] if RN >= 6 else [last])
            prev["ropescr_war"] = rtoks[-2:]
            ttw = [evq, evk, evv] + rtoks + prev["pairm_used"]
            tts = []
            for (src0, m) in ((128, 0), (0, 1), (384, 2), (256, 3)):
                tt = k.op("tensor", lambda t, src0=src0, m=m: t.transpose(
                    ps_tp[:, m * 128:(m + 1) * 128],
                    tok4[:, src0:src0 + 128], ident[:]), waits=ttw)
                tts.append(tt)
            pm = pairm[:].rearrange("p (m n) -> p m n", n=SEQ)
            ev = k.op("vector", lambda v, ti=ti, pm=pm: v.tensor_copy(
                pm[:, :, ti * 128:(ti + 1) * 128],
                ps_tp[:, 0:512].rearrange("p (m n) -> p m n", n=128)),
                waits=tts)
            ptoks.append(ev)
            ptoks.append(evv)
            prev["tok_used"] = [tts[-1], ev]
            prev["vtok_used"] = [evv]
        rc_toks = []
        for qt in range(NT if AA_F1 else 0):
            etoks = []
            for h in (0, 1):
                for kh in (0, 1):
                    m2 = []
                    for nn in (0, 512):
                        m2.append(k.op(
                            "tensor", lambda t, h=h, kh=kh, nn=nn, qt=qt:
                            t.matmul(
                                ps_l[:, h * 1024 + nn:h * 1024 + nn + 512],
                                qT[h * 64:(h + 1) * 64, qt * 128:(qt + 1) * 128],
                                kT[h * 64:(h + 1) * 64,
                                   kh * 1024 + nn:kh * 1024 + nn + 512],
                                start=True, stop=True),
                            waits=ptoks + etoks + prev["unrot_exp"]))
                    ex = k.op("scalar", lambda s, h=h, kh=kh: s.activation(
                        expscr[:], ps_l[:, h * 1024:h * 1024 + 1024],
                        AF.Exp, scale=0.125,
                        accum_out=denp[:, h * 2 + kh:h * 2 + kh + 1]),
                        waits=m2 + prev["denp_used"])
                    etoks.append(ex)
            prev["unrot_exp"] = etoks[-2:]
            d0 = k.op("vector", lambda v, qt=qt: v.tensor_tensor(
                reccol[:, qt:qt + 1], denp[:, 0:1], denp[:, 1:2], ALU.add),
                waits=etoks + prev["reccol_used"])
            d1 = k.op("vector", lambda v, qt=qt: v.tensor_tensor(
                reccol[:, 16 + qt:16 + qt + 1], denp[:, 2:3], denp[:, 3:4],
                ALU.add), waits=etoks)
            prev["denp_used"] = [d0, d1]
            r0 = k.op("vector", lambda v, qt=qt: v.reciprocal(
                reccol[:, qt:qt + 1], reccol[:, qt:qt + 1]), waits=[d0])
            r1 = k.op("vector", lambda v, qt=qt: v.reciprocal(
                reccol[:, 16 + qt:16 + qt + 1], reccol[:, 16 + qt:16 + qt + 1]),
                waits=[d1])
            rc_toks += [r0, r1]
        cb = k.op("vector", lambda v: v.tensor_copy(rcb[:], reccol[:, 0:32]),
                  waits=rc_toks + prev["reccol_used"])
        tt = k.op("tensor", lambda t: t.transpose(
            ps_tp[0:32, 0:128], rcb[:], ident[:]),
            waits=[cb] + prev["tok_used"])
        rtT = k.op("vector", lambda v: v.tensor_copy(recT[:], ps_tp[0:32, 0:128]),
                   waits=[tt])
        prev["reccol_used"] = [rtT]
        prev["tok_used"] = prev["tok_used"] + [rtT]
        yevs = []
        for qc in range(4 if AA_F2 else 0):
            pv_last = []
            nkt = 4 * qc + 4
            for kt in range(nkt):
                pp = kt % 2
                smm = {}
                for h in (0, 1):
                    smm[h] = k.op("tensor", lambda t, h=h, kt=kt, qc=qc, pp=pp:
                                  t.matmul(
                                      ps_l[:, pp * 1024 + h * 512:
                                           pp * 1024 + h * 512 + 512],
                                      krT[h * 64:(h + 1) * 64,
                                          kt * 128:(kt + 1) * 128],
                                      qrT[h * 64:(h + 1) * 64,
                                          qc * 512:(qc + 1) * 512],
                                      start=True, stop=True),
                                  waits=ptoks + prev["rot_exp"] +
                                        prev["unrot_exp"] +
                                        prev.get("bcast_used", []))
                j = kt - 4 * qc
                ex = k.op("scalar", lambda s, pp=pp: s.activation(
                    ptile[:], ps_l[:, pp * 1024:pp * 1024 + 1024],
                    AF.Exp, scale=0.125),
                    waits=[smm[0], smm[1]] + prev["pv"])
                extoks = [ex]
                prev["rot_exp"] = extoks
                masks = []
                if j >= 0:
                    for h in (0, 1):
                        mk = k.op("vector", lambda v, h=h, j=j: v.tensor_tensor(
                            ptile[:, h * 512 + j * 128:h * 512 + (j + 1) * 128],
                            ptile[:, h * 512 + j * 128:h * 512 + (j + 1) * 128],
                            tri[:], ALU.mult), waits=extoks)
                        masks.append(mk)
                        if j > 0:
                            mz = k.op("vector", lambda v, h=h, j=j: v.memset(
                                ptile[:, h * 512:h * 512 + j * 128], 0.0),
                                waits=extoks)
                            masks.append(mz)
                pvw = extoks + masks + ptoks
                pv2 = []
                for h in (0, 1):
                    pso = (ps_o, ps_x)[h]
                    pv = k.op("tensor", lambda t, h=h, kt=kt, qc=qc,
                              pso=pso, nkt=nkt: t.matmul(
                                  pso[h * 64:(h + 1) * 64, 0:512],
                                  vtok[:, kt * 128 + h * 64:
                                       kt * 128 + h * 64 + 64],
                                  ptile[:, h * 512:(h + 1) * 512],
                                  start=(kt == 0), stop=(kt == nkt - 1),
                                  tile_position=(0, 64 * h)),
                              waits=pvw + prev["yT_used"])
                    pv2.append(pv)
                prev["pv"] = pv2
                pv_last = pv2
            bmm = []
            for j in range(4):
                bm_ = k.op("tensor", lambda t, j=j, qc=qc: t.matmul(
                    ps_l[:, j * 128:(j + 1) * 128],
                    sel[:, (qc * 4 + j) * 128:(qc * 4 + j + 1) * 128],
                    recT[:], start=True, stop=True),
                    waits=[rtT] + prev["bc_used"] + prev["rot_exp"])
                bmm.append(bm_)
            bev = k.op("vector", lambda v: v.tensor_copy(bcsb[:], ps_l[:, 0:512]),
                       waits=bmm)
            prev["bcast_used"] = [bev]
            yevs = []
            for h in (0, 1):
                pso = (ps_o, ps_x)[h]
                yev = k.op("vector", lambda v, hp=hp, qc=qc, h=h, pso=pso:
                           v.tensor_tensor(
                               yT[h * 64:(h + 1) * 64,
                                  hp * SEQ + qc * 512:hp * SEQ + (qc + 1) * 512],
                               pso[h * 64:(h + 1) * 64, 0:512],
                               bcsb[h * 64:(h + 1) * 64, :], ALU.mult),
                           waits=[bev] + pv_last + prev["yT_used"])
                yevs.append(yev)
            prev["bc_used"] = yevs
            prev["yT_used"] = yevs
            yT_toks += yevs
        prev["pairm_used"] = yevs if yevs else prev["tok_used"]
        prev["vtok_used"] = prev["pv"] if prev["pv"] else prev["tok_used"]
    new_x_toks = []
    stt_prev = []
    last_user = []
    for ti in range(NT):
        x_tile = x_sb[:, ti * D:(ti + 1) * D]
        rs = []
        for (pso, n0) in ((ps_o, 0), (ps_x, 512)):
            mm = None
            for c in range(8):
                mm = k.op("tensor", lambda t, c=c, ti=ti, n0=n0, pso=pso:
                          t.matmul(
                              pso[:],
                              yT[:, c * SEQ + ti * 128:c * SEQ + (ti + 1) * 128],
                              wout[:, c * D + n0:c * D + n0 + 512],
                              start=(c == 0), stop=False),
                          waits=yT_toks + tw + stt_prev + prev["pv"] +
                                prev["bc_used"])
            mm = k.op("tensor", lambda t, ti=ti, n0=n0, pso=pso: t.matmul(
                pso[:], ones_row[:], bout[:, n0:n0 + 512],
                start=False, stop=True),
                waits=yT_toks + tw + stt_prev)
            rtok = k.op("vector", lambda v, x_tile=x_tile, n0=n0, pso=pso:
                        v.tensor_tensor(x_tile[:, n0:n0 + 512], pso[:],
                                        x_tile[:, n0:n0 + 512], ALU.add),
                        waits=[mm] + x_toks[ti] + ln_last)
            rs.append(rtok)
        stt_prev = rs
        new_x_toks.append(rs)
        last_user += rs
    e["aa_w_war"] = tmax(*last_user)
    return new_x_toks


# version 09 (final)
# speedup vs baseline: 1.2168x; 1.0005x over previous
import math
import numpy as np
import ml_dtypes
import concourse.bass as bass
import concourse.mybir as mybir
from concourse.bass_utils import run_bass_kernel_spmd
from concourse.alu_op_type import AluOpType as ALU
from concourse import dve_ops
from concourse.dve_op import DveOp
from concourse.dve_spec import Spec, Src0, Src1, C0, C1, Zero, maxx, minn, select

F32 = mybir.dt.float32
BF16 = mybir.dt.bfloat16
I32 = mybir.dt.int32
AF = mybir.ActivationFunctionType
AX = mybir.AxisListType.X
BF = ml_dtypes.bfloat16

D = 1024
SEQ = 2048
NSEQ = 4
NT = SEQ // 128
NH = 16
DH = 64
NPAIR = NH // 2

FF_PAR = 8
FF_NODES = 255
FF_W = FF_PAR * FF_NODES
FF_WP = 2048
FF_DEPTH = 7

FI = 2816
FI_NC = FI // 128

def _register_qops():
    if any(o.name == "QMX_SEL" for o in dve_ops.OPS):
        return
    qmx_sel = DveOp(
        "QMX_SEL",
        Spec(body=select(maxx(Src0, Zero - Src0) < C0, Src0, Src1)),
        subdim=False,
        uops_sha={},
        reference=lambda in0, in1, s0, s1, imm2: np.where(
            np.abs(in0) < s0, in0, in1).astype(np.float32),
    )
    from concourse.dve_spec import lower as _dve_lower
    for op in (qmx_sel,):
        for ver in ("v3", "v4"):
            try:
                r = _dve_lower(op.spec, ver=ver, perf_en=False, name=op.name)
                op.uops_sha[ver] = r.sha(ver)
            except Exception:
                pass
        dve_ops.OPS.append(op)
        dve_ops._SUB_OPCODE_FOR_NAME[op.name] = (
            dve_ops._CUSTOM_DVE_ROW_BASE + len(dve_ops.OPS) - 1
        )
        dve_ops.CUSTOM_DVE_REFS[op.name] = op.ref
    assert max(dve_ops._SUB_OPCODE_FOR_NAME.values()) < 0x20


ENGS = ("tensor", "vector", "scalar", "gpsimd", "sync")


class Sem:
    def __init__(self, k, name):
        self.h = k.ctx.enter_context(k.nc.semaphore(name))
        self.name = name
        self.n = 0


class K:
    def __init__(self, nc, ctx):
        self.nc = nc
        self.ctx = ctx
        self.plan = []
        self.s = {e: Sem(self, "s_" + e) for e in ENGS}
        NP = 8
        self.sdma_pool = {q: [Sem(self, f"d_{q}{i}") for i in range(NP)]
                          for q in ("sync", "gpsimd")}
        self.sdma_idx = {q: 0 for q in ("sync", "gpsimd")}
        self.last_wait = {e: {} for e in ENGS}
        self.n_emitted = {e: 0 for e in ENGS}
        self.phase = "pre"
        self.suppress = False
        self.P = {}
        self.Kper = {}
        self.allsems = list(self.s.values()) + [x for q in self.sdma_pool
                                                for x in self.sdma_pool[q]]
        self.body_waits = {e: set() for e in ENGS}
        self.semmap = {sem.name: sem for sem in self.allsems}
        self.cur_i = None
        self.cur_regs = None
        self.n_iter = NSEQ

    def begin_loop(self):
        self.P = {sem.name: sem.n for sem in self.allsems}
        self._lw_snap = {e: dict(self.last_wait[e]) for e in ENGS}
        self._idx_snap = dict(self.sdma_idx)
        self._semn_snap = {sem.name: sem.n for sem in self.allsems}
        self.suppress = True
        self.phase = "body"

    def mid_loop(self):
        self.Kper = {sem.name: sem.n - self.P[sem.name] for sem in self.allsems}
        self.last_wait = {e: dict(self._lw_snap[e]) for e in ENGS}
        self.sdma_idx = dict(self._idx_snap)
        self.suppress = False

    def end_loop(self):
        self.phase = "post"

    def _classify(self, sem, c):
        if self.phase == "pre" or not self.P:
            return ("imm", c)
        P = self.P[sem.name]
        if c <= P:
            return ("imm", c)
        K_ = self.Kper.get(sem.name)
        if K_ is None:
            return ("rel", c - P)
        if self.phase == "post":
            return ("imm", c + (self.n_iter - 2) * K_)
        return ("rel", c - P - K_)

    def sb(self, name, shape, dt=F32):
        return self.ctx.enter_context(self.nc.sbuf_tensor(name, shape, dt))

    def ps(self, name, shape, dt=F32):
        return self.ctx.enter_context(self.nc.psum_tensor(name, shape, dt))

    def _coalesce(self, eng, waits):
        byname = {}
        for (sem, n) in waits:
            if n <= 0:
                continue
            if sem.name not in byname or byname[sem.name][1] < n:
                byname[sem.name] = (sem, n)
        wl = []
        for (sem, n) in byname.values():
            kind, v = self._classify(sem, n)
            key = (sem.name, kind)
            prev = self.last_wait[eng].get(key, None)
            if prev is None or v > prev:
                wl.append((sem, kind, v))
                self.last_wait[eng][key] = v
                if kind == "rel":
                    self.body_waits[eng].add(sem.name)
        return wl

    def _emit_waits(self, e, eng, wl):
        for (sem, kind, v) in wl:
            if kind == "imm":
                e.wait_ge(sem.h, v)
            else:
                regs = self.cur_regs
                base, scr = regs["base"][sem.name], regs["scr"]
                if v >= 0:
                    e.reg_alu(scr, base, v, ALU.add)
                else:
                    e.reg_alu(scr, base, -v, ALU.subtract)
                    if self.P[sem.name] + v < 0:
                        e.reg_alu(scr, scr, 0, ALU.max)
                e.wait_ge(sem.h, scr)

    def op(self, eng, f, waits=(), inc=True):
        wl = self._coalesce(eng, waits)
        psem = self.s[eng]
        tok = None
        if inc:
            psem.n += 1
            tok = (psem, psem.n)

            def fn(e, wl=wl, f=f, h=psem.h, eng=eng):
                self._emit_waits(e, eng, wl)
                f(e).then_inc(h, 1)
        else:
            def fn(e, wl=wl, f=f, eng=eng):
                self._emit_waits(e, eng, wl)
                f(e)
        if not self.suppress:
            self.plan.append((eng, self.phase, fn))
        self.n_emitted[eng] += 1
        return tok

    def dma(self, queue, out, in_, waits=()):
        pool = self.sdma_pool[queue]
        dsem = pool[self.sdma_idx[queue] % len(pool)]
        self.sdma_idx[queue] += 1
        wl = self._coalesce(queue, list(waits) + [(dsem, dsem.n)])
        dsem.n += 16
        tok = (dsem, dsem.n)

        def fn(e, wl=wl, out=out, in_=in_, h=dsem.h, eng=queue):
            self._emit_waits(e, eng, wl)
            o = out(self) if callable(out) else out
            i = in_(self) if callable(in_) else in_
            e.dma_start(out=o, in_=i).then_inc(h, 16)

        if not self.suppress:
            self.plan.append((queue, self.phase, fn))
        self.n_emitted[queue] += 1
        return tok

    def barrier_token(self, eng):
        return (self.s[eng], self.s[eng].n)

    def replay(self):
        looped = bool(self.Kper)
        with self.nc.Block() as block:
            for engname in ENGS:
                pre = [fn for e, ph, fn in self.plan if e == engname and ph == "pre"]
                body = [fn for e, ph, fn in self.plan if e == engname and ph == "body"]
                post = [fn for e, ph, fn in self.plan if e == engname and ph == "post"]
                if not (pre or body or post):
                    continue
                dec = getattr(block, engname)

                def section(eng, pre=pre, body=body, post=post, engname=engname):
                    for fn in pre:
                        fn(eng)
                    if body:
                        if looped:
                            regs = {"scr": eng.alloc_register(f"scr_{engname}"),
                                    "base": {}}
                            for sname in sorted(self.body_waits[engname]):
                                regs["base"][sname] = eng.alloc_register(
                                    f"b_{engname}_{sname}")
                            with eng.Fori(0, self.n_iter) as i:
                                self.cur_i = i
                                self.cur_regs = regs
                                for sname, r in regs["base"].items():
                                    eng.reg_alu(r, i, self.Kper[sname], ALU.mult)
                                    eng.reg_alu(r, r, self.P[sname], ALU.add)
                                for fn in body:
                                    fn(eng)
                        else:
                            for fn in body:
                                fn(eng)
                    for fn in post:
                        fn(eng)

                dec(section)


def tmax(*toks):
    best = {}
    for t in toks:
        if t is None:
            continue
        sem, n = t
        if sem.name not in best or best[sem.name][1] < n:
            best[sem.name] = (sem, n)
    return list(best.values())


def host_consts():
    c = {}
    i = np.arange(DH // 2, dtype=np.float32)
    theta = np.power(np.float32(10000.0), -2.0 * (i - 1.0) / DH).astype(np.float32)
    ang = np.arange(SEQ, dtype=np.float32)[:, None] * theta[None, :]
    cos = np.cos(ang).astype(np.float32)
    sin = np.sin(ang).astype(np.float32)
    c["rope_cos"] = cos.reshape(NT, 128, 32).transpose(1, 0, 2).reshape(128, NT * 32).copy()
    c["rope_sin"] = sin.reshape(NT, 128, 32).transpose(1, 0, 2).reshape(128, NT * 32).copy()
    tri = np.tril(np.ones((128, 128), dtype=np.float32)).T
    c["tri_mask"] = tri.astype(BF)
    sel = np.zeros((4, 4, 32, 128), dtype=np.float32)
    for qc in range(4):
        for j in range(4):
            for d in range(128):
                sel[qc, j, (d // 64) * 16 + 4 * qc + j, d] = 1.0
    c["selmat"] = np.ascontiguousarray(sel.reshape(16, 32, 128).transpose(1, 0, 2).reshape(32, 16 * 128)).astype(BF)
    c["ones_col"] = np.ones((128, 1), dtype=BF)
    c["ones_row"] = np.ones((1, 128), dtype=BF)
    c["ident"] = np.eye(128, dtype=BF)
    return c


def emit_ln(k, bufs, x_ap, x_tok, out_ap, sc):
    junk, ssq, s1, mean, var, rstd, nmr = (sc["junk"], sc["ssq"], sc["s1"],
                                           sc["mean"], sc["var"], sc["rstd"],
                                           sc["nmr"])
    war = sc.get("war", [])
    t1 = k.op("scalar", lambda s: s.activation(junk[:], x_ap, AF.Square,
                                               accum_out=ssq[:]),
              waits=list(x_tok) + war)
    t2 = k.op("vector", lambda v: v.tensor_reduce(s1[:], x_ap, AX, ALU.add),
              waits=list(x_tok) + war)
    t3 = k.op("vector", lambda v: v.tensor_scalar(mean[:], s1[:], 1.0 / D, None,
                                                  ALU.mult), waits=[t2])
    t4 = k.op("vector", lambda v: v.tensor_tensor(var[:], mean[:], mean[:],
                                                  ALU.mult), waits=[t3])
    t5 = k.op("vector", lambda v: v.scalar_tensor_tensor(var[:], ssq[:], 1.0 / D,
                                                         var[:], ALU.mult,
                                                         ALU.subtract),
              waits=[t1, t4])
    t5b = k.op("vector", lambda v: v.tensor_scalar(var[:], var[:], 1e-5, None,
                                                   ALU.add), waits=[t5])
    t6 = k.op("scalar", lambda s: s.activation(rstd[:], var[:], AF.Sqrt),
              waits=[t5b])
    t7 = k.op("vector", lambda v: v.reciprocal(rstd[:], rstd[:]), waits=[t6])
    t8 = k.op("vector", lambda v: v.scalar_tensor_tensor(nmr[:], mean[:], -1.0,
                                                         rstd[:], ALU.mult,
                                                         ALU.mult),
              waits=[t7])
    t9 = k.op("scalar", lambda s: s.activation(out_ap, x_ap, AF.Identity,
                                               bias=nmr[:], scale=rstd[:]),
              waits=[t8] + list(x_tok))
    sc["war"] = [t9]
    return [t9]


def make_ln_scratch(k, tag):
    return {
        "junk": k.nc_junk_view,
        "ssq": k.sb(f"ln_sq_{tag}", [128, 1], F32),
        "s1": k.sb(f"ln_s1_{tag}", [128, 1], F32),
        "mean": k.sb(f"ln_m_{tag}", [128, 1], F32),
        "var": k.sb(f"ln_v_{tag}", [128, 1], F32),
        "rstd": k.sb(f"ln_rs_{tag}", [128, 1], F32),
        "nmr": k.sb(f"ln_nm_{tag}", [128, 1], F32),
    }


def emit_transpose(k, src_ap, nchunks, psum, dst_ap, ident, waits, war):
    ttoks = []
    for c in range(nchunks):
        tt = k.op("tensor",
                  lambda t, c=c: t.transpose(psum[:, c * 128:(c + 1) * 128],
                                             src_ap(c), ident[:]),
                  waits=list(waits) + list(war))
        ttoks.append(tt)
    ev = k.op("vector",
              lambda v: v.tensor_copy(dst_ap, psum[:, 0:nchunks * 128]),
              waits=ttoks)
    return [ev]


def emit_ffnA(k, env, seq, call, x_toks):
    nc = k.nc
    e = env
    wtag = f"fa{seq}_{call}"
    war_w = e["ffnA_w_war"] + e.get("arena_war", [])
    tw1 = k.dma("gpsimd", e["w_inT"][:].rearrange("p (c n) -> p c n", n=FF_W),
                e["in_w_inT"].rearrange("(c p) n -> p c n", p=128), waits=war_w)
    tw2 = k.dma("gpsimd", e["b_in"][:], e["in_b_in"].unsqueeze(0), waits=war_w)
    tw3 = k.dma("gpsimd", e["w_outT"][:].rearrange("p (c n) -> p c n", n=D),
                e["in_w_outT"].rearrange("(c p) n -> p c n", p=128), waits=war_w)
    wtok = [tw1, tw2, tw3]

    lnx, lnxT = e["lnx_bf"], e["lnxT"]
    vtree = e["vtree"]
    masked, maskedT = e["masked"], e["maskedT"]
    ps_tp, ps_l = e["ps_tp"], e["ps_l"]
    lnsc = e["lnsc"]
    x_sb = e["x_sb"]
    ones_row, b_in = e["ones_row"], e["b_in"]
    ident = e["ident"]

    nch = [(0, 512), (512, 1024), (1024, 1536), (1536, 2040)]
    st = {
        "ln": [[] for _ in range(NT)],
        "lg_last": [],
        "evA": [],
        "silu": [[], []],
        "dec": [[], []],
        "mask": [[], []],
        "treelast": [[], []],
        "psl_free": [],
        "evB": [],
        "stt": [],
        "maskprev": [],
    }
    acts2, dec2 = e["acts2"], e["dec2"]
    new_x_toks = [None] * NT
    last_user = []

    def stageA(ti):
        s = ti % 2
        x_tile = x_sb[:, ti * D:(ti + 1) * D]
        lntok = emit_ln(k, None, x_tile, x_toks[ti] + st["lg_last"],
                        lnx[:], lnsc)
        st["ln"][ti] = lntok
        ttok = emit_transpose(
            k, lambda c: lnx[:, c * 128:(c + 1) * 128], 8, ps_tp[:, 0:1024],
            lnxT[:], ident, lntok, st["evA"] + st["lg_last"])
        st["evA"] = ttok
        mm_waits = ttok + wtok + st["psl_free"]
        ltoks = []
        for (n0, n1) in nch:
            for c in range(8):
                k.op("tensor",
                     lambda t, c=c, n0=n0, n1=n1: t.matmul(
                         ps_l[:, n0:n1],
                         lnxT[:, c * 128:(c + 1) * 128],
                         e["w_inT"][:, c * FF_W + n0:c * FF_W + n1],
                         start=(c == 0), stop=False),
                     waits=mm_waits)
            tok = k.op("tensor",
                       lambda t, n0=n0, n1=n1: t.matmul(
                           ps_l[:, n0:n1], ones_row[:],
                           b_in[:, n0:n1], start=False, stop=True),
                       waits=mm_waits)
            ltoks.append(tok)
        st["lg_last"] = [ltoks[-1]]
        acts = acts2[s]
        stok = k.op("scalar", lambda sc_, acts=acts: sc_.activation(
            acts[:, 0:FF_W], ps_l[:, 0:FF_W], AF.Silu),
            waits=ltoks + st["mask"][s])
        st["silu"][s] = [stok]
        st["psl_free"] = [stok]
        acts_t = acts[:].rearrange("p (t n) -> p t n", t=FF_PAR)
        dec_t = dec2[s][:].rearrange("p (t n) -> p t n", t=FF_PAR)
        dtok = k.op("vector", lambda v, acts_t=acts_t, dec_t=dec_t:
                    v.tensor_scalar(
                        dec_t[:, :, 0:127], acts_t[:, :, 0:127], 0.0, None,
                        ALU.is_gt),
                    waits=[stok] + st["treelast"][s])
        st["dec"][s] = [dtok]

    def stageB(ti):
        s = ti % 2
        x_tile = x_sb[:, ti * D:(ti + 1) * D]
        acts = acts2[s]
        dec_t = dec2[s][:].rearrange("p (t n) -> p t n", t=FF_PAR)
        v_t = vtree[:].rearrange("p (t n) -> p t n", t=FF_PAR)
        ttk = k.op("vector", lambda v: v.memset(v_t[:, :, 0:1], 1.0),
                   waits=st["maskprev"])
        cur = [ttk] + st["dec"][s]
        tl = tr = None
        for dlev in range(FF_DEPTH):
            pd = (1 << dlev) - 1
            cd = 1 << dlev
            c0 = (1 << (dlev + 1)) - 1
            tr = k.op("vector", lambda v, pd=pd, cd=cd, c0=c0, dec_t=dec_t:
                      v.tensor_tensor(
                          v_t[:, :, c0 + 1:c0 + 2 * cd:2],
                          v_t[:, :, pd:pd + cd],
                          dec_t[:, :, pd:pd + cd], ALU.mult), waits=cur)
            tl = k.op("vector", lambda v, pd=pd, cd=cd, c0=c0: v.tensor_tensor(
                v_t[:, :, c0:c0 + 2 * cd:2], v_t[:, :, pd:pd + cd],
                v_t[:, :, c0 + 1:c0 + 2 * cd:2], ALU.subtract), waits=[tr])
            cur = [tl, tr]
        st["treelast"][s] = [tl, tr]
        mtok = k.op("vector", lambda v, acts=acts: v.tensor_tensor(
            masked[:, 0:FF_W], acts[:, 0:FF_W], vtree[:, 0:FF_W], ALU.mult),
            waits=cur + st["silu"][s] + st["evB"])
        st["mask"][s] = [mtok]
        st["maskprev"] = [mtok]
        ev1 = emit_transpose(
            k, lambda c: masked[:, c * 128:(c + 1) * 128], 8,
            ps_tp[:, 1024:2048], maskedT[:, 0:1024], ident, [mtok], st["evB"])
        ev2 = emit_transpose(
            k, lambda c: masked[:, (8 + c) * 128:(9 + c) * 128], 8,
            ps_tp[:, 1024:2048], maskedT[:, 1024:2048], ident, [mtok], ev1)
        st["evB"] = ev1 + ev2
        ow = ev1 + ev2 + wtok + st["stt"]
        rts = []
        for (pso, n0) in ((e["ps_o"], 0), (e["ps_x"], 512)):
            tok = None
            for c in range(16):
                tok = k.op("tensor", lambda t, c=c, n0=n0, pso=pso: t.matmul(
                    pso[:], maskedT[:, c * 128:(c + 1) * 128],
                    e["w_outT"][:, c * D + n0:c * D + n0 + 512],
                    start=(c == 0), stop=(c == 15)), waits=ow)
            rtok = k.op("vector", lambda v, x_tile=x_tile, n0=n0, pso=pso:
                        v.scalar_tensor_tensor(
                            x_tile[:, n0:n0 + 512], pso[:], 0.5,
                            x_tile[:, n0:n0 + 512], ALU.mult, ALU.add),
                        waits=[tok] + x_toks[ti] + st["ln"][ti])
            rts.append(rtok)
        st["stt"] = rts
        new_x_toks[ti] = rts
        last_user.extend(rts)

    stageA(0)
    for ti in range(NT):
        if ti + 1 < NT:
            stageA(ti + 1)
        stageB(ti)
    env["ffnA_w_war"] = tmax(*last_user)
    return new_x_toks


ALL_BLOCKS = ("ffnA0", "attnA", "ffnA1", "ffnB0", "attnB", "ffnB1")


def build_program(blocks, n_seq=NSEQ, fused=False):
    from contextlib import ExitStack
    _register_qops()
    nc = bass.Bass("TRN2", target_bir_lowering=False)
    din = {}
    def dram(name, shape, dt=F32, kind="ExternalInput"):
        din[name] = nc.dram_tensor(name, shape, dt, kind=kind).ap()
        return din[name]

    x_in = dram("x", [n_seq * SEQ, D])
    y_out = dram("y", [n_seq * SEQ, D], kind="ExternalOutput")
    dram("c_rope_cos", [128, NT * 32]); dram("c_rope_sin", [128, NT * 32])
    dram("c_tri", [128, 128], BF16)
    dram("c_sel", [32, 16 * 128], BF16)
    dram("c_ones_col", [128, 1], BF16); dram("c_ones_row", [1, 128], BF16)
    dram("c_ident", [128, 128], BF16)
    need = set(blocks)
    if need & {"ffnA0", "ffnA1"}:
        dram("w_inT_f", [D, FF_W]); dram("b_in_f", [FF_W]); dram("w_outT_f", [FF_WP, D])
    if "attnA" in need:
        dram("wf_a", [D, 3 * D]); dram("wo_a", [D, D]); dram("bo_a", [D])
    if "attnB" in need:
        dram("wq_b", [D, D]); dram("wk_b", [D, D]); dram("wv_b", [D, D])
        dram("wo_b", [D, D])
        dram("bq_b", [D]); dram("bk_b", [D]); dram("bv_b", [D]); dram("bo_b", [D])
    if need & {"ffnB0", "ffnB1"}:
        dram("w_gate", [FI, D]); dram("w_up", [FI, D]); dram("w_down", [D, FI])
        din["wgqT"] = nc.dram_tensor("wgqT", [D, FI], BF16, kind="Internal").ap()
        din["wuqT"] = nc.dram_tensor("wuqT", [D, FI], BF16, kind="Internal").ap()
        din["wdqT"] = nc.dram_tensor("wdqT", [FI, D], BF16, kind="Internal").ap()

    with ExitStack() as ctx:
        k = K(nc, ctx)
        e = {"din": din}
        e["x_sb"] = k.sb("x_sb", [128, NT * D], BF16)
        e["stage"] = k.sb("stage", [128, D], F32)
        e["ident"] = k.sb("ident_sb", [128, 128], BF16)
        e["ones_row"] = k.sb("ones_row", [1, 128], BF16)
        e["ones_col"] = k.sb("ones_col", [128, 1], BF16)
        e["tri"] = k.sb("tri", [128, 128], BF16)
        e["lnx_bf"] = k.sb("lnx_bf", [128, D], BF16)
        e["lnxT"] = k.sb("lnxT", [128, D], BF16)
        k.nc_junk_view = e["lnx_bf"][:]
        e["lnsc"] = make_ln_scratch(k, "g")
        WA = 67584
        SA = 17280
        e["warena"] = k.sb("warena", [128, WA], BF16)
        e["sarena"] = k.sb("sarena", [128, SA], BF16)
        wa, sa = e["warena"], e["sarena"]
        if need & {"ffnA0", "ffnA1"}:
            e["in_w_inT"] = din["w_inT_f"]
            e["in_b_in"] = din["b_in_f"]
            e["in_w_outT"] = din["w_outT_f"]
            e["w_inT"] = wa[:, 0:8 * FF_W]
            e["b_in"] = wa[0:1, 32768:32768 + FF_W]
            e["w_outT"] = wa[:, 16320:16320 + 16 * D]
            e["acts2"] = [sa[:, 0:FF_W], sa[:, 10240:10240 + FF_W]]
            e["dec2"] = [sa[:, 2048:2048 + FF_W], sa[:, 12288:12288 + FF_W]]
            e["vtree"] = sa[:, 4096:4096 + FF_W]
            e["masked"] = sa[:, 6144:6144 + FF_WP]
            e["maskedT"] = sa[:, 8192:8192 + FF_WP]
            e["ffnA_w_war"] = []
        if need & {"ffnB0", "ffnB1"}:
            e["fb_wg"] = wa[:, 0:8 * FI]
            e["fb_wu"] = wa[:, 22528:45056]
            e["fb_wd"] = wa[:, 45056:45056 + FI_NC * D]
            e["fb_lnx32"] = sa[:, 0:2048].bitcast(F32)
            e["fb_xq"] = sa[:, 2048:3072]
            e["fb_xqT"] = sa[:, 3072:4096]
            qa = sa[:, 4096:6144].bitcast(F32)
            qb = sa[:, 6144:8192].bitcast(F32)
            qc = sa[:, 8192:10240].bitcast(F32)
            qbm = sa[:, 10240:10624].bitcast(F32)
            qsc = sa[:, 10624:11008].bitcast(F32)
            qrc = sa[:, 11008:11392].bitcast(F32)
            e["fb_qs"] = {"a": qa, "b": qb, "c": qc, "bm": qbm, "scale": qsc,
                          "rec": qrc}
            e["fb_hq"] = sa[:, 11392:11392 + FI]
            e["fb_hqT"] = sa[:, 14208:15232]
            e["fb_hsil"] = sa[:, 15232:16256].bitcast(F32)
            e["fb_hmul"] = sa[:, 16256:17280].bitcast(F32)
            e["wprep"] = wa[:, 0:5632].bitcast(F32)
            e["wprep_q"] = wa[:, 5632:8448]
            e["wprep_qT"] = wa[:, 8448:10496]
            e["wprep_qs"] = {"a": wa[:, 10496:16128].bitcast(F32),
                             "b": wa[:, 16128:21760].bitcast(F32),
                             "c": wa[:, 21760:27392].bitcast(F32),
                             "bm": wa[:, 27392:27744].bitcast(F32),
                             "scale": wa[:, 27744:28096].bitcast(F32),
                             "rec": wa[:, 28096:28448].bitcast(F32)}
        if "attnB" in need:
            e["ab_wq"] = wa[:, 0:8192]
            e["ab_wk"] = wa[:, 8192:16384]
            e["ab_wv"] = wa[:, 16384:24576]
            e["ab_wo"] = wa[:, 24576:32768]
            e["ab_brows"] = wa[0:1, 32768:32768 + 4 * D]
            e["ab_qm"] = sa[0:64, 0:2048]
            e["ab_km"] = sa[0:64, 2048:4096]
            e["ab_vaug"] = sa[:, 4096:4096 + 8 * 130]
            e["ab_pt0"] = sa[:, 5440:5952]
            e["ab_pt1"] = sa[:, 5952:6464]
            e["ab_aT"] = sa[:, 6464:7488]
            e["ab_rec"] = sa[:, 7488:11584].bitcast(F32)
            e["ab_bc"] = sa[0:64, 11584:15680].bitcast(F32)
            e["ab_ones"] = k.sb("ab_ones", [128, 64], BF16)
            e["ab_onesf"] = k.sb("ab_onesf", [128, 64], F32)
        if "attnA" in need:
            e["aa_lnxT"] = wa[:, 0:16384]
            e["aa_yT"] = wa[:, 16384:32768]
            e["aa_wf"] = wa[:, 32768:57344]
            e["aa_wout"] = wa[:, 57344:65536]
            e["aa_cos"] = wa[:, 65536:66560].bitcast(F32)
            e["aa_sin"] = wa[:, 66560:67584].bitcast(F32)
            e["aa_pairm"] = sa[:, 0:8192]
            e["aa_vtok"] = sa[:, 8192:10240]
            e["aa_tok4"] = sa[:, 10240:10752]
            e["aa_expscr"] = sa[:, 10752:11776]
            e["aa_ropesc"] = k.sb("aa_ropesc", [128, 128], F32)[:]
            e["aa_pt"] = sa[:, 11776:12800]
            e["aa_sel"] = sa[0:32, 12800:14848]
            e["aa_recT"] = sa[0:32, 14848:14976]
            e["aa_rcb"] = sa[:, 14976:15008]
            e["aa_reccol"] = sa[:, 15040:15104].bitcast(F32)
            e["aa_denp"] = sa[:, 15104:15112].bitcast(F32)
            e["aa_bc"] = sa[:, 15112:16136].bitcast(F32)
            e["aa_bout"] = sa[0:1, 16136:17160]
        e["ps_tp"] = k.ps("ps_tp", [128, 2048], BF16)
        e["ps_l"] = k.ps("ps_l", [128, 2048], F32)
        e["ps_o"] = k.ps("ps_o", [128, 512], F32)
        e["ps_x"] = k.ps("ps_x", [128, 512], F32)

        ct = []
        ct.append(k.dma("sync", e["ident"][:], din["c_ident"]))
        ct.append(k.dma("sync", e["ones_row"][:], din["c_ones_row"]))
        ct.append(k.dma("sync", e["ones_col"][:], din["c_ones_col"]))
        ct.append(k.dma("sync", e["tri"][:], din["c_tri"]))
        e["const_tok"] = ct
        if need & {"ffnA0", "ffnA1"}:
            k.op("vector", lambda v: v.memset(e["masked"][:, FF_W:FF_WP], 0.0))
        if "attnB" in need:
            k.op("vector", lambda v: v.memset(e["ab_ones"][:], 1.0))
            k.op("vector", lambda v: v.memset(e["ab_onesf"][:], 1.0))
        e["wprep_done"] = []
        if need & {"ffnB0", "ffnB1"}:
            emit_ffnB_weight_prep(k, e)
            e["arena_war"] = list(e["wprep_done"])

        x4 = x_in.rearrange("(s t p) d -> s p t d", s=n_seq, p=128)
        y4 = y_out.rearrange("(s t p) d -> s p t d", s=n_seq, p=128)

        def seq_body(seq):
            if fused:
                xin = lambda kk: x4[bass.ds(kk.cur_i, 1)].squeeze(0)
            else:
                xin = x4[seq]
            tld = k.dma("gpsimd", e["x_sb"][:].rearrange("p (t d) -> p t d", d=D),
                        xin, waits=e.get("x_war", []))
            x_toks = [[tld] + ct for _ in range(NT)]
            for b in blocks:
                if b.startswith("ffnA"):
                    x_toks = emit_ffnA(k, e, seq, b[-1], x_toks)
                elif b == "attnA":
                    x_toks = emit_attnA(k, e, seq, x_toks)
                elif b.startswith("ffnB"):
                    x_toks = emit_ffnB(k, e, seq, b[-1], x_toks)
                elif b == "attnB":
                    x_toks = emit_attnB(k, e, seq, x_toks)
                bar = tmax(*[t for toks in x_toks for t in toks])
                e["arena_war"] = bar
                x_toks = [toks + bar for toks in x_toks]
            st_done = []
            for ti in range(NT):
                tok = k.op("scalar", lambda s, ti=ti: s.copy(
                    e["stage"][:], e["x_sb"][:, ti * D:(ti + 1) * D]),
                    waits=x_toks[ti] + e.get("stage_war", []))
                if fused:
                    yv = lambda kk, ti=ti: y4[bass.ds(kk.cur_i, 1)].squeeze(0)[:, ti, :]
                else:
                    yv = y4[seq][:, ti, :]
                sd = k.dma("sync", yv, e["stage"][:], waits=[tok])
                e["stage_war"] = [sd]
                st_done.append(sd)
            e["x_war"] = tmax(*(t for toks in x_toks for t in toks)) + st_done
            e["final_store"] = st_done

        if fused:
            k.n_iter = n_seq
            k.begin_loop()
            seq_body(0)
            k.mid_loop()
            seq_body(1)
            k.end_loop()
        else:
            for seq in range(n_seq):
                seq_body(seq)
        fw = e["final_store"]
        k.op("gpsimd", lambda g: g.engine_nop(), waits=fw, inc=False)
        k.replay()
    return nc


def _rope_deinterleave_cols(w):
    idx = np.zeros(D, dtype=np.int64)
    for h in range(NH):
        for i in range(32):
            idx[h * 64 + i] = h * 64 + 2 * i
            idx[h * 64 + 32 + i] = h * 64 + 2 * i + 1
    return np.ascontiguousarray(w[:, idx])


def host_prep(inputs, blocks):
    need = set(blocks)
    m = {}
    if need & {"ffnA0", "ffnA1"}:
        m["w_inT_f"] = np.ascontiguousarray(inputs["L0_ffn_W_in"].T)
        m["b_in_f"] = np.ascontiguousarray(inputs["L0_ffn_b_in"])
        wo = np.zeros((FF_WP, D), dtype=np.float32)
        wo[:FF_W] = inputs["L0_ffn_W_out"].T
        m["w_outT_f"] = wo
    if "attnA" in need:
        wqkv = inputs["L0_attn_Wqkv"]
        wq = _rope_deinterleave_cols(np.ascontiguousarray(wqkv[:, 0:D]))
        wk = _rope_deinterleave_cols(np.ascontiguousarray(wqkv[:, D:2 * D]))
        wv = np.ascontiguousarray(wqkv[:, 2 * D:3 * D])
        wf = np.empty((D, 3 * D), dtype=np.float32)
        for hp in range(NPAIR):
            wf[:, hp * 384:hp * 384 + 128] = wq[:, hp * 128:(hp + 1) * 128]
            wf[:, hp * 384 + 128:hp * 384 + 256] = wk[:, hp * 128:(hp + 1) * 128]
            wf[:, hp * 384 + 256:hp * 384 + 384] = wv[:, hp * 128:(hp + 1) * 128]
        m["wf_a"] = wf
        m["wo_a"] = np.ascontiguousarray(inputs["L0_attn_Wout"])
        m["bo_a"] = np.ascontiguousarray(inputs["L0_attn_bout"])
    if "attnB" in need:
        m["wq_b"] = np.ascontiguousarray(inputs["L1_attn_Wq"])
        m["wk_b"] = np.ascontiguousarray(inputs["L1_attn_Wk"])
        m["wv_b"] = np.ascontiguousarray(inputs["L1_attn_Wv"])
        m["wo_b"] = np.ascontiguousarray(inputs["L1_attn_Wo"])
        m["bq_b"] = np.ascontiguousarray(inputs["L1_attn_bq"])
        m["bk_b"] = np.ascontiguousarray(inputs["L1_attn_bk"])
        m["bv_b"] = np.ascontiguousarray(inputs["L1_attn_bv"])
        m["bo_b"] = np.ascontiguousarray(inputs["L1_attn_bo"])
    if need & {"ffnB0", "ffnB1"}:
        m["w_gate"] = np.ascontiguousarray(inputs["L1_ffn_w_gate"])
        m["w_up"] = np.ascontiguousarray(inputs["L1_ffn_w_up"])
        m["w_down"] = np.ascontiguousarray(inputs["L1_ffn_w_down"])
    c = host_consts()
    m["c_rope_cos"] = c["rope_cos"]; m["c_rope_sin"] = c["rope_sin"]
    m["c_tri"] = c["tri_mask"]; m["c_sel"] = c["selmat"]
    m["c_ones_col"] = c["ones_col"]; m["c_ones_row"] = c["ones_row"]
    m["c_ident"] = c["ident"]
    return m


def run_blocks(x_full, inputs, blocks, n_seq=NSEQ, n_cores=8, sim=False,
               fused=False):
    nc = build_program(blocks, n_seq=n_seq, fused=fused)
    wm = host_prep(inputs, blocks)
    rows = n_seq * SEQ
    in_maps = []
    for c in range(n_cores):
        im = dict(wm)
        im["x"] = np.ascontiguousarray(x_full[c * rows:(c + 1) * rows])
        in_maps.append(im)
    if sim:
        from concourse import bass_interp
        s = bass_interp.CoreSim(nc)
        for kk, vv in in_maps[0].items():
            s.tensor(kk)[:] = vv
        s.simulate()
        return np.array(s.tensor("y"))
    res = run_bass_kernel_spmd(nc, in_maps, core_ids=list(range(n_cores)))
    return np.concatenate([r["y"] for r in res.results], axis=0)


_NC_CACHE = {}


def kernel(**inputs):
    x = np.ascontiguousarray(inputs["L0_attn_x"].reshape(32 * SEQ, D))
    n_cores = 8
    if "fused" not in _NC_CACHE:
        _NC_CACHE["fused"] = build_program(ALL_BLOCKS, n_seq=NSEQ, fused=True)
    nc = _NC_CACHE["fused"]
    wm = host_prep(inputs, ALL_BLOCKS)
    rows = NSEQ * SEQ
    in_maps = []
    for c in range(n_cores):
        im = dict(wm)
        im["x"] = np.ascontiguousarray(x[c * rows:(c + 1) * rows])
        in_maps.append(im)
    res = run_bass_kernel_spmd(nc, in_maps, core_ids=list(range(n_cores)))
    return np.concatenate([r["y"] for r in res.results],
                          axis=0).reshape(32, SEQ, D)


SEXPMASK = int(np.int32(np.uint32(0xFF800000).view(np.int32)))
EXPDOWN2 = 0x01000000
MANTKEEP = int(np.int32(np.uint32(0xFFC00000).view(np.int32)))


FIXC = 6291456.0


def emit_quant(k, e, src, dst, nb, qs, waits):
    W = 16 * nb
    a, b, c = qs["a"], qs["b"], qs["c"]
    bm, scale, rec = qs["bm"], qs["scale"], qs["rec"]
    war = qs.get("war", [])
    src3 = src.rearrange("p (b s) -> p b s", s=16)
    t0 = k.op("vector", lambda v: v.tensor_reduce(
        bm[:, 0:nb].unsqueeze(2), src3, AX, ALU.max, apply_absolute_value=True),
        waits=list(waits) + war)
    t1 = k.op("vector", lambda v: v.tensor_scalar(
        scale[:, 0:nb], bm[:, 0:nb], 1.0 / 6.0, 1e-12, ALU.mult, ALU.max),
        waits=[t0])
    t2 = k.op("vector", lambda v: v.reciprocal(rec[:, 0:nb], scale[:, 0:nb]),
              waits=[t1])
    rec_bc = rec[:, 0:nb].unsqueeze(2).broadcast_to((128, nb, 16))
    sc_bc = scale[:, 0:nb].unsqueeze(2).broadcast_to((128, nb, 16))
    a3 = a[:, 0:W].rearrange("p (b s) -> p b s", s=16)
    t3 = k.op("vector", lambda v: v.tensor_tensor(a3, src3, rec_bc, ALU.mult),
              waits=[t2] + list(waits))
    ai = a[:, 0:W].bitcast(I32)
    bi = b[:, 0:W].bitcast(I32)
    ci = c[:, 0:W].bitcast(I32)
    t4a = k.op("vector", lambda v: v.tensor_scalar(
        bi, ai, SEXPMASK, None, ALU.bitwise_and), waits=[t3])
    t4 = k.op("vector", lambda v: v.tensor_scalar(
        ci, bi, EXPDOWN2, None, ALU.subtract), waits=[t4a])
    t5 = k.op("vector", lambda v: v.tensor_tensor(
        b[:, 0:W], a[:, 0:W], c[:, 0:W], ALU.add), waits=[t4])
    t6 = k.op("vector", lambda v: v.tensor_scalar(
        ci, bi, MANTKEEP, None, ALU.bitwise_and), waits=[t5])
    t6b = k.op("vector", lambda v: v.tensor_scalar(
        b[:, 0:W], a[:, 0:W], FIXC, FIXC, ALU.add, ALU.subtract), waits=[t6])
    t7a1 = k.op("vector", lambda v: v.tensor_scalar(
        ai, ai, 0x7FFFFFFF, None, ALU.bitwise_and), waits=[t6b])
    t7a = k.op("vector", lambda v: v.tensor_scalar(
        ai, ai, 0x40000000, None, ALU.is_ge), waits=[t7a1])
    t7 = k.op("vector", lambda v: v.copy_predicated(
        b[:, 0:W], ai, c[:, 0:W]), waits=[t7a])
    b3 = b[:, 0:W].rearrange("p (b s) -> p b s", s=16)
    t8 = k.op("vector", lambda v: v.tensor_tensor(
        dst.rearrange("p (b s) -> p b s", s=16), b3, sc_bc, ALU.mult),
        waits=[t7])
    qs["war"] = [t8]
    return [t8]


def emit_ffnB_weight_prep(k, e):
    qs = e["wprep_qs"]
    wtmp = e["wprep"]
    wq = e["wprep_q"]
    wqT = e["wprep_qT"]
    ident = e["ident"]
    ps_tp = e["ps_tp"]
    last = list(e["const_tok"])
    for (src_name, dst_name, nrow_ch, W) in (
            ("w_gate", "wgqT", FI_NC, D), ("w_up", "wuqT", FI_NC, D),
            ("w_down", "wdqT", 8, FI)):
        src = e["din"][src_name].rearrange("(t p) n -> t p n", p=128)
        dstv = e["din"][dst_name].rearrange("(c p) n -> p c n", p=128)
        for t in range(nrow_ch):
            ld = k.dma("sync", wtmp[:, 0:W], src[t], waits=qs.get("war", []) + last)
            qt = emit_quant(k, e, wtmp[:, 0:W], wq[:, 0:W], W // 16, qs, [ld])
            nch = W // 128
            for g0 in range(0, nch, 16):
                g1 = min(g0 + 16, nch)
                tt = emit_transpose(
                    k, lambda cc, g0=g0: wq[:, (g0 + cc) * 128:(g0 + cc + 1) * 128],
                    g1 - g0, ps_tp, wqT[:, 0:(g1 - g0) * 128], ident,
                    qt, e.get("wprep_tp_war", []))
                sd = k.dma("sync",
                           dstv[:, g0:g1, t * 128:(t + 1) * 128],
                           wqT[:, 0:(g1 - g0) * 128].rearrange(
                               "p (c j) -> p c j", j=128),
                           waits=tt)
                e["wprep_tp_war"] = [sd]
                last = [sd]
    e["wprep_done"] = last


def emit_ffnB(k, e, seq, call, x_toks):
    lnx32, xq, xqT = e["fb_lnx32"], e["fb_xq"], e["fb_xqT"]
    hsil, hmul, hq, hqT = e["fb_hsil"], e["fb_hmul"], e["fb_hq"], e["fb_hqT"]
    wg, wu, wd = e["fb_wg"], e["fb_wu"], e["fb_wd"]
    qs = e["fb_qs"]
    ps_tp, ps_l, ps_o = e["ps_tp"], e["ps_l"], e["ps_o"]
    lnsc = e["lnsc"]
    x_sb = e["x_sb"]
    ident = e["ident"]
    war_w = e.get("fb_w_war", []) + e["wprep_done"] + e.get("arena_war", [])
    tw = []
    tw.append(k.dma("sync", wg[:].rearrange("p (c n) -> p c n", n=FI),
                    e["din"]["wgqT"].rearrange("(c p) n -> p c n", p=128),
                    waits=war_w))
    tw.append(k.dma("sync", wu[:].rearrange("p (c n) -> p c n", n=FI),
                    e["din"]["wuqT"].rearrange("(c p) n -> p c n", p=128),
                    waits=war_w))
    tw.append(k.dma("sync", wd[:].rearrange("p (c n) -> p c n", n=D),
                    e["din"]["wdqT"].rearrange("(c p) n -> p c n", p=128),
                    waits=war_w))

    JCH = [(0, 512), (512, 1024), (1024, 1536), (1536, 2048), (2048, 2560),
           (2560, 2816)]
    prev = {"hq_used": [], "stt": [], "xqT_used": [], "gate_used": [[], []],
            "hmul_used": []}
    new_x_toks = []
    last_user = list(tw)
    for ti in range(NT):
        x_tile = x_sb[:, ti * D:(ti + 1) * D]
        lntok = emit_ln(k, None, x_tile, x_toks[ti], lnx32[:, 0:D], lnsc)
        qtok = emit_quant(k, e, lnx32[:, 0:D], xq[:, 0:D], D // 16, qs, lntok)
        ttok = emit_transpose(k, lambda cc: xq[:, cc * 128:(cc + 1) * 128], 8,
                              ps_tp, xqT[:, 0:D], ident, qtok,
                              prev["xqT_used"] + prev["hq_used"])
        prev["xqT_used"] = []
        htoks = [None] * len(JCH)
        for ji, (j0, j1) in enumerate(JCH):
            jw = j1 - j0
            po = (ji % 2) * 1024
            gw = ttok + tw + prev["gate_used"][ji % 2]
            gt = ut = None
            for c in range(8):
                gt = k.op("tensor", lambda t, c=c, j0=j0, j1=j1, po=po: t.matmul(
                    ps_l[:, po:po + j1 - j0], xqT[:, c * 128:(c + 1) * 128],
                    wg[:, c * FI + j0:c * FI + j1],
                    start=(c == 0), stop=(c == 7)), waits=gw)
            for c in range(8):
                ut = k.op("tensor", lambda t, c=c, j0=j0, j1=j1, po=po: t.matmul(
                    ps_l[:, po + 512:po + 512 + j1 - j0],
                    xqT[:, c * 128:(c + 1) * 128],
                    wu[:, c * FI + j0:c * FI + j1],
                    start=(c == 0), stop=(c == 7)), waits=gw)
            prev["xqT_used"] = [gt, ut]
            st = k.op("scalar", lambda s, jw=jw, po=po: s.activation(
                hsil[:, 0:jw], ps_l[:, po:po + jw], AF.Silu),
                waits=[gt] + prev["hmul_used"])
            ht = k.op("vector", lambda v, jw=jw, po=po: v.tensor_tensor(
                hmul[:, 0:jw], hsil[:, 0:jw], ps_l[:, po + 512:po + 512 + jw],
                ALU.mult), waits=[st, ut])
            prev["gate_used"][ji % 2] = [ht]
            prev["hmul_used"] = [ht]
            qh = emit_quant(k, e, hmul[:, 0:jw], hq[:, j0:j1], jw // 16, qs, [ht])
            prev["hmul_used"] = prev["hmul_used"] + qh
            htoks[ji] = qh
        otoks = []
        ow_war = prev["stt"]
        dtoks = []
        for gi, (g0, g1) in enumerate(((0, 8), (8, 16), (16, 22))):
            need = [t for ji, (j0, j1) in enumerate(JCH)
                    if j0 < g1 * 128 and j1 > g0 * 128 for t in htoks[ji]]
            tt = emit_transpose(
                k, lambda cc, g0=g0: hq[:, (g0 + cc) * 128:(g0 + cc + 1) * 128],
                g1 - g0, ps_tp, hqT[:, 0:(g1 - g0) * 128], ident,
                need + prev["hq_used"], ttok + dtoks)
            prev["hq_used"] = []
            for cc in range(g0, g1):
                for (pso, n0) in ((e["ps_o"], 0), (e["ps_x"], 512)):
                    dt_ = k.op("tensor", lambda t, cc=cc, g0=g0, n0=n0, pso=pso:
                               t.matmul(pso[:],
                                        hqT[:, (cc - g0) * 128:(cc - g0 + 1) * 128],
                                        wd[:, cc * D + n0:cc * D + n0 + 512],
                                        start=(cc == 0), stop=(cc == 21)),
                               waits=tt + tw + ow_war)
                    if cc == g1 - 1:
                        otoks.append(dt_)
                    dtoks.append(dt_)
            prev["hq_used"] = [dtoks[-1]]
        rts = []
        for (pso, n0, dt_) in ((e["ps_o"], 0, otoks[-2]), (e["ps_x"], 512, otoks[-1])):
            rtok = k.op("vector", lambda v, x_tile=x_tile, n0=n0, pso=pso:
                        v.scalar_tensor_tensor(
                            x_tile[:, n0:n0 + 512], pso[:], 0.5,
                            x_tile[:, n0:n0 + 512], ALU.mult, ALU.add),
                        waits=[dt_] + x_toks[ti] + lntok)
            rts.append(rtok)
        prev["stt"] = rts
        new_x_toks.append(rts)
        last_user += rts
    e["fb_w_war"] = tmax(*last_user)
    return new_x_toks


def emit_attnB(k, e, seq, x_toks):
    x_sb = e["x_sb"]
    lnx, lnxT = e["lnx_bf"], e["lnxT"]
    lnsc = e["lnsc"]
    ident = e["ident"]
    ones_row = e["ones_row"]
    ps_tp, ps_l, ps_o, ps_x = e["ps_tp"], e["ps_l"], e["ps_o"], e["ps_x"]
    Qm, Km = e["ab_qm"], e["ab_km"]
    vaug = e["ab_vaug"]
    pt0, pt1 = e["ab_pt0"], e["ab_pt1"]
    recs = e["ab_rec"]
    bc = e["ab_bc"]
    aT = e["ab_aT"]
    onesb = e["ab_ones"]
    onesf = e["ab_onesf"]
    wq, wk, wv, wo = e["ab_wq"], e["ab_wk"], e["ab_wv"], e["ab_wo"]
    brows = e["ab_brows"]

    war_w = e.get("ab_w_war", []) + e.get("arena_war", [])
    tw = []
    for wsb, nm in ((wq, "wq_b"), (wk, "wk_b"), (wv, "wv_b"), (wo, "wo_b")):
        tw.append(k.dma("gpsimd", wsb.rearrange("p (c n) -> p c n", n=D),
                        e["din"][nm].rearrange("(c p) n -> p c n", p=128),
                        waits=war_w))
    for i, nm in enumerate(("bq_b", "bk_b", "bv_b", "bo_b")):
        tw.append(k.dma("gpsimd", brows[:, i * D:(i + 1) * D],
                        e["din"][nm].unsqueeze(0), waits=war_w))
    va3m = vaug[:].rearrange("p (c s) -> p c s", s=130)
    mz1 = k.op("vector", lambda v: v.memset(va3m[:, :, 64:65], 1.0), waits=war_w)
    mz2 = k.op("vector", lambda v: v.memset(va3m[:, :, 129:130], 1.0),
               waits=war_w + [mz1])
    tw = tw + [mz1, mz2]

    prev = {"lnxT_used": [], "qk_evac": [], "v_evac": [], "exp": [],
            "pv": [], "stt": [], "aT_used": [], "bc_used": [], "rec_used": []}
    new_x_toks = []
    last_user = list(tw)
    for ti in range(NT):
        x_tile = x_sb[:, ti * D:(ti + 1) * D]
        lntok = emit_ln(k, None, x_tile, x_toks[ti] + prev["lnxT_used"], lnx[:],
                        lnsc)
        ttok = emit_transpose(k, lambda c: lnx[:, c * 128:(c + 1) * 128], 8,
                              ps_tp, lnxT[:], ident, lntok, prev["lnxT_used"])
        prev["lnxT_used"] = [ttok[0]]
        qk_toks = []
        for (dst, wsb, boff, scl, poff) in ((Qm, wq, 0, 1.0 / 1024.0, 0),
                                            (Km, wk, 1, 1.0, 1024)):
            mm = None
            for par in (0, 1):
                for c in range(8):
                    for (n0, n1) in ((0, 512), (512, 1024)):
                        mm = k.op("tensor", lambda t, c=c, n0=n0, n1=n1,
                                  par=par, wsb=wsb, poff=poff: t.matmul(
                                      ps_l[par * 64:(par + 1) * 64,
                                           poff + n0:poff + n1],
                                      lnxT[:, c * 128 + par:(c + 1) * 128:2],
                                      wsb[:, c * D + n0:c * D + n1],
                                      start=(c == 0), stop=False,
                                      tile_position=(0, 64 * par)),
                                  waits=ttok + tw + prev["qk_evac"])
                for (n0, n1) in ((0, 512), (512, 1024)):
                    mm = k.op("tensor", lambda t, n0=n0, n1=n1, par=par,
                              boff=boff, poff=poff: t.matmul(
                                  ps_l[par * 64:(par + 1) * 64,
                                       poff + n0:poff + n1],
                                  ones_row[:, 0:64],
                                  brows[:, boff * D + n0:boff * D + n1],
                                  start=False, stop=True,
                                  tile_position=(0, 64 * par)),
                              waits=ttok + tw + prev["qk_evac"])
            ev0 = k.op("scalar", lambda s, dst=dst, scl=scl, poff=poff:
                       s.activation(dst[:, 0:1024], ps_l[0:64, poff:poff + 1024],
                                    AF.Identity, scale=float(scl)),
                       waits=[mm] + prev["pv"])
            ev1 = k.op("scalar", lambda s, dst=dst, scl=scl, poff=poff:
                       s.activation(dst[:, 1024:2048],
                                    ps_l[64:128, poff:poff + 1024],
                                    AF.Identity, scale=float(scl)),
                       waits=[mm] + prev["pv"])
            qk_toks += [ev0, ev1]
        vev = []
        for half in (0, 1):
            pso = (ps_o, ps_x)[half]
            hm = []
            for c2 in range(4):
                c = half * 4 + c2
                mm = None
                for cd in range(8):
                    mm = k.op("tensor", lambda t, c=c, c2=c2, cd=cd, pso=pso:
                              t.matmul(
                                  pso[:, c2 * 128:(c2 + 1) * 128],
                                  wv[:, cd * D + c * 128:cd * D + (c + 1) * 128],
                                  lnxT[:, cd * 128:(cd + 1) * 128],
                                  start=(cd == 0), stop=False),
                              waits=ttok + tw + prev["v_evac"])
                mm = k.op("tensor", lambda t, c=c, c2=c2, pso=pso: t.matmul(
                    pso[:, c2 * 128:(c2 + 1) * 128],
                    brows[:, 2 * D + c * 128:2 * D + (c + 1) * 128],
                    ones_row[:],
                    start=False, stop=True),
                    waits=ttok + tw + prev["v_evac"])
                hm.append(mm)
            va4 = vaug[:].rearrange("p (c q s) -> p c q s", q=2, s=65)
            psv = pso[:].rearrange("p (c q s) -> p c s q", q=64, s=2)
            ev = k.op("vector", lambda v, half=half, psv=psv, va4=va4:
                      v.tensor_copy(
                          va4[:, half * 4:(half + 1) * 4, :, 0:64], psv),
                      waits=hm)
            vev.append(ev)
        vmm = vev
        pv_toks = []
        exp_toks = []
        for kt in range(16):
            par, kc = (0, kt) if kt < 8 else (1, kt - 8)
            vslice = vaug[:, kc * 130 + par * 65:kc * 130 + par * 65 + 65]
            for qc in range(4):
                pbuf = (pt0, pt1)[(kt * 4 + qc) % 2]
                pso = (ps_o, ps_x)[(kt * 4 + qc) % 2][:]
                smm = k.op("tensor", lambda t, kt=kt, qc=qc, pso=pso: t.matmul(
                    pso, Km[:, kt * 128:(kt + 1) * 128],
                    Qm[:, qc * 512:(qc + 1) * 512],
                    start=True, stop=True),
                    waits=qk_toks + prev["exp"] + vmm)
                ex = k.op("scalar", lambda s, pbuf=pbuf, pso=pso: s.activation(
                    pbuf[:], pso, AF.Exp), waits=[smm] + prev["pv"])
                exp_toks.append(ex)
                prev["exp"] = exp_toks[-2:]
                pv = k.op("tensor", lambda t, kt=kt, qc=qc, vslice=vslice,
                          pbuf=pbuf: t.matmul(
                              ps_l[0:65, qc * 512:(qc + 1) * 512],
                              vslice, pbuf[:],
                              start=(kt == 0), stop=(kt == 15)),
                          waits=[ex] + prev["aT_used"] + qk_toks)
                pv_toks.append(pv)
                prev["pv"] = pv_toks[-2:]
        rc = k.op("vector", lambda v: v.reciprocal(recs[64:65, :],
                                                   ps_l[64:65, 0:2048]),
                  waits=pv_toks[-4:] + prev["rec_used"])
        bcm = []
        for qc in range(4):
            psb = (ps_o, ps_x)[qc % 2]
            bm_ = k.op("tensor", lambda t, qc=qc, psb=psb: t.matmul(
                psb[0:64, 0:512],
                onesf[64:65, 0:64],
                recs[64:65, qc * 512:(qc + 1) * 512],
                start=True, stop=True),
                waits=[rc] + prev["bc_used"] + prev["exp"] + bcm[-2:])
            bv_ = k.op("vector", lambda v, qc=qc, psb=psb: v.tensor_copy(
                bc[:, qc * 512:(qc + 1) * 512], psb[0:64, 0:512]),
                waits=[bm_])
            bcm.append(bv_)
        prev["rec_used"] = bcm[-1:]
        prev["bc_used"] = []
        aT_toks = []
        for c in range(8):
            for half in (0, 1):
                at = k.op("vector", lambda v, c=c, half=half: v.tensor_tensor(
                    aT[half * 64:half * 64 + 64, c * 128:(c + 1) * 128],
                    ps_l[0:64, 2 * c + half:2048:16],
                    bc[:, 2 * c + half:2048:16], ALU.mult),
                    waits=bcm + pv_toks[-4:]),
                aT_toks.append(at[0])
        prev["bc_used"] = aT_toks
        omm = []
        rts = []
        ow = aT_toks + tw + prev["stt"]
        for (pso, n0) in ((ps_o, 0), (ps_x, 512)):
            mm = None
            for c in range(8):
                mm = k.op("tensor", lambda t, c=c, n0=n0, pso=pso: t.matmul(
                    pso[:], aT[:, c * 128:(c + 1) * 128],
                    wo[:, c * D + n0:c * D + n0 + 512],
                    start=(c == 0), stop=False), waits=ow)
            mm = k.op("tensor", lambda t, n0=n0, pso=pso: t.matmul(
                pso[:], ones_row[:],
                brows[:, 3 * D + n0:3 * D + n0 + 512], start=False, stop=True),
                waits=ow)
            omm.append(mm)
            rtok = k.op("vector", lambda v, x_tile=x_tile, n0=n0, pso=pso:
                        v.tensor_tensor(x_tile[:, n0:n0 + 512], pso[:],
                                        x_tile[:, n0:n0 + 512], ALU.add),
                        waits=[mm] + x_toks[ti] + lntok)
            rts.append(rtok)
        prev["aT_used"] = omm
        prev["stt"] = rts
        prev["qk_evac"] = rts
        prev["v_evac"] = rts
        new_x_toks.append(rts)
        last_user += rts
    e["ab_w_war"] = tmax(*last_user)
    return new_x_toks


def emit_attnA(k, e, seq, x_toks):
    import os
    AA_NPAIR = int(os.environ.get("AA_NPAIR", str(NPAIR)))
    AA_F1 = int(os.environ.get("AA_F1", "1"))
    AA_F2 = int(os.environ.get("AA_F2", "1"))
    x_sb = e["x_sb"]
    lnx = e["lnx_bf"]
    lnsc = e["lnsc"]
    ident, ones_row, tri = e["ident"], e["ones_row"], e["tri"]
    ps_tp, ps_l, ps_o, ps_x = e["ps_tp"], e["ps_l"], e["ps_o"], e["ps_x"]
    lnxT = e["aa_lnxT"]
    yT = e["aa_yT"]
    wf = e["aa_wf"]
    wout = e["aa_wout"]
    cosf, sinf = e["aa_cos"], e["aa_sin"]
    tok4 = e["aa_tok4"]
    pairm = e["aa_pairm"]
    vtok = e["aa_vtok"]
    expscr = e["aa_expscr"]
    ptile = e["aa_pt"]
    sel = e["aa_sel"]
    recT = e["aa_recT"]
    rcb = e["aa_rcb"]
    reccol = e["aa_reccol"]
    denp = e["aa_denp"]
    bcsb = e["aa_bc"]
    bout = e["aa_bout"]
    ropesc = e["aa_ropesc"]

    war_w = e.get("aa_w_war", []) + e.get("arena_war", [])
    tw = []
    for wsb, nm, nn in ((wf, "wf_a", 3 * D), (wout, "wo_a", D)):
        tw.append(k.dma("gpsimd", wsb.rearrange("p (c n) -> p c n", n=nn),
                        e["din"][nm].rearrange("(c p) n -> p c n", p=128),
                        waits=war_w))
    tw.append(k.dma("gpsimd", bout[:], e["din"]["bo_a"].unsqueeze(0), waits=war_w))
    tw.append(k.dma("sync", cosf, e["din"]["c_rope_cos"], waits=war_w))
    tw.append(k.dma("sync", sinf, e["din"]["c_rope_sin"], waits=war_w))
    tw.append(k.dma("sync", sel[:], e["din"]["c_sel"], waits=war_w))

    lnxT_toks = []
    prev_tp = []
    ln_last = []
    tp_war = []
    for ti in range(NT):
        x_tile = x_sb[:, ti * D:(ti + 1) * D]
        lntok = emit_ln(k, None, x_tile, x_toks[ti] + war_w + tp_war, lnx[:],
                        lnsc)
        ln_last.append(lntok[0])
        dst = lnxT[:].rearrange("p (c n) -> p c n", n=SEQ)[
            :, :, ti * 128:(ti + 1) * 128]
        tt = emit_transpose(k, lambda c: lnx[:, c * 128:(c + 1) * 128], 8,
                            ps_tp, dst, ident, lntok, prev_tp + tp_war)
        prev_tp = tt
        tp_war = [k.barrier_token("tensor")]
        lnxT_toks += tt

    prev = {"pairm_used": [], "vtok_used": [], "unrot_exp": [], "rot_exp": [],
            "pv": [], "bc_used": [], "yT_used": [], "tok_used": [],
            "denp_used": [], "reccol_used": [], "ropescr_war": []}
    yT_toks = []
    qrT = pairm[:, 0 * SEQ:1 * SEQ]
    qT = pairm[:, 1 * SEQ:2 * SEQ]
    krT = pairm[:, 2 * SEQ:3 * SEQ]
    kT = pairm[:, 3 * SEQ:4 * SEQ]
    for hp in range(AA_NPAIR):
        ptoks = []
        for ti in range(NT):
            pw = lnxT_toks + tw + prev["tok_used"] + prev["unrot_exp"] + \
                prev["rot_exp"] + prev["bc_used"] + prev.get("bcast_used", [])
            m = None
            for c in range(8):
                m = k.op("tensor", lambda t, c=c, hp=hp, ti=ti: t.matmul(
                    ps_l[:, 0:384],
                    lnxT[:, c * SEQ + ti * 128:c * SEQ + (ti + 1) * 128],
                    wf[:, c * 3 * D + hp * 384:c * 3 * D + (hp + 1) * 384],
                    start=(c == 0), stop=(c == 7)),
                    waits=pw + prev["vtok_used"])
            mm = {0: m, 1: m, 2: m}
            evq = k.op("scalar", lambda s: s.copy(tok4[:, 0:128], ps_l[:, 0:128]),
                       waits=[m] + prev["tok_used"])
            evk = k.op("scalar", lambda s: s.copy(tok4[:, 256:384],
                                                  ps_l[:, 128:256]),
                       waits=[m] + prev["tok_used"])
            evv = k.op("scalar", lambda s, ti=ti: s.copy(
                vtok[:, ti * 128:(ti + 1) * 128], ps_l[:, 256:384]),
                waits=[m] + prev["vtok_used"])
            rtoks = []
            AA_ROPE = int(os.environ.get("AA_ROPE", "1"))
            for (src0, out0, src_ev) in (((0, 128, evq), (256, 384, evk))
                                         if AA_ROPE else ()):
                pv3 = tok4[:, src0:src0 + 128].rearrange("p (h i) -> p h i", i=64)
                te, to = pv3[:, :, 0:32], pv3[:, :, 32:64]
                cosb = cosf[:, ti * 32:(ti + 1) * 32].unsqueeze(1).broadcast_to(
                    (128, 2, 32))
                sinb = sinf[:, ti * 32:(ti + 1) * 32].unsqueeze(1).broadcast_to(
                    (128, 2, 32))
                sc3 = ropesc[:].rearrange("p (h i) -> p h i", i=64)
                sc_a, sc_b = sc3[:, :, 0:32], sc3[:, :, 32:64]
                o3 = tok4[:, out0:out0 + 128].rearrange("p (h i) -> p h i", i=64)
                ro_re, ro_ro = o3[:, :, 0:32], o3[:, :, 32:64]
                w0 = [src_ev] + prev["tok_used"] + prev["ropescr_war"] + rtoks
                RN = int(os.environ.get("AA_RN", "6"))
                r1 = k.op("vector", lambda v, te=te, cosb=cosb, sc_a=sc_a:
                          v.tensor_tensor(sc_a, te, cosb, ALU.mult), waits=w0)
                last = r1
                if RN >= 2:
                    r2 = k.op("vector", lambda v, to=to, sinb=sinb, sc_b=sc_b:
                              v.tensor_tensor(sc_b, to, sinb, ALU.mult), waits=w0)
                    last = r2
                if RN >= 3:
                    r3 = k.op("vector", lambda v, sc_a=sc_a, sc_b=sc_b, ro_re=ro_re:
                              v.tensor_tensor(ro_re, sc_a, sc_b, ALU.subtract),
                              waits=[r1, r2])
                    last = r3
                if RN >= 4:
                    r4 = k.op("vector", lambda v, te=te, sinb=sinb, sc_a=sc_a:
                              v.tensor_tensor(sc_a, te, sinb, ALU.mult), waits=[r3])
                    last = r4
                if RN >= 5:
                    r5 = k.op("vector", lambda v, to=to, cosb=cosb, sc_b=sc_b:
                              v.tensor_tensor(sc_b, to, cosb, ALU.mult), waits=[r3])
                    last = r5
                if RN >= 6:
                    r6 = k.op("vector", lambda v, sc_a=sc_a, sc_b=sc_b, ro_ro=ro_ro:
                              v.tensor_tensor(ro_ro, sc_a, sc_b, ALU.add),
                              waits=[r4, r5])
                    last = r6
                rtoks += ([r3, r6] if RN >= 6 else [last])
            prev["ropescr_war"] = rtoks[-2:]
            ttw = [evq, evk, evv] + rtoks + prev["pairm_used"]
            tts = []
            for (src0, m) in ((128, 0), (0, 1), (384, 2), (256, 3)):
                tt = k.op("tensor", lambda t, src0=src0, m=m: t.transpose(
                    ps_tp[:, m * 128:(m + 1) * 128],
                    tok4[:, src0:src0 + 128], ident[:]), waits=ttw)
                tts.append(tt)
            pm = pairm[:].rearrange("p (m n) -> p m n", n=SEQ)
            ev = k.op("vector", lambda v, ti=ti, pm=pm: v.tensor_copy(
                pm[:, :, ti * 128:(ti + 1) * 128],
                ps_tp[:, 0:512].rearrange("p (m n) -> p m n", n=128)),
                waits=tts)
            ptoks.append(ev)
            ptoks.append(evv)
            prev["tok_used"] = [tts[-1], ev]
            prev["vtok_used"] = [evv]
        rc_toks = []
        for qt in range(NT if AA_F1 else 0):
            etoks = []
            for h in (0, 1):
                for kh in (0, 1):
                    m2 = []
                    for nn in (0, 512):
                        m2.append(k.op(
                            "tensor", lambda t, h=h, kh=kh, nn=nn, qt=qt:
                            t.matmul(
                                ps_l[:, h * 1024 + nn:h * 1024 + nn + 512],
                                qT[h * 64:(h + 1) * 64, qt * 128:(qt + 1) * 128],
                                kT[h * 64:(h + 1) * 64,
                                   kh * 1024 + nn:kh * 1024 + nn + 512],
                                start=True, stop=True),
                            waits=ptoks + etoks + prev["unrot_exp"]))
                    ex = k.op("scalar", lambda s, h=h, kh=kh: s.activation(
                        expscr[:], ps_l[:, h * 1024:h * 1024 + 1024],
                        AF.Exp, scale=0.125,
                        accum_out=denp[:, h * 2 + kh:h * 2 + kh + 1]),
                        waits=m2 + prev["denp_used"])
                    etoks.append(ex)
            prev["unrot_exp"] = etoks[-2:]
            d0 = k.op("vector", lambda v, qt=qt: v.tensor_tensor(
                reccol[:, qt:qt + 1], denp[:, 0:1], denp[:, 1:2], ALU.add),
                waits=etoks + prev["reccol_used"])
            d1 = k.op("vector", lambda v, qt=qt: v.tensor_tensor(
                reccol[:, 16 + qt:16 + qt + 1], denp[:, 2:3], denp[:, 3:4],
                ALU.add), waits=etoks)
            prev["denp_used"] = [d0, d1]
            r0 = k.op("vector", lambda v, qt=qt: v.reciprocal(
                reccol[:, qt:qt + 1], reccol[:, qt:qt + 1]), waits=[d0])
            r1 = k.op("vector", lambda v, qt=qt: v.reciprocal(
                reccol[:, 16 + qt:16 + qt + 1], reccol[:, 16 + qt:16 + qt + 1]),
                waits=[d1])
            rc_toks += [r0, r1]
        cb = k.op("vector", lambda v: v.tensor_copy(rcb[:], reccol[:, 0:32]),
                  waits=rc_toks + prev["reccol_used"])
        tt = k.op("tensor", lambda t: t.transpose(
            ps_tp[0:32, 0:128], rcb[:], ident[:]),
            waits=[cb] + prev["tok_used"])
        rtT = k.op("vector", lambda v: v.tensor_copy(recT[:], ps_tp[0:32, 0:128]),
                   waits=[tt])
        prev["reccol_used"] = [rtT]
        prev["tok_used"] = prev["tok_used"] + [rtT]
        yevs = []
        for qc in range(4 if AA_F2 else 0):
            pv_last = []
            nkt = 4 * qc + 4
            for kt in range(nkt):
                pp = kt % 2
                smm = {}
                for h in (0, 1):
                    smm[h] = k.op("tensor", lambda t, h=h, kt=kt, qc=qc, pp=pp:
                                  t.matmul(
                                      ps_l[:, pp * 1024 + h * 512:
                                           pp * 1024 + h * 512 + 512],
                                      krT[h * 64:(h + 1) * 64,
                                          kt * 128:(kt + 1) * 128],
                                      qrT[h * 64:(h + 1) * 64,
                                          qc * 512:(qc + 1) * 512],
                                      start=True, stop=True),
                                  waits=ptoks + prev["rot_exp"] +
                                        prev["unrot_exp"] +
                                        prev.get("bcast_used", []))
                j = kt - 4 * qc
                ex = k.op("scalar", lambda s, pp=pp: s.activation(
                    ptile[:], ps_l[:, pp * 1024:pp * 1024 + 1024],
                    AF.Exp, scale=0.125),
                    waits=[smm[0], smm[1]] + prev["pv"])
                extoks = [ex]
                prev["rot_exp"] = extoks
                masks = []
                if j >= 0:
                    for h in (0, 1):
                        mk = k.op("vector", lambda v, h=h, j=j: v.tensor_tensor(
                            ptile[:, h * 512 + j * 128:h * 512 + (j + 1) * 128],
                            ptile[:, h * 512 + j * 128:h * 512 + (j + 1) * 128],
                            tri[:], ALU.mult), waits=extoks)
                        masks.append(mk)
                        if j > 0:
                            mz = k.op("vector", lambda v, h=h, j=j: v.memset(
                                ptile[:, h * 512:h * 512 + j * 128], 0.0),
                                waits=extoks)
                            masks.append(mz)
                pvw = extoks + masks + ptoks
                pv2 = []
                for h in (0, 1):
                    pso = (ps_o, ps_x)[h]
                    pv = k.op("tensor", lambda t, h=h, kt=kt, qc=qc,
                              pso=pso, nkt=nkt: t.matmul(
                                  pso[h * 64:(h + 1) * 64, 0:512],
                                  vtok[:, kt * 128 + h * 64:
                                       kt * 128 + h * 64 + 64],
                                  ptile[:, h * 512:(h + 1) * 512],
                                  start=(kt == 0), stop=(kt == nkt - 1),
                                  tile_position=(0, 64 * h)),
                              waits=pvw + prev["yT_used"])
                    pv2.append(pv)
                prev["pv"] = pv2
                pv_last = pv2
            bmm = []
            for j in range(4):
                bm_ = k.op("tensor", lambda t, j=j, qc=qc: t.matmul(
                    ps_l[:, j * 128:(j + 1) * 128],
                    sel[:, (qc * 4 + j) * 128:(qc * 4 + j + 1) * 128],
                    recT[:], start=True, stop=True),
                    waits=[rtT] + prev["bc_used"] + prev["rot_exp"])
                bmm.append(bm_)
            bev = k.op("vector", lambda v: v.tensor_copy(bcsb[:], ps_l[:, 0:512]),
                       waits=bmm)
            prev["bcast_used"] = [bev]
            yevs = []
            for h in (0, 1):
                pso = (ps_o, ps_x)[h]
                yev = k.op("vector", lambda v, hp=hp, qc=qc, h=h, pso=pso:
                           v.tensor_tensor(
                               yT[h * 64:(h + 1) * 64,
                                  hp * SEQ + qc * 512:hp * SEQ + (qc + 1) * 512],
                               pso[h * 64:(h + 1) * 64, 0:512],
                               bcsb[h * 64:(h + 1) * 64, :], ALU.mult),
                           waits=[bev] + pv_last + prev["yT_used"])
                yevs.append(yev)
            prev["bc_used"] = yevs
            prev["yT_used"] = yevs
            yT_toks += yevs
        prev["pairm_used"] = yevs if yevs else prev["tok_used"]
        prev["vtok_used"] = prev["pv"] if prev["pv"] else prev["tok_used"]
    new_x_toks = []
    stt_prev = []
    last_user = []
    for ti in range(NT):
        x_tile = x_sb[:, ti * D:(ti + 1) * D]
        rs = []
        for (pso, n0) in ((ps_o, 0), (ps_x, 512)):
            mm = None
            for c in range(8):
                mm = k.op("tensor", lambda t, c=c, ti=ti, n0=n0, pso=pso:
                          t.matmul(
                              pso[:],
                              yT[:, c * SEQ + ti * 128:c * SEQ + (ti + 1) * 128],
                              wout[:, c * D + n0:c * D + n0 + 512],
                              start=(c == 0), stop=False),
                          waits=yT_toks + tw + stt_prev + prev["pv"] +
                                prev["bc_used"])
            mm = k.op("tensor", lambda t, ti=ti, n0=n0, pso=pso: t.matmul(
                pso[:], ones_row[:], bout[:, n0:n0 + 512],
                start=False, stop=True),
                waits=yT_toks + tw + stt_prev)
            rtok = k.op("vector", lambda v, x_tile=x_tile, n0=n0, pso=pso:
                        v.tensor_tensor(x_tile[:, n0:n0 + 512], pso[:],
                                        x_tile[:, n0:n0 + 512], ALU.add),
                        waits=[mm] + x_toks[ti] + ln_last)
            rs.append(rtok)
        stt_prev = rs
        new_x_toks.append(rs)
        last_user += rs
    e["aa_w_war"] = tmax(*last_user)
    return new_x_toks
